# version 06
# baseline (speedup 1.0000x reference)
import math
import numpy as np
import concourse.bass as bass
import concourse.mybir as mybir
from concourse.bass_utils import run_bass_kernel_spmd

F32 = mybir.dt.float32
BF16 = mybir.dt.bfloat16
AF = mybir.ActivationFunctionType
ALU = mybir.AluOpType
AX = mybir.AxisListType

NEG = float(-np.finfo(np.float32).max)

D = 1024
T = 2048
CH = 8
H = 16
HD = 64
WIN = 256
NW = 8
DI = 2730
DIP = 2816
NI = 22
G = 8
NTPC = 32
NCL = 64
CMAX = 56

CONST_VALS = [1.0, -1.0, 1.0 / 1024.0, 8.0, 0.125, 0.5, 1e-5, 1e-12, 0.0, -0.5, 1e-24]
C_ONE, C_NEG1, C_INV1024 = 0, 1, 2

XN0, XN1 = 0, 2048
QROT = 4096
KROT = 6144
VTM = 10240
VSTG = 14336
PHT = 14592
OW = 15360
PBUF = 17408
XNH = 0
SQKV = 0
SXN = 6144
SKCR = 8192
SVCT = 8704
SVLT = 9728
SPHT = 11776
SY_ = 12160
SAG = 14208
SPB = 16256
SSCR_F = 8320
SSML_F = 8644

ATTN_KEYS = ("xn0", "xn1", "qrot", "krot", "vtm", "vstg", "pht", "ow", "pbuf")
SP_KEYS = ("sqkv", "xng", "kcr", "vct", "vlt", "phs", "yfm", "aag", "spbuf",
           "sscr", "sscr2")


class Sem:
    def __init__(self, ctx, name, step):
        self.h = ctx.stack.enter_context(ctx.nc.semaphore(name))
        self.n = 0
        self.step = step
        self.name = name


class Ctx:
    LOGICAL = ("te", "dv", "ac", "po", "sy", "gq")
    PHYS = dict(te="tensor", dv="vector", ac="scalar", po="gpsimd",
                sy="sync", gq="gpsimd")

    def __init__(self, nc, stack, block):
        self.nc = nc
        self.stack = stack
        self.block = block
        self.sems = dict(
            te=Sem(self, "s_te", 1), dv=Sem(self, "s_dv", 1),
            ac=Sem(self, "s_ac", 1), po=Sem(self, "s_po", 1))
        self.dsems = {}
        self.by_name = {s.name: s for s in self.sems.values()}
        self.steps = {e: [] for e in self.LOGICAL}
        self.seq = 0
        self.writer = {}
        self.readers = {}
        self.waited = {}
        self.n_inst = 0

    def dsem(self, name):
        if name not in self.dsems:
            s = Sem(self, "d_" + name, 16)
            self.dsems[name] = s
            self.by_name[s.name] = s
        return self.dsems[name]

    def _wait(self, eng, sem, n):
        if n <= 0:
            return
        got = self.waited.get((eng, sem.name), -1)
        if got >= n:
            return
        self.waited[(eng, sem.name)] = n
        self.seq += 1
        self.steps[eng].append((self.seq, "wait_ge", (sem.h, n), {}, []))
        self.n_inst += 1

    def op(self, eng, fname, *args, reads=(), writes=(), **kw):
        dsem_name = kw.pop("dsem", None)
        for k in reads:
            w = self.writer.get(k)
            if w is not None:
                self._wait(eng, w[0], w[0].n if w[0].step == 16 else w[1])
        for k in writes:
            w = self.writer.get(k)
            if w is not None:
                self._wait(eng, w[0], w[0].n if w[0].step == 16 else w[1])
            for sname, n in self.readers.get(k, {}).items():
                s = self.by_name[sname]
                self._wait(eng, s, s.n if s.step == 16 else n)
        if fname == "dma_start":
            assert dsem_name is not None, "dma_start needs dsem"
            sem = self.dsem(dsem_name)
        else:
            sem = self.sems[eng]
        sem.n += sem.step
        self.seq += 1
        self.steps[eng].append((self.seq, fname, args, kw, [(sem.h, sem.step)]))
        self.n_inst += 1
        for k in reads:
            self.readers.setdefault(k, {})[sem.name] = sem.n
        for k in writes:
            self.writer[k] = (sem, sem.n)
            self.readers[k] = {}

    def finalize_stream(self, keys, dsem_name):
        sem = self.dsem(dsem_name)
        for k in keys:
            if k in self.writer:
                self.writer[k] = (sem, sem.n)

    def handover(self, newkeys, oldkeys):
        acc = {}
        for ok in oldkeys:
            w = self.writer.pop(ok, None)
            if w is not None:
                acc[w[0].name] = max(acc.get(w[0].name, 0), w[1])
            for sname, n in self.readers.pop(ok, {}).items():
                acc[sname] = max(acc.get(sname, 0), n)
        for nk in newkeys:
            rd = self.readers.setdefault(nk, {})
            for sname, n in acc.items():
                rd[sname] = max(rd.get(sname, 0), n)

    def barrier_wait(self, eng, keys):
        for k in keys:
            w = self.writer.get(k)
            if w is not None:
                got = self.waited.get((eng, w[0].name), -1)
                if got < w[1]:
                    self.waited[(eng, w[0].name)] = w[1]
                    self.seq += 1
                    self.steps[eng].append(
                        (self.seq, "wait_ge", (w[0].h, w[1]), {}, []))
                    self.n_inst += 1

    def flush(self):
        phys_groups = {}
        for e in self.LOGICAL:
            if self.steps[e]:
                phys_groups.setdefault(self.PHYS[e], []).extend(self.steps[e])
                self.steps[e] = []
        for phys, steps in phys_groups.items():
            steps.sort(key=lambda s: s[0])

            def emit(engine, steps=steps):
                for _, fname, args, kw, incs in steps:
                    if fname == "wait_ge":
                        engine.wait_ge(*args)
                    else:
                        inst = getattr(engine, fname)(*args, **kw)
                        for h, v in incs:
                            inst = inst.then_inc(h, v)
            getattr(self.block, phys)(emit)


def _ffn_tables():
    half = np.arange(0, HD, 2, dtype=np.float32)
    inv_freq = (1.0 / 10000.0 ** (half / HD)).astype(np.float32)
    t = np.arange(T, dtype=np.float32)
    fr = t[:, None] * inv_freq[None, :]
    fr = np.concatenate([fr, fr], 1).astype(np.float32)
    base = ((half + 0.4 * HD) / (1.4 * HD)).astype(np.float32)
    power = ((t - T // 2) / (WIN // 2)).astype(np.float32)
    sc = (base[None, :] ** power[:, None]).astype(np.float32)
    sc = np.concatenate([sc, sc], 1).astype(np.float32)
    cos = np.cos(fr).astype(np.float32)
    sin = np.sin(fr).astype(np.float32)
    inv = (1.0 / sc).astype(np.float32)

    def dev(tab):
        return np.ascontiguousarray(np.tile(tab.T, (2, 1)).astype(np.float32))
    return np.stack([dev(cos * sc), dev(sin * sc), dev(cos * inv), dev(sin * inv)])


def _sparse_tables():
    pos = np.arange(CMAX + WIN, dtype=np.float32)[:, None]
    div = np.exp(np.arange(0, HD, 2, dtype=np.float32)
                 * np.float32(-math.log(10000.0) / HD)).astype(np.float32)
    ang = (pos * div).astype(np.float32)
    S = np.stack([np.sin(ang), np.cos(ang)], -1).reshape(CMAX + WIN, HD)
    S = S.astype(np.float32)

    def dev(posarr):
        Sp = S[posarr]
        Tt = np.empty((64, len(posarr)), np.float32)
        for j in range(64):
            Tt[j] = -Sp[:, 32 + j // 2] if j % 2 == 0 else Sp[:, (j - 1) // 2]
        return np.ascontiguousarray(np.tile(Tt, (2, 1)))
    t_sq = dev(np.arange(WIN))
    t_skl = dev((np.arange(T) // WIN) * (WIN // NTPC) + np.arange(T) % WIN)
    t_skc = dev(np.minimum(np.arange(NCL), CMAX - 1))
    return t_sq, t_skl, t_skc


def _masks():
    p = np.arange(128)
    maskl = np.zeros((128, 2, 512), np.float32)
    for qc in range(2):
        qpos = qc * 128 + p
        j = np.arange(512)
        bad = (j[None, :] < qpos[:, None]) | (j[None, :] > qpos[:, None] + WIN)
        maskl[:, qc][bad] = NEG
    msp_c = np.zeros((128, G, NCL), np.float32)
    for g in range(G):
        msp_c[:, g, :][:, np.arange(NCL) >= (WIN // NTPC) * g] = NEG
    msp_l = np.zeros((128, 2, WIN), np.float32)
    for qc in range(2):
        qpos = qc * 128 + p
        j = np.arange(WIN)
        msp_l[:, qc][j[None, :] > qpos[:, None]] = NEG
    return maskl, msp_c, msp_l


def _perm_mats():
    rbase = np.zeros((128, 128), np.float32)
    for pp in range(128):
        j = pp % 64
        m = pp + 32 if j < 32 else pp - 32
        rbase[m, pp] = -1.0 if j < 32 else 1.0
    p2 = np.zeros((128, 128), np.float32)
    for pp in range(128):
        p2[pp ^ 1, pp] = 1.0
    emat = np.zeros((128, 128), np.float32)
    for c in range(8):
        for pp in range(128):
            emat[pp, c * 16 + 2 * c + (1 if pp >= 64 else 0)] = 1.0
    bh = np.zeros((16, 8, 128), np.float32)
    for c in range(8):
        for pp in range(128):
            bh[2 * c + (1 if pp >= 64 else 0), c, pp] = 1.0
    idm = np.eye(128, dtype=np.float32)
    return rbase, p2, emat, bh, idm


def prep_inputs(inp):
    d = {}
    x = np.asarray(inp["L0_attn_x"], np.float32)
    d["x_all"] = np.ascontiguousarray(x.transpose(0, 2, 1))
    d["wqkv"] = np.ascontiguousarray(np.asarray(inp["L0_ffn_Wqkv"], np.float32))
    d["wo"] = np.ascontiguousarray(np.asarray(inp["L0_ffn_Wo"], np.float32))
    w1 = np.asarray(inp["L0_ffn_W1"], np.float32)
    w1p = np.zeros((6, D, 2 * DIP), np.float32)
    w1p[:, :, :DI] = w1[:, :, :DI]
    w1p[:, :, DIP:DIP + DI] = w1[:, :, DI:]
    d["w1"] = w1p
    w2 = np.asarray(inp["L0_ffn_W2"], np.float32)
    w2p = np.zeros((6, DIP, D), np.float32)
    w2p[:, :DI] = w2
    d["w2"] = w2p
    d["wattn"] = np.ascontiguousarray(np.asarray(inp["L0_attn_w_attn"], np.float32).T)
    d["wproj"] = np.ascontiguousarray(np.asarray(inp["L0_attn_w_proj"], np.float32).T)
    wass = np.asarray(inp["L0_attn_w_assign"], np.float32)
    d["aexp"] = np.ascontiguousarray(np.repeat(wass, NTPC, axis=0).T)
    qs = np.asarray(inp["L0_ffn_q_scale"], np.float32)
    ks = np.asarray(inp["L0_ffn_k_scale"], np.float32)
    d["qs"] = np.ascontiguousarray(np.tile(qs, (1, 2)).T)
    d["ks"] = np.ascontiguousarray(np.tile(ks, (1, 2)).T)
    b1 = np.asarray(inp["L0_ffn_b1"], np.float32)
    b1a = np.zeros((6, DIP), np.float32)
    b1a[:, :DI] = b1[:, :DI]
    b1g = np.zeros((6, DIP), np.float32)
    b1g[:, :DI] = b1[:, DI:]
    b1l = np.zeros((128, 6, 2 * NI), np.float32)
    for li in range(6):
        b1l[:, li, :NI] = b1a[li].reshape(NI, 128).T
        b1l[:, li, NI:] = b1g[li].reshape(NI, 128).T
    d["b1"] = np.ascontiguousarray(b1l)
    rs = lambda a: np.ascontiguousarray(
        np.asarray(a, np.float32).reshape(6, 8, 128).transpose(2, 0, 1)
        .reshape(128, 48))
    d["b2"] = rs(inp["L0_ffn_b2"])
    d["lng"] = rs(inp["L0_ffn_ln_g"])
    d["lnb"] = rs(inp["L0_ffn_ln_b"])
    d["battn"] = np.ascontiguousarray(
        np.asarray(inp["L0_attn_b_attn"], np.float32).reshape(24, 128).T)
    d["bproj"] = np.ascontiguousarray(
        np.asarray(inp["L0_attn_b_proj"], np.float32).reshape(8, 128).T)
    d["tabs"] = _ffn_tables()
    d["t_sq"], d["t_skl"], d["t_skc"] = _sparse_tables()
    d["maskl"], d["msp_c"], d["msp_l"] = _masks()
    d["rbase"], d["p2m"], d["emat"], d["bh"], d["idm"] = _perm_mats()
    d["consts"] = np.ascontiguousarray(
        np.tile(np.array(CONST_VALS + [0.0] * (16 - len(CONST_VALS)),
                         np.float32)[None, :], (128, 1)))
    d["bconst"] = np.ascontiguousarray(
        np.tile(np.array([1.0, 1.0 / 1024.0, 0.0, 0.0], np.float32)[None, :],
                (128, 1)))
    d["ones_r"] = np.ones((1, 128), np.float32)
    d["neg_r"] = np.full((1, 128), -1.0, np.float32)
    return d


class B:
    pass


def build(elems=2, calls=2, depth=6, sparse=True, sub="both", dbg=False):
    from contextlib import ExitStack
    nc = bass.Bass("TRN2", target_bir_lowering=False)
    b = B()
    b.nc = nc
    b.elems, b.calls, b.depth, b.sparse = elems, calls, depth, sparse
    b.sub = sub
    b.dbg = dbg
    b.dbg_tensors = {}
    di = lambda n, sh: nc.dram_tensor(n, sh, F32, kind="ExternalInput")
    b.x_in = di("x_in", [elems, D, T])
    b.h_wqkv = di("wqkv", [6, D, 3 * D])
    b.h_wo = di("wo", [6, D, D])
    b.h_w1 = di("w1", [6, D, 2 * DIP])
    b.h_w2 = di("w2", [6, DIP, D])
    b.h_wattn = di("wattn", [D, 3 * D])
    b.h_wproj = di("wproj", [D, D])
    b.h_aexp = di("aexp", [D, T])
    b.h_tabs = di("tabs", [4, 128, T])
    b.h_tsq = di("t_sq", [128, WIN])
    b.h_tskl = di("t_skl", [128, T])
    b.h_tskc = di("t_skc", [128, NCL])
    b.h_maskl = di("maskl", [128, 2, 512])
    b.h_mspc = di("msp_c", [128, G, NCL])
    b.h_mspl = di("msp_l", [128, 2, WIN])
    b.h_rbase = di("rbase", [128, 128])
    b.h_p2 = di("p2m", [128, 128])
    b.h_emat = di("emat", [128, 128])
    b.h_bh = di("bh", [16, 8, 128])
    b.h_idm = di("idm", [128, 128])
    b.h_consts = di("consts", [128, 16])
    b.h_bconst = di("bconst", [128, 4])
    b.h_onesr = di("ones_r", [1, 128])
    b.h_negr = di("neg_r", [1, 128])
    b.h_qs = di("qs", [128, 6])
    b.h_ks = di("ks", [128, 6])
    b.h_b1 = di("b1", [128, 6, 2 * NI])
    b.h_b2 = di("b2", [128, 48])
    b.h_lng = di("lng", [128, 48])
    b.h_lnb = di("lnb", [128, 48])
    b.h_battn = di("battn", [128, 24])
    b.h_bproj = di("bproj", [128, 8])
    b.xpark = nc.dram_tensor("xpark", [D, T], F32, kind="Internal")
    b.x_out = nc.dram_tensor("x_out", [elems, D, T], F32, kind="ExternalOutput")

    stack = ExitStack()
    with stack:
        E = stack.enter_context
        sb = lambda n, sh, dt: E(nc.sbuf_tensor(n, sh, dt))
        b.stream = sb("stream", [128, CH, T], F32)
        b.tblw = sb("tblw", [128, 2, 4, WIN], BF16)
        b.maskl = sb("masklS", [128, 2, 512], BF16)
        b.mspc = sb("mspcS", [128, G, NCL], BF16)
        b.mspl = sb("msplS", [128, 2, WIN], BF16)
        b.consts = sb("constsS", [128, 16], F32)
        b.bconsts = sb("bconstsS", [128, 4], BF16)
        b.onesr = sb("onesrS", [1, 128], F32)
        b.negr = sb("negrS", [1, 128], F32)
        b.idb = sb("idb", [128, 128], BF16)
        b.rbase = sb("rbaseS", [128, 128], BF16)
        b.rqm = sb("rqm", [128, 128], BF16)
        b.rkm = sb("rkm", [128, 128], BF16)
        b.p2m = sb("p2mS", [128, 128], BF16)
        b.emat = sb("ematS", [128, 128], BF16)
        b.qs = sb("qsS", [128, 6], F32)
        b.ks = sb("ksS", [128, 6], F32)
        b.b1 = sb("b1S", [128, 6, 2 * NI], F32)
        b.b2 = sb("b2S", [128, 48], F32)
        b.lng = sb("lngS", [128, 48], F32)
        b.lnb = sb("lnbS", [128, 48], F32)
        b.battn = sb("battnS", [128, 24], F32)
        b.bproj = sb("bprojS", [128, 8], F32)
        b.bh = sb("bhS", [16, 8 * 128], BF16)
        b.rows = sb("rows", [1, 1024], F32)
        b.rarena = sb("rarena", [16, 512], F32)
        b.rkbf = sb("rkbf", [16, WIN], BF16)
        b.rqbf = sb("rqbf", [16, WIN], BF16)
        b.bsave = sb("bsave", [128, 4, 1], F32)
        b.sqs = sb("sqs", [128, 4, 512], BF16)
        b.scrf = sb("scrf", [128, 1152], F32)
        b.bigbuf = sb("bigbuf", [128, 24576], BF16)
        b.wo = sb("woS", [128, CH, D], BF16)
        b.wsm = sb("wsm", [128, 5632], BF16)
        b.scratch = sb("scratchS", [128, 18944], BF16)
        b.scratch_f = b.scratch.bitcast(F32)
        b.scrf_b = b.scrf.bitcast(BF16)
        b.sqs_f = b.sqs.bitcast(F32)
        b.ps = [E(nc.psum_tensor(f"ps_{i}", [128, 512], F32)) for i in range(6)]
        b.pstr = E(nc.psum_tensor("ps_tr", [128, 1024], BF16))
        b.pss = E(nc.psum_tensor("ps_s", [16, 512], F32))

        for i, v in enumerate(CONST_VALS):
            nc.const_aps.register(F32, float(v), b.consts[:, i:i + 1])

        block = E(nc.Block())
        c = Ctx(nc, stack, block)
        b.ctx = c

        emit_init(b)
        for engx in ("te", "dv", "ac", "po"):
            for sname in ("init", "initg"):
                s = c.dsem(sname)
                c._wait(engx, s, s.n)
        c.flush()
        for e in range(elems):
            emit_load_x(b, e)
            for call in range(calls):
                if call > 0 and sparse:
                    emit_sparse(b)
                emit_park_and_outer_ln(b)
                for li in range(depth):
                    emit_ffn_layer(b, li, call)
                emit_merge(b)
                c.flush()
            emit_store_x(b, e)
            c.flush()
        for s in c.dsems.values():
            c._wait("sy", s, s.n)
        c.flush()
    return nc, b


def dump(b, name, ap, reads):
    if not b.dbg or name in b.dbg_tensors:
        return
    sh = list(ap.shape)
    t = b.nc.dram_tensor("dbg_" + name, sh, ap.dtype, kind="ExternalOutput")
    b.dbg_tensors[name] = t
    b.ctx.op("sy", "dma_start", out=t.ap(), in_=ap, reads=reads,
             writes=("dbg",), dsem="dbg")


def sc_bf(b, off, shape):
    n = int(np.prod(shape[1:]))
    ap = b.scratch[0:shape[0], off:off + n]
    if len(shape) == 3:
        ap = ap.rearrange("p (a c) -> p a c", a=shape[1])
    return ap


def emit_init(b):
    c = b.ctx
    for sbuf, src, key in [
            (b.consts, b.h_consts.ap(), "consts"),
            (b.qs, b.h_qs.ap(), "qs"),
            (b.ks, b.h_ks.ap(), "ks"),
            (b.b1, b.h_b1.ap(), "b1"),
            (b.b2, b.h_b2.ap(), "b2"),
            (b.lng, b.h_lng.ap(), "lng"),
            (b.lnb, b.h_lnb.ap(), "lnb"),
            (b.battn, b.h_battn.ap(), "battn"),
            (b.onesr, b.h_onesr.ap(), "onesr"),
            (b.negr, b.h_negr.ap(), "negr"),
            (b.bproj, b.h_bproj.ap(), "bproj")]:
        c.op("sy", "dma_start", out=sbuf[:], in_=src, writes=(key,), dsem="init")
    for dst, src, key in [
            (b.idb[:], b.h_idm.ap(), "idb"),
            (b.rbase[:], b.h_rbase.ap(), "rbase"),
            (b.p2m[:], b.h_p2.ap(), "p2m"),
            (b.emat[:], b.h_emat.ap(), "emat"),
            (b.maskl[:], b.h_maskl.ap(), "maskl"),
            (b.mspc[:], b.h_mspc.ap(), "mspc"),
            (b.mspl[:], b.h_mspl.ap(), "mspl"),
            (b.bh[:], b.h_bh.rearrange("h c p -> h (c p)"), "bh"),
            (b.bconsts[:], b.h_bconst.ap(), "bconsts"),
            (b.wo[:], b.h_wo[0].rearrange("(c p) m -> p c m", p=128), "wo")]:
        c.op("gq", "dma_start", out=dst, in_=src, writes=(key,), dsem="initg")


def emit_load_x(b, e):
    c = b.ctx
    c.op("sy", "dma_start", out=b.stream[:],
         in_=b.x_in[e].rearrange("(c p) t -> p c t", p=128),
         writes=("stream",), dsem="xin")
    c.flush()


def emit_store_x(b, e):
    c = b.ctx
    c.op("sy", "dma_start", out=b.x_out[e].rearrange("(c p) t -> p c t", p=128),
         in_=b.stream[:], reads=("stream",), writes=("xout",), dsem="xout")


def _ln_rows(b, n):
    c = b.ctx
    MU, RSTD = 0, 512
    c.op("dv", "tensor_copy", b.rows[:, MU:MU + n], b.pss[0:1, 0:n],
         reads=("pss",), writes=("rows",))
    c.op("dv", "tensor_tensor", b.rows[:, RSTD:RSTD + n], b.rows[:, MU:MU + n],
         b.rows[:, MU:MU + n], op=ALU.mult, reads=("rows",), writes=("rows",))
    c.op("dv", "scalar_tensor_tensor", b.rows[:, RSTD:RSTD + n],
         b.ps[4][0:1, 0:n], 1e-5, b.rows[:, RSTD:RSTD + n],
         op0=ALU.add, op1=ALU.subtract, reads=("ps4", "rows"), writes=("rows",))
    c.op("ac", "activation", out=b.rows[:, RSTD:RSTD + n],
         in_=b.rows[:, RSTD:RSTD + n], func=AF.Ln, reads=("rows",),
         writes=("rows",))
    c.op("ac", "activation", out=b.rows[:, RSTD:RSTD + n],
         in_=b.rows[:, RSTD:RSTD + n], func=AF.Exp, scale=-0.5,
         reads=("rows",), writes=("rows",))
    c.op("dv", "tensor_tensor", b.rows[:, MU:MU + n], b.rows[:, MU:MU + n],
         b.rows[:, RSTD:RSTD + n], op=ALU.mult, reads=("rows",), writes=("rows",))


def _ln_stats(b, src_fn, n):
    c = b.ctx
    for ch in range(CH):
        slot = ch % 2
        c.op("dv", "tensor_copy", b.sqs[:, slot, 0:n], src_fn(ch),
             reads=("stream",), writes=(f"sqs{slot}",))
        c.op("ac", "activation", out=b.sqs[:, 2 + slot, 0:n],
             in_=b.sqs[:, slot, 0:n], func=AF.Square,
             reads=(f"sqs{slot}",), writes=(f"sqs{2 + slot}",))
        c.op("te", "matmul", b.pss[0:1, 0:n], b.bconsts[:, 1:2],
             b.sqs[:, slot, 0:n], start=(ch == 0), stop=(ch == CH - 1),
             reads=(f"sqs{slot}", "bconsts"),
             writes=("pss",) if ch in (0, CH - 1) else ())
        c.op("te", "matmul", b.ps[4][0:1, 0:n], b.bconsts[:, 1:2],
             b.sqs[:, 2 + slot, 0:n], start=(ch == 0), stop=(ch == CH - 1),
             reads=(f"sqs{2 + slot}", "bconsts"),
             writes=("ps4",) if ch in (0, CH - 1) else ())
    _ln_rows(b, n)


def _ln_bcasts(b, n):
    c = b.ctx
    MU, RSTD = 0, 512
    c.op("te", "matmul", b.ps[2][:, 0:n], b.onesr[:],
         b.rows[0:1, RSTD:RSTD + n], start=True, stop=True,
         reads=("rows", "onesr"), writes=("ps2",))
    c.op("te", "matmul", b.ps[3][:, 0:n], b.negr[:],
         b.rows[0:1, MU:MU + n], start=True, stop=True,
         reads=("rows", "negr"), writes=("ps3",))


def emit_park_and_outer_ln(b):
    c = b.ctx
    c.op("sy", "dma_start", out=b.xpark.rearrange("(c p) t -> p c t", p=128),
         in_=b.stream[:], reads=("stream",), writes=("xpark",), dsem="xpark")
    for nt in range(4):
        ts = slice(nt * 512, (nt + 1) * 512)
        if nt == 0:
            c.handover(["scrf0", "scrf1"],
                       ["ck", "cv", "sc0", "sc1", "sc2", "sc3"])
            c.handover(["scrf2", "scrf3"],
                       [f"nm{s}" for s in range(4)]
                       + [f"zs{s}" for s in range(4)])
        _ln_stats(b, lambda ch: b.stream[:, ch, ts], 512)
        _ln_bcasts(b, 512)
        for ch in range(CH):
            c.op("dv", "tensor_tensor", b.scrf[:, 0:512], b.stream[:, ch, ts],
                 b.ps[2][:], op=ALU.mult, reads=("stream", "ps2"),
                 writes=("scrf0",))
            c.op("dv", "tensor_tensor", b.stream[:, ch, ts], b.scrf[:, 0:512],
                 b.ps[3][:], op=ALU.add, reads=("scrf0", "ps3"),
                 writes=("stream",))
    b.n_ln = getattr(b, "n_ln", 0) + 1
    dump(b, f"postln{b.n_ln}", b.stream[:], ("stream",))
    c.flush()


def emit_merge(b):
    c = b.ctx
    b.n_merge = getattr(b, "n_merge", 0) + 1
    xp = b.scratch_f
    c.handover(["xmerge"], ["xnh"])
    for nt in range(4):
        ts = slice(nt * 512, (nt + 1) * 512)
        c.op("sy", "dma_start",
             out=xp[:, 0:4096].rearrange("p (c t) -> p c t", c=8),
             in_=b.xpark.rearrange("(c p) t -> p c t", p=128)[:, :, ts],
             reads=("xpark",), writes=("xmerge",), dsem="xmerge")
        for ch in range(CH):
            c.op("dv", "scalar_tensor_tensor", b.stream[:, ch, ts],
                 b.stream[:, ch, ts], 0.5, xp[:, ch * 512:(ch + 1) * 512],
                 op0=ALU.mult, op1=ALU.add,
                 reads=("stream", "xmerge"), writes=("stream",))
        c.flush()
    dump(b, f"postmerge{b.n_merge}", b.stream[:], ("stream",))


def emit_ffn_layer(b, li, call):
    c = b.ctx
    depth, calls, sparse = b.depth, b.calls, b.sparse
    c.op("ac", "activation", out=b.rqm[:], in_=b.rbase[:], func=AF.Identity,
         scale=b.qs[:, li:li + 1], reads=("rbase", "qs"), writes=("rqm",))
    c.op("ac", "activation", out=b.rkm[:], in_=b.rbase[:], func=AF.Identity,
         scale=b.ks[:, li:li + 1], reads=("rbase", "ks"), writes=("rkm",))
    c.handover([f"wqkv{sl}" for sl in range(6)], ["Hbuf"])
    for sl in range(6):
        c.op("gq", "dma_start",
             out=b.bigbuf[:, sl * 4096:(sl + 1) * 4096].rearrange(
                 "p (k m) -> p k m", k=8),
             in_=b.h_wqkv[li].rearrange("(k p) m -> p k m", p=128)[
                 :, :, sl * 512:(sl + 1) * 512],
             writes=(f"wqkv{sl}",), dsem="wqkv")
    c.finalize_stream([f"wqkv{sl}" for sl in range(6)], "wqkv")
    c.handover(list(ATTN_KEYS), ["xnh", "xmerge"] + list(SP_KEYS))
    c.flush()

    def wqkv_ap(k, mcol0):
        sl = mcol0 // 512
        off = sl * 4096 + k * 512 + (mcol0 - sl * 512)
        return b.bigbuf[:, off:off + 128]

    xn = sc_bf(b, XN0, (128, 8, WIN))
    qrotP = [sc_bf(b, QROT, (128, 8, WIN)),
             sc_bf(b, XN1, (128, 8, WIN))]
    wsmv = b.wsm[:, 0:4096].rearrange("p (ch t) -> p ch t", ch=8)
    krotP = [sc_bf(b, KROT, (128, 8, 512)), wsmv]
    vtm = sc_bf(b, VTM, (128, 4, 1024))
    vstg = sc_bf(b, VSTG, (128, WIN))
    pht = sc_bf(b, PHT, (128, 2, 3 * 128)).rearrange(
        "p q (j k) -> p q j k", j=3)
    ow = sc_bf(b, OW, (128, 8, WIN))
    pbuf = sc_bf(b, PBUF, (128, 4, 384))
    c.handover(["krot1"], [f"w1{x}{s}" for x in "ag" for s in (0, 1)]
               + ["w2s0", "w2s1"])

    def thunks_b(w):
        th = []
        par = w % 2
        qrot, krot = qrotP[par], krotP[par]
        qk, kk = f"qrot{par}", f"krot{par}"
        t0 = w * WIN

        def cast_one(ch):
            if ch < 4:
                c.op("dv", "tensor_copy", xn[:, ch, :],
                     b.stream[:, ch, t0:t0 + WIN], reads=("stream",),
                     writes=("xn",))
            elif t0 == 0:
                c.op("dv", "tensor_copy", xn[:, ch, 1:WIN],
                     b.stream[:, ch, 0:WIN - 1], reads=("stream",),
                     writes=("xn",))
                c.op("po", "memset", xn[:, ch, 0:1], 0.0, writes=("xn",))
            else:
                c.op("dv", "tensor_copy", xn[:, ch, :],
                     b.stream[:, ch, t0 - 1:t0 + WIN - 1], reads=("stream",),
                     writes=("xn",))
        for ch in range(CH):
            th.append(lambda ch=ch: cast_one(ch))

        def tbl_one(t):
            c.op("gq", "dma_start", out=b.tblw[:, par, t, :],
                 in_=b.h_tabs[t][:, t0:t0 + WIN],
                 writes=(f"tblw{par}",), dsem=f"tblw{par}")
        for t in range(4):
            th.append(lambda t=t: tbl_one(t))

        def proj_group(m):
            ps, pk = b.ps[m % 2], f"ps{m % 2}"
            for k in range(CH):
                c.op("te", "matmul", ps[:, 0:WIN], wqkv_ap(k, m * 128),
                     xn[:, k, :], start=(k == 0), stop=(k == CH - 1),
                     reads=("xn", f"wqkv{(m * 128) // 512}"),
                     writes=(pk,) if k in (0, CH - 1) else ())
            if m < 8:
                c.op("ac", "copy", qrot[:, m, :], ps[:, 0:WIN], reads=(pk,),
                     writes=(qk,))
            elif m < 16:
                c.op("ac", "copy", krot[:, m - 8, WIN:512], ps[:, 0:WIN],
                     reads=(pk,), writes=(kk,))
            else:
                ch = m - 16
                c.op("ac", "copy", vstg[:], ps[:, 0:WIN], reads=(pk,),
                     writes=("vstg",))
                for j in range(2):
                    c.op("te", "transpose",
                         b.pstr[:, 768 + j * 128:768 + (j + 1) * 128],
                         vstg[:, j * 128:(j + 1) * 128], b.idb[:],
                         reads=("vstg", "idb"),
                         writes=("pstrv",) if j in (0, 1) else ())
                c.op("dv", "tensor_copy",
                     vtm[:, 2:4, ch * 128:(ch + 1) * 128],
                     b.pstr[:, 768:1024].rearrange("p (j k) -> p j k", j=2),
                     reads=("pstrv",), writes=("vtm",))
        for m in range(16):
            th.append(lambda m=m: proj_group(m))

        def norm_chunk(which, ch):
            slot = ch % 2
            roff = 0 if which == "q" else WIN
            src_ = qrot[:, ch, :] if which == "q" else krot[:, ch, WIN:512]
            skey = qk if which == "q" else kk
            c.op("ac", "activation", out=b.sqs[:, slot, 0:WIN], in_=src_,
                 func=AF.Square, reads=(skey,), writes=(f"sqs{slot}",))
            c.op("te", "matmul", b.pss[:, roff:roff + WIN],
                 b.emat[:, ch * 16:(ch + 1) * 16], b.sqs[:, slot, 0:WIN],
                 start=(ch == 0), stop=(ch == CH - 1),
                 reads=(f"sqs{slot}", "emat"),
                 writes=("pss",) if ch in (0, CH - 1) else ())

        def norm_fin(which):
            roff = 0 if which == "q" else WIN
            ra = b.rarena[:, roff:roff + WIN]
            c.op("dv", "tensor_scalar", out=ra, in0=b.pss[:, roff:roff + WIN],
                 scalar1=1e-24, scalar2=None, op0=ALU.max, reads=("pss",),
                 writes=("rarena",))
            c.op("ac", "activation", out=ra, in_=ra, func=AF.Ln,
                 reads=("rarena",), writes=("rarena",))
            c.op("ac", "activation", out=ra, in_=ra, func=AF.Exp, scale=-0.5,
                 reads=("rarena",), writes=("rarena",))
            if which == "q":
                c.op("ac", "copy", b.rqbf[:], b.rarena[:, 0:WIN],
                     reads=("rarena",), writes=("rqbf",))
            else:
                c.op("ac", "copy", b.rkbf[:], b.rarena[:, WIN:2 * WIN],
                     reads=("rarena",), writes=("rkbf",))
        for ch in range(CH):
            th.append(lambda ch=ch: norm_chunk("q", ch))
        th.append(lambda: norm_fin("q"))
        for ch in range(CH):
            th.append(lambda ch=ch: norm_chunk("k", ch))
        th.append(lambda: norm_fin("k"))

        def rot_chunk(which, ch):
            tkey = f"tblw{par}"
            tw = lambda t: b.tblw[:, par, t, :]
            ps, pk = b.ps[0], "ps0"
            psr, pkr = b.ps[1], "ps1"
            t1 = b.sqs_f[:, 2, :]
            t2 = b.sqs_f[:, 3, :]
            if which == "q":
                c.op("te", "matmul", ps[:, 0:WIN], b.rqm[:], qrot[:, ch, :],
                     start=True, stop=True, reads=("rqm", qk), writes=(pk,))
                c.op("te", "matmul", psr[:, 0:WIN],
                     b.bh[:, ch * 128:(ch + 1) * 128], b.rqbf[:],
                     start=True, stop=True, reads=("bh", "rqbf"),
                     writes=(pkr,))
                c.op("dv", "tensor_tensor", t1, ps[:, 0:WIN],
                     tw(1), op=ALU.mult, reads=(pk, tkey), writes=("sqs2",))
                c.op("dv", "tensor_tensor", t2,
                     qrot[:, ch, :], tw(0), op=ALU.mult, reads=(qk, tkey),
                     writes=("sqs3",))
                c.op("dv", "scalar_tensor_tensor", t1,
                     t2, b.qs[:, li:li + 1],
                     t1, op0=ALU.mult, op1=ALU.add,
                     reads=("sqs2", "sqs3", "qs"), writes=("sqs2",))
                c.op("dv", "tensor_tensor", qrot[:, ch, :], t1,
                     psr[:, 0:WIN], op=ALU.mult, reads=("sqs2", pkr),
                     writes=(qk,))
            else:
                c.op("te", "matmul", ps[:, 0:WIN], b.rkm[:],
                     krot[:, ch, WIN:512], start=True, stop=True,
                     reads=("rkm", kk), writes=(pk,))
                c.op("te", "matmul", psr[:, 0:WIN],
                     b.bh[:, ch * 128:(ch + 1) * 128], b.rkbf[:],
                     start=True, stop=True, reads=("bh", "rkbf"),
                     writes=(pkr,))
                c.op("dv", "tensor_tensor", t1, ps[:, 0:WIN],
                     tw(3), op=ALU.mult, reads=(pk, tkey), writes=("sqs2",))
                c.op("dv", "tensor_tensor", t2,
                     krot[:, ch, WIN:512], tw(2), op=ALU.mult,
                     reads=(kk, tkey), writes=("sqs3",))
                c.op("dv", "scalar_tensor_tensor", t1,
                     t2, b.ks[:, li:li + 1],
                     t1, op0=ALU.mult, op1=ALU.add,
                     reads=("sqs2", "sqs3", "ks"), writes=("sqs2",))
                c.op("dv", "tensor_tensor", krot[:, ch, WIN:512],
                     t1, psr[:, 0:WIN], op=ALU.mult,
                     reads=("sqs2", pkr), writes=(kk,))
        for ch in range(CH):
            th.append(lambda ch=ch: rot_chunk("q", ch))
        for ch in range(CH):
            th.append(lambda ch=ch: rot_chunk("k", ch))

        def vproj_group(m):
            proj_group(m)
        for m in range(16, 24):
            th.append(lambda m=m: vproj_group(m))

        def rolls():
            if w > 0:
                c.op("po", "tensor_copy", krot[:, :, 0:WIN],
                     krotP[1 - par][:, :, WIN:512],
                     reads=(f"krot{1 - par}",), writes=(kk,))
                c.op("po", "tensor_copy", vtm[:, 0:2, :], vtm[:, 2:4, :],
                     reads=("vtm",), writes=("vtm",))
        return th

    def thunks_b_ordered(w):
        th = thunks_b(w)
        par = w % 2

        def kroll():
            if w > 0:
                c.op("po", "tensor_copy", krotP[par][:, :, 0:WIN],
                     krotP[1 - par][:, :, WIN:512],
                     reads=(f"krot{1 - par}",), writes=(f"krot{par}",))

        def vroll():
            if w > 0:
                c.op("po", "tensor_copy", vtm[:, 0:2, :], vtm[:, 2:4, :],
                     reads=("vtm",), writes=("vtm",))
        return th[:8] + [kroll] + th[8:-8] + [vroll] + th[-8:]

    def thunks_a(w):
        th = []
        par = w % 2
        qrot, krot = qrotP[par], krotP[par]
        qk, kk = f"qrot{par}", f"krot{par}"
        scb = b.scrf_b
        hqs = [(h, qc) for h in range(H) for qc in range(2)]

        def krange(qc):
            if w == 0:
                return (WIN, 128 if qc == 0 else WIN)
            return (qc * 128, 384)

        def attn_s1(i):
            h, qc = hqs[i]
            ch, hp = h // 2, (h % 2) * 64
            sl = i % 4
            psS, pkS = b.ps[4 + i % 2], f"ps{4 + i % 2}"
            k0, kn = krange(qc)
            c.op("te", "matmul", psS[:, 0:kn],
                 qrot[hp:hp + 64, ch, qc * 128:(qc + 1) * 128],
                 krot[hp:hp + 64, ch, k0:k0 + kn], start=True, stop=True,
                 reads=(qk, kk), writes=(pkS,))
            c.op("dv", "scalar_tensor_tensor",
                 scb[:, sl * 384:sl * 384 + kn], psS[:, 0:kn], 8.0,
                 b.maskl[:, qc, k0:k0 + kn], op0=ALU.mult, op1=ALU.add,
                 reads=(pkS, "maskl"), writes=(f"sc{sl}",))
            c.op("dv", "reduce_max", b.scrf[:, 1024 + sl:1025 + sl],
                 scb[:, sl * 384:sl * 384 + kn], axis=AX.X, negate=True,
                 reads=(f"sc{sl}",), writes=(f"nm{sl}",))

        def attn_s2(i):
            h, qc = hqs[i]
            sl = i % 4
            k0, kn = krange(qc)
            c.op("ac", "activation", out=pbuf[:, sl, 0:kn],
                 in_=scb[:, sl * 384:sl * 384 + kn], func=AF.Exp,
                 bias=b.scrf[:, 1024 + sl:1025 + sl],
                 accum_out=b.scrf[:, 1028 + sl:1029 + sl],
                 reads=(f"sc{sl}", f"nm{sl}"),
                 writes=(f"pbuf{sl}", f"zs{sl}"))
            c.op("dv", "reciprocal", b.scrf[:, 1032 + sl:1033 + sl],
                 b.scrf[:, 1028 + sl:1029 + sl], reads=(f"zs{sl}",),
                 writes=(f"zs{sl}",))

        def attn_s3(i):
            h, qc = hqs[i]
            ch, hp = h // 2, (h % 2) * 64
            sl = i % 4
            k0, kn = krange(qc)
            psO, pkO = b.ps[2 + h % 2], f"ps{2 + h % 2}"
            c.op("ac", "activation", out=pbuf[:, sl, 0:kn],
                 in_=pbuf[:, sl, 0:kn], func=AF.Identity,
                 scale=b.scrf[:, 1032 + sl:1033 + sl],
                 reads=(f"pbuf{sl}", f"zs{sl}"), writes=(f"pbuf{sl}",))
            nj = kn // 128
            po_ = (i % 2) * 384
            pstk = f"pstr{i % 2}"
            for jj in range(nj):
                c.op("te", "transpose",
                     b.pstr[:, po_ + jj * 128:po_ + (jj + 1) * 128],
                     pbuf[:, sl, jj * 128:(jj + 1) * 128], b.idb[:],
                     reads=(f"pbuf{sl}", "idb"),
                     writes=(pstk,) if jj in (0, nj - 1) else ())
            c.op("dv", "tensor_copy", pht[:, i % 2, 0:nj, :],
                 b.pstr[:, po_:po_ + nj * 128].rearrange(
                     "p (jx k) -> p jx k", jx=nj),
                 reads=(pstk,), writes=(f"pht{i % 2}",))
            slots = ([2] if qc == 0 else [2, 3]) if w == 0 else \
                [qc, qc + 1, qc + 2]
            for idx, vs in enumerate(slots):
                c.op("te", "matmul",
                     psO[hp:hp + 64, qc * 128:(qc + 1) * 128],
                     vtm[:, vs, h * 64:(h + 1) * 64],
                     pht[:, i % 2, idx, :],
                     start=(idx == 0), stop=(idx == len(slots) - 1),
                     tile_position=(0, hp), reads=("vtm", f"pht{i % 2}"),
                     writes=(pkO,) if idx in (0, len(slots) - 1) else ())
            if qc == 1:
                c.op("ac", "copy", ow[hp:hp + 64, ch, :],
                     psO[hp:hp + 64, 0:WIN], reads=(pkO,), writes=("ow",))

        NI_ = len(hqs)

        def step(i):
            if i < NI_:
                attn_s1(i)
            if 3 <= i < NI_ + 3:
                attn_s2(i - 3)
            if i >= 6:
                attn_s3(i - 6)
        for i in range(NI_ + 6):
            th.append(lambda i=i: step(i))

        def wo_group(m):
            ps, pk = b.ps[2 + m % 2], f"ps{2 + m % 2}"
            for k in range(CH):
                c.op("te", "matmul", ps[:, 0:WIN],
                     b.wo[:, k, m * 128:(m + 1) * 128], ow[:, k, :],
                     start=(k == 0), stop=(k == CH - 1), reads=("ow", "wo"),
                     writes=(pk,) if k in (0, CH - 1) else ())
            c.op("dv", "tensor_tensor",
                 b.stream[:, m, w * WIN:(w + 1) * WIN], ps[:, 0:WIN],
                 b.stream[:, m, w * WIN:(w + 1) * WIN], op=ALU.add,
                 reads=(pk, "stream"), writes=("stream",))
        for m in range(CH):
            th.append(lambda m=m: wo_group(m))
        return th

    def interleave(la, lb):
        ia = ib = 0
        while ia < len(la) or ib < len(lb):
            if ib < len(lb):
                lb[ib]()
                ib += 1
            if ia < len(la):
                la[ia]()
                ia += 1

    for t in thunks_b_ordered(0):
        t()
    for w in range(NW):
        A = thunks_a(w)
        B = thunks_b_ordered(w + 1) if w + 1 < NW else []
        interleave(A, B)
        c.flush()

    for i in range(4):
        c.op("dv", "tensor_copy", b.bsave[:, i, :],
             b.stream[:, 4 + i, 1023:1024], reads=("stream",),
             writes=("bsave",))
    nxt = _next_wo(b, li, call)
    if nxt is not None:
        c.op("gq", "dma_start", out=b.wo[:],
             in_=nxt.rearrange("(c p) m -> p c m", p=128), writes=("wo",),
             dsem="wo")
    c.handover(["xnh"], ["xn0", "xn1", "qrot", "krot"])
    c.handover(["scrf0", "scrf1"], ["sc0", "sc1", "sc2", "sc3"])
    c.handover(["scrf2", "scrf3"],
               [f"nm{s}" for s in range(4)] + [f"zs{s}" for s in range(4)])
    xnh = sc_bf(b, XNH, (128, 8, 1024))
    Hbuf = b.bigbuf[:, 0:NI * 1024].rearrange("p (i t) -> p i t", i=NI)
    w1a = [b.wsm[:, o:o + 1024].rearrange("p (k m) -> p k m", k=8)
           for o in (0, 1024)]
    w1g = [b.wsm[:, o:o + 1024].rearrange("p (k m) -> p k m", k=8)
           for o in (2048, 3072)]
    w2s = [b.wsm[:, o:o + 2816].rearrange("p (i m) -> p i m", i=NI)
           for o in (0, 2816)]
    for hf in (range(2) if b.sub in ("both", "ffn") else ()):
        t0 = hf * 1024
        for nt in range(2):
            tt0 = t0 + nt * 512
            w0 = (hf == 0 and nt == 0)
            bs = (hf == 1 and nt == 0)
            for ch in range(CH):
                slot = ch % 2
                if ch >= 4 and (w0 or bs):
                    c.op("dv", "tensor_copy", b.sqs[:, slot, 1:512],
                         b.stream[:, ch, tt0:tt0 + 511], reads=("stream",),
                         writes=(f"sqs{slot}",))
                    if w0:
                        c.op("po", "memset", b.sqs[:, slot, 0:1], 0.0,
                             writes=(f"sqs{slot}",))
                    else:
                        c.op("dv", "tensor_copy", b.sqs[:, slot, 0:1],
                             b.bsave[:, ch - 4, :], reads=("bsave",),
                             writes=(f"sqs{slot}",))
                else:
                    src = b.stream[:, ch, tt0:tt0 + 512] if ch < 4 else \
                        b.stream[:, ch, tt0 - 1:tt0 + 511]
                    c.op("dv", "tensor_copy", b.sqs[:, slot, 0:512], src,
                         reads=("stream",), writes=(f"sqs{slot}",))
                c.op("ac", "activation", out=b.sqs[:, 2 + slot, 0:512],
                     in_=b.sqs[:, slot, 0:512], func=AF.Square,
                     reads=(f"sqs{slot}",), writes=(f"sqs{2 + slot}",))
                c.op("te", "matmul", b.pss[0:1, 0:512], b.bconsts[:, 1:2],
                     b.sqs[:, slot, 0:512], start=(ch == 0),
                     stop=(ch == CH - 1), reads=(f"sqs{slot}", "bconsts"),
                     writes=("pss",) if ch in (0, CH - 1) else ())
                c.op("te", "matmul", b.ps[4][0:1, 0:512], b.bconsts[:, 1:2],
                     b.sqs[:, 2 + slot, 0:512], start=(ch == 0),
                     stop=(ch == CH - 1), reads=(f"sqs{2 + slot}", "bconsts"),
                     writes=("ps4",) if ch in (0, CH - 1) else ())
            _ln_rows(b, 512)
            _ln_bcasts(b, 512)
            for ch in range(CH):
                dst = xnh[:, ch, nt * 512:(nt + 1) * 512]
                if ch >= 4 and (w0 or bs):
                    c.op("dv", "tensor_tensor", b.scrf[:, 1:512],
                         b.stream[:, ch, tt0:tt0 + 511], b.ps[2][:, 1:512],
                         op=ALU.mult, reads=("stream", "ps2"),
                         writes=("scrf0",))
                    if bs:
                        c.op("dv", "tensor_tensor", b.scrf[:, 0:1],
                             b.bsave[:, ch - 4, :], b.ps[2][:, 0:1],
                             op=ALU.mult, reads=("bsave", "ps2"),
                             writes=("scrf0",))
                    else:
                        c.op("po", "memset", b.scrf[:, 0:1], 0.0,
                             writes=("scrf0",))
                else:
                    src = b.stream[:, ch, tt0:tt0 + 512] if ch < 4 else \
                        b.stream[:, ch, tt0 - 1:tt0 + 511]
                    c.op("dv", "tensor_tensor", b.scrf[:, 0:512], src,
                         b.ps[2][:], op=ALU.mult, reads=("stream", "ps2"),
                         writes=("scrf0",))
                c.op("dv", "tensor_tensor", b.scrf[:, 0:512],
                     b.scrf[:, 0:512], b.ps[3][:], op=ALU.add,
                     reads=("scrf0", "ps3"), writes=("scrf0",))
                c.op("ac", "activation", out=dst, in_=b.scrf[:, 0:512],
                     func=AF.Identity,
                     scale=b.lng[:, li * 8 + ch:li * 8 + ch + 1],
                     bias=b.lnb[:, li * 8 + ch:li * 8 + ch + 1],
                     reads=("scrf0", "lng", "lnb"), writes=("xnh",))
        c.flush()
        if hf == 0:
            c.handover(["Hbuf"], [f"wqkv{sl}" for sl in range(6)])
        c.handover([f"w1{x}{s}" for x in "ag" for s in (0, 1)],
                   ["w2s0", "w2s1"])
        for i in range(NI):
            sl = i % 2
            c.op("gq", "dma_start", out=w1a[sl][:],
                 in_=b.h_w1[li].rearrange("(k p) m -> p k m", p=128)[
                     :, :, i * 128:(i + 1) * 128], writes=(f"w1a{sl}",),
                 dsem=f"w1a{sl}")
            c.op("gq", "dma_start", out=w1g[sl][:],
                 in_=b.h_w1[li].rearrange("(k p) m -> p k m", p=128)[
                     :, :, DIP + i * 128:DIP + (i + 1) * 128],
                 writes=(f"w1g{sl}",), dsem=f"w1g{sl}")
            for nt in range(2):
                psA, pkA = b.ps[nt], f"ps{nt}"
                psB, pkB = b.ps[4 + nt], f"ps{4 + nt}"
                for k in range(CH):
                    c.op("te", "matmul", psA[:], w1a[sl][:, k, :],
                         xnh[:, k, nt * 512:(nt + 1) * 512],
                         start=(k == 0), stop=(k == CH - 1),
                         reads=("xnh", f"w1a{sl}"),
                         writes=(pkA,) if k in (0, CH - 1) else ())
                for k in range(CH):
                    c.op("te", "matmul", psB[:], w1g[sl][:, k, :],
                         xnh[:, k, nt * 512:(nt + 1) * 512],
                         start=(k == 0), stop=(k == CH - 1),
                         reads=("xnh", f"w1g{sl}"),
                         writes=(pkB,) if k in (0, CH - 1) else ())
                c.op("ac", "activation", out=b.scrf[:, 0:512], in_=psB[:],
                     func=AF.Gelu, bias=b.b1[:, li, NI + i:NI + i + 1],
                     reads=(pkB, "b1"), writes=("scrf0",))
                c.op("dv", "scalar_tensor_tensor",
                     Hbuf[:, i, nt * 512:(nt + 1) * 512], psA[:],
                     b.b1[:, li, i:i + 1], b.scrf[:, 0:512],
                     op0=ALU.add, op1=ALU.mult,
                     reads=(pkA, "b1", "scrf0"), writes=("Hbuf",))
        c.flush()
        c.handover(["w2s0", "w2s1"],
                   [f"w1{x}{s}" for x in "ag" for s in (0, 1)])
        for m in range(CH):
            sl = m % 2
            c.op("gq", "dma_start", out=w2s[sl][:],
                 in_=b.h_w2[li].rearrange("(i p) m -> p i m", p=128)[
                     :, :, m * 128:(m + 1) * 128], writes=(f"w2s{sl}",),
                 dsem=f"w2s{sl}")
            pso = (b.ps[0], b.ps[1]) if m % 2 == 0 else (b.ps[2], b.ps[3])
            pko = ("ps0", "ps1") if m % 2 == 0 else ("ps2", "ps3")
            for i in range(NI):
                for nt in range(2):
                    c.op("te", "matmul", pso[nt][:], w2s[sl][:, i, :],
                         Hbuf[:, i, nt * 512:(nt + 1) * 512],
                         start=(i == 0), stop=(i == NI - 1),
                         reads=("Hbuf", f"w2s{sl}"),
                         writes=(pko[nt],) if i in (0, NI - 1) else ())
            for nt in range(2):
                ts = slice(t0 + nt * 512, t0 + (nt + 1) * 512)
                c.op("dv", "scalar_tensor_tensor", b.stream[:, m, ts],
                     pso[nt][:], b.b2[:, li * 8 + m:li * 8 + m + 1],
                     b.stream[:, m, ts], op0=ALU.add, op1=ALU.add,
                     reads=(pko[nt], "b2", "stream"), writes=("stream",))
        c.flush()


def _next_wo(b, li, call):
    if li + 1 < b.depth:
        return b.h_wo[li + 1]
    if b.sparse and call + 1 < b.calls:
        return b.h_wproj.ap()
    return b.h_wo[0]


def emit_sparse(b):
    c = b.ctx
    qkv = sc_bf(b, SQKV, (128, 24, WIN))
    xng = sc_bf(b, SXN, (128, 8, WIN))
    kcr = sc_bf(b, SKCR, (128, 8, 64))
    vct = b.scratch[0:64, SVCT:SVCT + 1024]
    vlt = sc_bf(b, SVLT, (128, 2, 1024))
    phs = sc_bf(b, SPHT, (128, 3 * 128)).rearrange("p (j k) -> p j k", j=3)
    yfm = sc_bf(b, SY_, (128, 8, WIN))
    aag = sc_bf(b, SAG, (128, 8, WIN))
    spb = sc_bf(b, SPB, (128, 320))
    sscr = b.scratch_f[:, SSCR_F:SSCR_F + 320]
    ssml = b.scratch_f[:, SSML_F:SSML_F + 8]
    sqf = b.sqs_f[:]
    ck = b.scrf[:, 0:512].rearrange("p (ch l) -> p ch l", ch=8)
    cv = b.scrf[:, 512:1024].rearrange("p (ch l) -> p ch l", ch=8)
    c.handover(list(SP_KEYS), list(ATTN_KEYS) + ["xnh", "xmerge"])
    c.handover(["ck", "cv"], ["scrf0", "scrf1", "scrf2", "scrf3",
                               "sc0", "sc1", "sc2", "sc3"]
               + [f"nm{s}" for s in range(4)] + [f"zs{s}" for s in range(4)])
    c.op("po", "memset", b.scrf[:, 0:512], 0.0, writes=("ck",))
    c.op("po", "memset", b.scrf[:, 512:1024], 0.0, writes=("cv",))
    c.handover([f"wqkv{sl}" for sl in range(6)], ["Hbuf"])
    for sl in range(6):
        c.op("gq", "dma_start",
             out=b.bigbuf[:, sl * 4096:(sl + 1) * 4096].rearrange(
                 "p (k m) -> p k m", k=8),
             in_=b.h_wattn.rearrange("(k p) m -> p k m", p=128)[
                 :, :, sl * 512:(sl + 1) * 512],
             writes=(f"wqkv{sl}",), dsem="wqkv")
    c.finalize_stream([f"wqkv{sl}" for sl in range(6)], "wqkv")
    c.op("gq", "dma_start", out=b.wo[:],
         in_=b.h_wproj.ap().rearrange("(c p) m -> p c m", p=128),
         writes=("wo",), dsem="wo")
    c.op("gq", "dma_start", out=b.tblw[:, 0, 0, :], in_=b.h_tsq.ap(),
         writes=("tblw0",), dsem="tblw0")
    c.op("gq", "dma_start", out=b.tblw[:, 0, 1, 0:64], in_=b.h_tskc.ap(),
         writes=("tblw0",), dsem="tblw0")
    c.flush()

    def wattn_ap(k, mcol0):
        sl = mcol0 // 512
        off = sl * 4096 + k * 512 + (mcol0 - sl * 512)
        return b.bigbuf[:, off:off + 128]

    for g in range(G):
        ts = slice(g * WIN, (g + 1) * WIN)
        c.op("gq", "dma_start", out=aag[:],
             in_=b.h_aexp.rearrange("(ch p) t -> p ch t", p=128)[:, :, ts],
             writes=("aag",), dsem="aag")
        c.op("gq", "dma_start", out=b.tblw[:, 1, g % 2, :],
             in_=b.h_tskl[:, ts], writes=("tblw1",), dsem="tblw1")
        _ln_stats(b, lambda ch: b.stream[:, ch, ts], WIN)
        _ln_bcasts(b, WIN)
        for ch in range(CH):
            c.op("dv", "tensor_tensor", sqf[:, 3, :], b.stream[:, ch, ts],
                 b.ps[2][:, 0:WIN], op=ALU.mult, reads=("stream", "ps2"),
                 writes=("sqs3",))
            c.op("dv", "tensor_tensor", xng[:, ch, :], sqf[:, 3, :],
                 b.ps[3][:, 0:WIN], op=ALU.add, reads=("sqs3", "ps3"),
                 writes=("xng",))
        for m in range(24):
            ps, pk = b.ps[m % 4], f"ps{m % 4}"
            for k in range(CH):
                c.op("te", "matmul", ps[:, 0:WIN], wattn_ap(k, m * 128),
                     xng[:, k, :], start=(k == 0), stop=(k == CH - 1),
                     reads=("xng", f"wqkv{(m * 128) // 512}"),
                     writes=(pk,) if k in (0, CH - 1) else ())
            c.op("ac", "activation", out=qkv[:, m, :], in_=ps[:, 0:WIN],
                 func=AF.Identity, bias=b.battn[:, m:m + 1],
                 reads=(pk, "battn"), writes=("sqkv",))
        if g == 1:
            dump(b, "xng", xng, ("xng",))
            dump(b, "qkv_raw", qkv, ("sqkv",))
        for ch in range(CH):
            for j in range(2):
                c.op("te", "transpose", b.pstr[:, j * 128:(j + 1) * 128],
                     qkv[:, 16 + ch, j * 128:(j + 1) * 128], b.idb[:],
                     reads=("sqkv", "idb"),
                     writes=("pstr",) if j in (0, 1) else ())
            c.op("dv", "tensor_copy", vlt[:, :, ch * 128:(ch + 1) * 128],
                 b.pstr[:, 0:256].rearrange("p (j k) -> p j k", j=2),
                 reads=("pstr",), writes=("vlt",))
        for ch in range(CH):
            c.op("dv", "tensor_tensor", sqf[:, 3, :], xng[:, ch, :],
                 aag[:, ch, :], op=ALU.mult, reads=("xng", "aag"),
                 writes=("sqs3",))
            c.op("te", "matmul", b.pss[0:1, 0:WIN],
                 b.consts[:, C_ONE:C_ONE + 1], sqf[:, 3, :],
                 start=(ch == 0), stop=(ch == CH - 1),
                 reads=("sqs3", "consts"),
                 writes=("pss",) if ch in (0, CH - 1) else ())
        LG, WR, SM = 0, 256, 768
        c.op("dv", "tensor_copy", b.rows[:, LG:LG + WIN], b.pss[0:1, 0:WIN],
             reads=("pss",), writes=("rows",))
        lg3 = b.rows[:, LG:LG + WIN].rearrange("o (s t) -> o s t", s=8)
        wr3 = b.rows[:, WR:WR + WIN].rearrange("o (s t) -> o s t", s=8)
        if g == 1:
            dump(b, "row_logits", b.rows[:, LG:LG + WIN], ("rows",))
        c.op("dv", "reduce_max", b.rows[:, SM:SM + 8], lg3, axis=AX.X,
             negate=True, reads=("rows",), writes=("rows",))
        if g == 1:
            dump(b, "row_negmax", b.rows[:, SM:SM + 8], ("rows",))
        c.op("dv", "tensor_tensor", lg3, lg3,
             b.rows[:, SM:SM + 8][:, :, None].broadcast_to((1, 8, 32)),
             op=ALU.add, reads=("rows",), writes=("rows",))
        if g == 1:
            dump(b, "row_sub", b.rows[:, LG:LG + WIN], ("rows",))
        c.op("ac", "activation", out=b.rows[:, LG:LG + WIN],
             in_=b.rows[:, LG:LG + WIN], func=AF.Exp, reads=("rows",),
             writes=("rows",))
        if g == 1:
            dump(b, "row_exp", b.rows[:, LG:LG + WIN], ("rows",))
        c.op("dv", "reduce_sum", b.rows[:, SM:SM + 8], lg3, axis=AX.X,
             reads=("rows",), writes=("rows",))
        c.op("dv", "reciprocal", b.rows[:, SM:SM + 8], b.rows[:, SM:SM + 8],
             reads=("rows",), writes=("rows",))
        if g == 1:
            dump(b, "row_rz", b.rows[:, SM:SM + 8], ("rows",))
        c.op("dv", "tensor_tensor", wr3, lg3,
             b.rows[:, SM:SM + 8][:, :, None].broadcast_to((1, 8, 32)),
             op=ALU.mult, reads=("rows",), writes=("rows",))
        if g == 1:
            dump(b, "wrow", b.rows[:, WR:WR + WIN], ("rows",))
        c.op("te", "matmul", b.ps[4][:, 0:WIN], b.onesr[:],
             b.rows[0:1, WR:WR + WIN], start=True, stop=True,
             reads=("rows", "onesr"), writes=("ps4",))
        for ch in range(CH):
            c.op("dv", "tensor_tensor", sqf[:, 3, :], qkv[:, 8 + ch, :],
                 b.ps[4][:, 0:WIN], op=ALU.mult, reads=("sqkv", "ps4"),
                 writes=("sqs3",))
            c.op("dv", "tensor_reduce", ck[:, ch, g * 8:(g + 1) * 8],
                 sqf[:, 3, :].rearrange("p (s t) -> p s t", s=8),
                 axis=AX.X, op=ALU.add, reads=("sqs3",), writes=("ck",))
            c.op("dv", "tensor_tensor", sqf[:, 3, :], qkv[:, 16 + ch, :],
                 b.ps[4][:, 0:WIN], op=ALU.mult, reads=("sqkv", "ps4"),
                 writes=("sqs3",))
            c.op("dv", "tensor_reduce", cv[:, ch, g * 8:(g + 1) * 8],
                 sqf[:, 3, :].rearrange("p (s t) -> p s t", s=8),
                 axis=AX.X, op=ALU.add, reads=("sqs3",), writes=("cv",))
        if g == 1:
            dump(b, "ck", b.scrf[:, 0:512], ("ck",))
            dump(b, "cv", b.scrf[:, 512:1024], ("cv",))
        for ch in range(CH):
            ps, pk = b.ps[2 + ch % 2], f"ps{2 + ch % 2}"
            c.op("te", "matmul", ps[:, 0:WIN], b.p2m[:], qkv[:, ch, :],
                 start=True, stop=True, reads=("p2m", "sqkv"), writes=(pk,))
            c.op("dv", "tensor_tensor", qkv[:, ch, :], ps[:, 0:WIN],
                 b.tblw[:, 0, 0, :], op=ALU.mult, reads=(pk, "tblw0"),
                 writes=("sqkv",))
            c.op("te", "matmul", ps[:, 0:WIN], b.p2m[:], qkv[:, 8 + ch, :],
                 start=True, stop=True, reads=("p2m", "sqkv"), writes=(pk,))
            c.op("dv", "tensor_tensor", qkv[:, 8 + ch, :], ps[:, 0:WIN],
                 b.tblw[:, 1, g % 2, :], op=ALU.mult,
                 reads=(pk, "tblw1"), writes=("sqkv",))
        for ch in range(CH):
            c.op("ac", "copy", b.sqs[:, 0, 0:64], ck[:, ch, :],
                 reads=("ck",), writes=("sqs0",))
            c.op("te", "matmul", b.ps[4][:, 0:64], b.p2m[:],
                 b.sqs[:, 0, 0:64], start=True, stop=True,
                 reads=("p2m", "sqs0"), writes=("ps4",))
            c.op("dv", "tensor_tensor", kcr[:, ch, :], b.ps[4][:, 0:64],
                 b.tblw[:, 0, 1, 0:64], op=ALU.mult, reads=("ps4", "tblw0"),
                 writes=("kcr",))
            c.op("ac", "copy", b.sqs[:, 1, 0:64], cv[:, ch, :],
                 reads=("cv",), writes=("sqs1",))
            for half in range(2):
                h2, hp = ch * 2 + half, half * 64
                c.op("te", "transpose", b.pstr[0:64, 0:64],
                     b.sqs[hp:hp + 64, 1, 0:64], b.idb[hp:hp + 64, hp:hp + 64],
                     reads=("sqs1", "idb"), writes=("pstr",))
                c.op("dv", "tensor_copy", vct[:, h2 * 64:(h2 + 1) * 64],
                     b.pstr[0:64, 0:64], reads=("pstr",), writes=("vct",))
        if g == 1:
            dump(b, "qkv_rot", qkv, ("sqkv",))
            dump(b, "kcr", kcr, ("kcr",))
            dump(b, "vct", vct, ("vct",))
        c.flush()
        for h in range(H):
            ch, hp = h // 2, (h % 2) * 64
            psS, pkS = b.ps[4 + h % 2], f"ps{4 + h % 2}"
            psO, pkO = b.ps[2 + h % 2], f"ps{2 + h % 2}"
            for qc in range(2):
                c.op("te", "matmul", psS[:, 0:64],
                     qkv[hp:hp + 64, ch, qc * 128:(qc + 1) * 128],
                     kcr[hp:hp + 64, ch, :], start=True, stop=True,
                     reads=("sqkv", "kcr"), writes=(pkS,))
                c.op("te", "matmul", psS[:, 64:320],
                     qkv[hp:hp + 64, ch, qc * 128:(qc + 1) * 128],
                     qkv[hp:hp + 64, 8 + ch, :], start=True, stop=True,
                     reads=("sqkv",), writes=(pkS,))
                c.op("dv", "scalar_tensor_tensor", sscr[:, 0:64],
                     psS[:, 0:64], 0.125, b.mspc[:, g, :], op0=ALU.mult,
                     op1=ALU.add, reads=(pkS, "mspc"), writes=("sscr",))
                c.op("dv", "scalar_tensor_tensor", sscr[:, 64:320],
                     psS[:, 64:320], 0.125, b.mspl[:, qc, :], op0=ALU.mult,
                     op1=ALU.add, reads=(pkS, "mspl"), writes=("sscr",))
                c.op("dv", "reduce_max", ssml[:, 0:1], sscr[:, 0:320],
                     axis=AX.X, negate=True, reads=("sscr",),
                     writes=("sscr2",))
                c.op("ac", "activation", out=spb[:], in_=sscr[:, 0:320],
                     func=AF.Exp, bias=ssml[:, 0:1], accum_out=ssml[:, 2:3],
                     reads=("sscr", "sscr2"), writes=("spbuf", "sscr2"))
                if g == 1 and h == 0 and qc == 0:
                    dump(b, "scores00", sscr[:, 0:320], ("sscr",))
                c.op("dv", "reciprocal", ssml[:, 4:5], ssml[:, 2:3],
                     reads=("sscr2",), writes=("sscr2",))
                c.op("ac", "activation", out=spb[:], in_=spb[:],
                     func=AF.Identity, scale=ssml[:, 4:5],
                     reads=("spbuf", "sscr2"), writes=("spbuf",))
                c.op("te", "transpose", b.pstr[0:64, 0:128], spb[:, 0:64],
                     b.idb[:], reads=("spbuf", "idb"), writes=("pstr",))
                for j in range(2):
                    c.op("te", "transpose",
                         b.pstr[:, 128 + j * 128:128 + (j + 1) * 128],
                         spb[:, 64 + j * 128:64 + (j + 1) * 128], b.idb[:],
                         reads=("spbuf", "idb"),
                         writes=("pstr",) if j == 1 else ())
                c.op("dv", "tensor_copy", phs[0:64, 0, :],
                     b.pstr[0:64, 0:128], reads=("pstr",), writes=("phs",))
                c.op("dv", "tensor_copy", phs[:, 1:3, :],
                     b.pstr[:, 128:384].rearrange("p (j k) -> p j k", j=2),
                     reads=("pstr",), writes=("phs",))
                c.op("te", "matmul",
                     psO[hp:hp + 64, qc * 128:(qc + 1) * 128],
                     vct[0:64, h * 64:(h + 1) * 64], phs[0:64, 0, :],
                     start=True, stop=False, tile_position=(0, hp),
                     reads=("vct", "phs"), writes=(pkO,))
                for j in range(2):
                    c.op("te", "matmul",
                         psO[hp:hp + 64, qc * 128:(qc + 1) * 128],
                         vlt[:, j, h * 64:(h + 1) * 64], phs[:, 1 + j, :],
                         start=False, stop=(j == 1), tile_position=(0, hp),
                         reads=("vlt", "phs"),
                         writes=(pkO,) if j == 1 else ())
            c.op("ac", "copy", yfm[hp:hp + 64, ch, :], psO[hp:hp + 64, 0:WIN],
                 reads=(pkO,), writes=("yfm",))
        if g == 1:
            dump(b, "yfm", yfm, ("yfm",))
        for m in range(CH):
            ps, pk = b.ps[m % 4], f"ps{m % 4}"
            for k in range(CH):
                c.op("te", "matmul", ps[:, 0:WIN],
                     b.wo[:, k, m * 128:(m + 1) * 128], yfm[:, k, :],
                     start=(k == 0), stop=(k == CH - 1), reads=("yfm", "wo"),
                     writes=(pk,) if k in (0, CH - 1) else ())
            c.op("dv", "scalar_tensor_tensor", b.stream[:, m, ts],
                 ps[:, 0:WIN], b.bproj[:, m:m + 1], b.stream[:, m, ts],
                 op0=ALU.add, op1=ALU.add, reads=(pk, "bproj", "stream"),
                 writes=("stream",))
        c.flush()
    c.op("gq", "dma_start", out=b.wo[:],
         in_=b.h_wo[0].rearrange("(c p) m -> p c m", p=128), writes=("wo",),
         dsem="wo")
    c.handover(["scrf0", "scrf1"], ["ck", "cv"])
    dump(b, "postsparse", b.stream[:], ("stream",))
    c.flush()


def kernel(**inputs):
    d = prep_inputs(inputs)
    nc, b = build(elems=2, calls=2, depth=6, sparse=True)
    shared = {k: v for k, v in d.items() if k != "x_all"}
    in_maps = []
    for cidx in range(8):
        m = dict(shared)
        m["x_in"] = np.ascontiguousarray(d["x_all"][2 * cidx:2 * cidx + 2])
        in_maps.append(m)
    res = run_bass_kernel_spmd(nc, in_maps, core_ids=list(range(8)))
    out = np.concatenate([r["x_out"] for r in res.results], axis=0)
    return np.ascontiguousarray(out.transpose(0, 2, 1)).astype(np.float32)


# version 07
# speedup vs baseline: 1.0005x; 1.0005x over previous
import math
import numpy as np
import concourse.bass as bass
import concourse.mybir as mybir
from concourse.bass_utils import run_bass_kernel_spmd

F32 = mybir.dt.float32
BF16 = mybir.dt.bfloat16
AF = mybir.ActivationFunctionType
ALU = mybir.AluOpType
AX = mybir.AxisListType

NEG = float(-np.finfo(np.float32).max)

D = 1024
T = 2048
CH = 8
H = 16
HD = 64
WIN = 256
NW = 8
DI = 2730
DIP = 2816
NI = 22
G = 8
NTPC = 32
NCL = 64
CMAX = 56

CONST_VALS = [1.0, -1.0, 1.0 / 1024.0, 8.0, 0.125, 0.5, 1e-5, 1e-12, 0.0, -0.5, 1e-24]
C_ONE, C_NEG1, C_INV1024 = 0, 1, 2

XN0, XN1 = 0, 2048
QROT = 4096
KROT = 6144
VTM = 10240
VSTG = 14336
PHT = 14592
OW = 15360
PBUF = 17408
XNH = 0
SQKV = 0
SXN = 6144
SKCR = 8192
SVCT = 8704
SVLT = 9728
SPHT = 11776
SY_ = 12160
SAG = 14208
SPB = 16256
SSCR_F = 8320
SSML_F = 8644

ATTN_KEYS = ("xn0", "xn1", "qrot", "krot", "vtm", "vstg", "pht", "ow", "pbuf")
SP_KEYS = ("sqkv", "xng", "kcr", "vct", "vlt", "phs", "yfm", "aag", "spbuf",
           "sscr", "sscr2")


class Sem:
    def __init__(self, ctx, name, step):
        self.h = ctx.stack.enter_context(ctx.nc.semaphore(name))
        self.n = 0
        self.step = step
        self.name = name


class Ctx:
    LOGICAL = ("te", "dv", "ac", "po", "sy", "gq")
    PHYS = dict(te="tensor", dv="vector", ac="scalar", po="gpsimd",
                sy="sync", gq="gpsimd")

    def __init__(self, nc, stack, block):
        self.nc = nc
        self.stack = stack
        self.block = block
        self.sems = dict(
            te=Sem(self, "s_te", 1), dv=Sem(self, "s_dv", 1),
            ac=Sem(self, "s_ac", 1), po=Sem(self, "s_po", 1))
        self.dsems = {}
        self.by_name = {s.name: s for s in self.sems.values()}
        self.steps = {e: [] for e in self.LOGICAL}
        self.seq = 0
        self.writer = {}
        self.readers = {}
        self.waited = {}
        self.n_inst = 0

    def dsem(self, name):
        if name not in self.dsems:
            s = Sem(self, "d_" + name, 16)
            self.dsems[name] = s
            self.by_name[s.name] = s
        return self.dsems[name]

    def _wait(self, eng, sem, n):
        if n <= 0:
            return
        got = self.waited.get((eng, sem.name), -1)
        if got >= n:
            return
        self.waited[(eng, sem.name)] = n
        self.seq += 1
        self.steps[eng].append((self.seq, "wait_ge", (sem.h, n), {}, []))
        self.n_inst += 1

    def op(self, eng, fname, *args, reads=(), writes=(), **kw):
        dsem_name = kw.pop("dsem", None)
        for k in reads:
            w = self.writer.get(k)
            if w is not None:
                self._wait(eng, w[0], w[0].n if w[0].step == 16 else w[1])
        for k in writes:
            w = self.writer.get(k)
            if w is not None:
                self._wait(eng, w[0], w[0].n if w[0].step == 16 else w[1])
            for sname, n in self.readers.get(k, {}).items():
                s = self.by_name[sname]
                self._wait(eng, s, s.n if s.step == 16 else n)
        if fname == "dma_start":
            assert dsem_name is not None, "dma_start needs dsem"
            sem = self.dsem(dsem_name)
        else:
            sem = self.sems[eng]
        sem.n += sem.step
        self.seq += 1
        self.steps[eng].append((self.seq, fname, args, kw, [(sem.h, sem.step)]))
        self.n_inst += 1
        for k in reads:
            self.readers.setdefault(k, {})[sem.name] = sem.n
        for k in writes:
            self.writer[k] = (sem, sem.n)
            self.readers[k] = {}

    def finalize_stream(self, keys, dsem_name):
        sem = self.dsem(dsem_name)
        for k in keys:
            if k in self.writer:
                self.writer[k] = (sem, sem.n)

    def handover(self, newkeys, oldkeys):
        acc = {}
        for ok in oldkeys:
            w = self.writer.pop(ok, None)
            if w is not None:
                acc[w[0].name] = max(acc.get(w[0].name, 0), w[1])
            for sname, n in self.readers.pop(ok, {}).items():
                acc[sname] = max(acc.get(sname, 0), n)
        for nk in newkeys:
            rd = self.readers.setdefault(nk, {})
            for sname, n in acc.items():
                rd[sname] = max(rd.get(sname, 0), n)

    def barrier_wait(self, eng, keys):
        for k in keys:
            w = self.writer.get(k)
            if w is not None:
                got = self.waited.get((eng, w[0].name), -1)
                if got < w[1]:
                    self.waited[(eng, w[0].name)] = w[1]
                    self.seq += 1
                    self.steps[eng].append(
                        (self.seq, "wait_ge", (w[0].h, w[1]), {}, []))
                    self.n_inst += 1

    def flush(self):
        phys_groups = {}
        for e in self.LOGICAL:
            if self.steps[e]:
                phys_groups.setdefault(self.PHYS[e], []).extend(self.steps[e])
                self.steps[e] = []
        for phys, steps in phys_groups.items():
            steps.sort(key=lambda s: s[0])

            def emit(engine, steps=steps):
                for _, fname, args, kw, incs in steps:
                    if fname == "wait_ge":
                        engine.wait_ge(*args)
                    else:
                        inst = getattr(engine, fname)(*args, **kw)
                        for h, v in incs:
                            inst = inst.then_inc(h, v)
            getattr(self.block, phys)(emit)


def _ffn_tables():
    half = np.arange(0, HD, 2, dtype=np.float32)
    inv_freq = (1.0 / 10000.0 ** (half / HD)).astype(np.float32)
    t = np.arange(T, dtype=np.float32)
    fr = t[:, None] * inv_freq[None, :]
    fr = np.concatenate([fr, fr], 1).astype(np.float32)
    base = ((half + 0.4 * HD) / (1.4 * HD)).astype(np.float32)
    power = ((t - T // 2) / (WIN // 2)).astype(np.float32)
    sc = (base[None, :] ** power[:, None]).astype(np.float32)
    sc = np.concatenate([sc, sc], 1).astype(np.float32)
    cos = np.cos(fr).astype(np.float32)
    sin = np.sin(fr).astype(np.float32)
    inv = (1.0 / sc).astype(np.float32)

    def dev(tab):
        return np.ascontiguousarray(np.tile(tab.T, (2, 1)).astype(np.float32))
    return np.stack([dev(cos * sc), dev(sin * sc), dev(cos * inv), dev(sin * inv)])


def _sparse_tables():
    pos = np.arange(CMAX + WIN, dtype=np.float32)[:, None]
    div = np.exp(np.arange(0, HD, 2, dtype=np.float32)
                 * np.float32(-math.log(10000.0) / HD)).astype(np.float32)
    ang = (pos * div).astype(np.float32)
    S = np.stack([np.sin(ang), np.cos(ang)], -1).reshape(CMAX + WIN, HD)
    S = S.astype(np.float32)

    def dev(posarr):
        Sp = S[posarr]
        Tt = np.empty((64, len(posarr)), np.float32)
        for j in range(64):
            Tt[j] = -Sp[:, 32 + j // 2] if j % 2 == 0 else Sp[:, (j - 1) // 2]
        return np.ascontiguousarray(np.tile(Tt, (2, 1)))
    t_sq = dev(np.arange(WIN))
    t_skl = dev((np.arange(T) // WIN) * (WIN // NTPC) + np.arange(T) % WIN)
    t_skc = dev(np.minimum(np.arange(NCL), CMAX - 1))
    return t_sq, t_skl, t_skc


def _masks():
    p = np.arange(128)
    maskl = np.zeros((128, 2, 512), np.float32)
    for qc in range(2):
        qpos = qc * 128 + p
        j = np.arange(512)
        bad = (j[None, :] < qpos[:, None]) | (j[None, :] > qpos[:, None] + WIN)
        maskl[:, qc][bad] = NEG
    msp_c = np.zeros((128, G, NCL), np.float32)
    for g in range(G):
        msp_c[:, g, :][:, np.arange(NCL) >= (WIN // NTPC) * g] = NEG
    msp_l = np.zeros((128, 2, WIN), np.float32)
    for qc in range(2):
        qpos = qc * 128 + p
        j = np.arange(WIN)
        msp_l[:, qc][j[None, :] > qpos[:, None]] = NEG
    return maskl, msp_c, msp_l


def _perm_mats():
    rbase = np.zeros((128, 128), np.float32)
    for pp in range(128):
        j = pp % 64
        m = pp + 32 if j < 32 else pp - 32
        rbase[m, pp] = -1.0 if j < 32 else 1.0
    p2 = np.zeros((128, 128), np.float32)
    for pp in range(128):
        p2[pp ^ 1, pp] = 1.0
    emat = np.zeros((128, 128), np.float32)
    for c in range(8):
        for pp in range(128):
            emat[pp, c * 16 + 2 * c + (1 if pp >= 64 else 0)] = 1.0
    bh = np.zeros((16, 8, 128), np.float32)
    for c in range(8):
        for pp in range(128):
            bh[2 * c + (1 if pp >= 64 else 0), c, pp] = 1.0
    idm = np.eye(128, dtype=np.float32)
    return rbase, p2, emat, bh, idm


def prep_inputs(inp):
    d = {}
    x = np.asarray(inp["L0_attn_x"], np.float32)
    d["x_all"] = np.ascontiguousarray(x.transpose(0, 2, 1))
    d["wqkv"] = np.ascontiguousarray(np.asarray(inp["L0_ffn_Wqkv"], np.float32))
    d["wo"] = np.ascontiguousarray(np.asarray(inp["L0_ffn_Wo"], np.float32))
    w1 = np.asarray(inp["L0_ffn_W1"], np.float32)
    w1p = np.zeros((6, D, 2 * DIP), np.float32)
    w1p[:, :, :DI] = w1[:, :, :DI]
    w1p[:, :, DIP:DIP + DI] = w1[:, :, DI:]
    d["w1"] = w1p
    w2 = np.asarray(inp["L0_ffn_W2"], np.float32)
    w2p = np.zeros((6, DIP, D), np.float32)
    w2p[:, :DI] = w2
    d["w2"] = w2p
    d["wattn"] = np.ascontiguousarray(np.asarray(inp["L0_attn_w_attn"], np.float32).T)
    d["wproj"] = np.ascontiguousarray(np.asarray(inp["L0_attn_w_proj"], np.float32).T)
    wass = np.asarray(inp["L0_attn_w_assign"], np.float32)
    d["aexp"] = np.ascontiguousarray(np.repeat(wass, NTPC, axis=0).T)
    qs = np.asarray(inp["L0_ffn_q_scale"], np.float32)
    ks = np.asarray(inp["L0_ffn_k_scale"], np.float32)
    d["qs"] = np.ascontiguousarray(np.tile(qs, (1, 2)).T)
    d["ks"] = np.ascontiguousarray(np.tile(ks, (1, 2)).T)
    b1 = np.asarray(inp["L0_ffn_b1"], np.float32)
    b1a = np.zeros((6, DIP), np.float32)
    b1a[:, :DI] = b1[:, :DI]
    b1g = np.zeros((6, DIP), np.float32)
    b1g[:, :DI] = b1[:, DI:]
    b1l = np.zeros((128, 6, 2 * NI), np.float32)
    for li in range(6):
        b1l[:, li, :NI] = b1a[li].reshape(NI, 128).T
        b1l[:, li, NI:] = b1g[li].reshape(NI, 128).T
    d["b1"] = np.ascontiguousarray(b1l)
    rs = lambda a: np.ascontiguousarray(
        np.asarray(a, np.float32).reshape(6, 8, 128).transpose(2, 0, 1)
        .reshape(128, 48))
    d["b2"] = rs(inp["L0_ffn_b2"])
    d["lng"] = rs(inp["L0_ffn_ln_g"])
    d["lnb"] = rs(inp["L0_ffn_ln_b"])
    d["battn"] = np.ascontiguousarray(
        np.asarray(inp["L0_attn_b_attn"], np.float32).reshape(24, 128).T)
    d["bproj"] = np.ascontiguousarray(
        np.asarray(inp["L0_attn_b_proj"], np.float32).reshape(8, 128).T)
    d["tabs"] = _ffn_tables()
    d["t_sq"], d["t_skl"], d["t_skc"] = _sparse_tables()
    d["maskl"], d["msp_c"], d["msp_l"] = _masks()
    d["rbase"], d["p2m"], d["emat"], d["bh"], d["idm"] = _perm_mats()
    d["consts"] = np.ascontiguousarray(
        np.tile(np.array(CONST_VALS + [0.0] * (16 - len(CONST_VALS)),
                         np.float32)[None, :], (128, 1)))
    d["bconst"] = np.ascontiguousarray(
        np.tile(np.array([1.0, 1.0 / 1024.0, 0.0, 0.0], np.float32)[None, :],
                (128, 1)))
    d["ones_r"] = np.ones((1, 128), np.float32)
    d["neg_r"] = np.full((1, 128), -1.0, np.float32)
    return d


class B:
    pass


def build(elems=2, calls=2, depth=6, sparse=True, sub="both", dbg=False):
    from contextlib import ExitStack
    nc = bass.Bass("TRN2", target_bir_lowering=False)
    b = B()
    b.nc = nc
    b.elems, b.calls, b.depth, b.sparse = elems, calls, depth, sparse
    b.sub = sub
    b.dbg = dbg
    b.dbg_tensors = {}
    di = lambda n, sh: nc.dram_tensor(n, sh, F32, kind="ExternalInput")
    b.x_in = di("x_in", [elems, D, T])
    b.h_wqkv = di("wqkv", [6, D, 3 * D])
    b.h_wo = di("wo", [6, D, D])
    b.h_w1 = di("w1", [6, D, 2 * DIP])
    b.h_w2 = di("w2", [6, DIP, D])
    b.h_wattn = di("wattn", [D, 3 * D])
    b.h_wproj = di("wproj", [D, D])
    b.h_aexp = di("aexp", [D, T])
    b.h_tabs = di("tabs", [4, 128, T])
    b.h_tsq = di("t_sq", [128, WIN])
    b.h_tskl = di("t_skl", [128, T])
    b.h_tskc = di("t_skc", [128, NCL])
    b.h_maskl = di("maskl", [128, 2, 512])
    b.h_mspc = di("msp_c", [128, G, NCL])
    b.h_mspl = di("msp_l", [128, 2, WIN])
    b.h_rbase = di("rbase", [128, 128])
    b.h_p2 = di("p2m", [128, 128])
    b.h_emat = di("emat", [128, 128])
    b.h_bh = di("bh", [16, 8, 128])
    b.h_idm = di("idm", [128, 128])
    b.h_consts = di("consts", [128, 16])
    b.h_bconst = di("bconst", [128, 4])
    b.h_onesr = di("ones_r", [1, 128])
    b.h_negr = di("neg_r", [1, 128])
    b.h_qs = di("qs", [128, 6])
    b.h_ks = di("ks", [128, 6])
    b.h_b1 = di("b1", [128, 6, 2 * NI])
    b.h_b2 = di("b2", [128, 48])
    b.h_lng = di("lng", [128, 48])
    b.h_lnb = di("lnb", [128, 48])
    b.h_battn = di("battn", [128, 24])
    b.h_bproj = di("bproj", [128, 8])
    b.xpark = nc.dram_tensor("xpark", [D, T], F32, kind="Internal")
    b.x_out = nc.dram_tensor("x_out", [elems, D, T], F32, kind="ExternalOutput")

    stack = ExitStack()
    with stack:
        E = stack.enter_context
        sb = lambda n, sh, dt: E(nc.sbuf_tensor(n, sh, dt))
        b.stream = sb("stream", [128, CH, T], F32)
        b.tblw = sb("tblw", [128, 2, 4, WIN], BF16)
        b.maskl = sb("masklS", [128, 2, 512], BF16)
        b.mspc = sb("mspcS", [128, G, NCL], BF16)
        b.mspl = sb("msplS", [128, 2, WIN], BF16)
        b.consts = sb("constsS", [128, 16], F32)
        b.bconsts = sb("bconstsS", [128, 4], BF16)
        b.onesr = sb("onesrS", [1, 128], F32)
        b.negr = sb("negrS", [1, 128], F32)
        b.idb = sb("idb", [128, 128], BF16)
        b.rbase = sb("rbaseS", [128, 128], BF16)
        b.rqm = sb("rqm", [128, 128], BF16)
        b.rkm = sb("rkm", [128, 128], BF16)
        b.p2m = sb("p2mS", [128, 128], BF16)
        b.emat = sb("ematS", [128, 128], BF16)
        b.qs = sb("qsS", [128, 6], F32)
        b.ks = sb("ksS", [128, 6], F32)
        b.b1 = sb("b1S", [128, 6, 2 * NI], F32)
        b.b2 = sb("b2S", [128, 48], F32)
        b.lng = sb("lngS", [128, 48], F32)
        b.lnb = sb("lnbS", [128, 48], F32)
        b.battn = sb("battnS", [128, 24], F32)
        b.bproj = sb("bprojS", [128, 8], F32)
        b.bh = sb("bhS", [16, 8 * 128], BF16)
        b.rows = sb("rows", [1, 1024], F32)
        b.rarena = sb("rarena", [16, 512], F32)
        b.rkbf = sb("rkbf", [16, WIN], BF16)
        b.rqbf = sb("rqbf", [16, WIN], BF16)
        b.bsave = sb("bsave", [128, 4, 1], F32)
        b.sqs = sb("sqs", [128, 4, 512], BF16)
        b.scrf = sb("scrf", [128, 1152], F32)
        b.bigbuf = sb("bigbuf", [128, 24576], BF16)
        b.wo = sb("woS", [128, CH, D], BF16)
        b.wsm = sb("wsm", [128, 5632], BF16)
        b.scratch = sb("scratchS", [128, 18944], BF16)
        b.scratch_f = b.scratch.bitcast(F32)
        b.scrf_b = b.scrf.bitcast(BF16)
        b.sqs_f = b.sqs.bitcast(F32)
        b.ps = [E(nc.psum_tensor(f"ps_{i}", [128, 512], F32)) for i in range(6)]
        b.pstr = E(nc.psum_tensor("ps_tr", [128, 1024], BF16))
        b.pss = E(nc.psum_tensor("ps_s", [16, 512], F32))

        for i, v in enumerate(CONST_VALS):
            nc.const_aps.register(F32, float(v), b.consts[:, i:i + 1])

        block = E(nc.Block())
        c = Ctx(nc, stack, block)
        b.ctx = c

        emit_init(b)
        for engx in ("te", "dv", "ac", "po"):
            for sname in ("init", "initg"):
                s = c.dsem(sname)
                c._wait(engx, s, s.n)
        c.flush()
        for e in range(elems):
            emit_load_x(b, e)
            for call in range(calls):
                if call > 0 and sparse:
                    emit_sparse(b)
                emit_park_and_outer_ln(b)
                for li in range(depth):
                    emit_ffn_layer(b, li, call)
                emit_merge(b)
                c.flush()
            emit_store_x(b, e)
            c.flush()
        for s in c.dsems.values():
            c._wait("sy", s, s.n)
        c.flush()
    return nc, b


def dump(b, name, ap, reads):
    if not b.dbg or name in b.dbg_tensors:
        return
    sh = list(ap.shape)
    t = b.nc.dram_tensor("dbg_" + name, sh, ap.dtype, kind="ExternalOutput")
    b.dbg_tensors[name] = t
    b.ctx.op("sy", "dma_start", out=t.ap(), in_=ap, reads=reads,
             writes=("dbg",), dsem="dbg")


def sc_bf(b, off, shape):
    n = int(np.prod(shape[1:]))
    ap = b.scratch[0:shape[0], off:off + n]
    if len(shape) == 3:
        ap = ap.rearrange("p (a c) -> p a c", a=shape[1])
    return ap


def emit_init(b):
    c = b.ctx
    for sbuf, src, key in [
            (b.consts, b.h_consts.ap(), "consts"),
            (b.qs, b.h_qs.ap(), "qs"),
            (b.ks, b.h_ks.ap(), "ks"),
            (b.b1, b.h_b1.ap(), "b1"),
            (b.b2, b.h_b2.ap(), "b2"),
            (b.lng, b.h_lng.ap(), "lng"),
            (b.lnb, b.h_lnb.ap(), "lnb"),
            (b.battn, b.h_battn.ap(), "battn"),
            (b.onesr, b.h_onesr.ap(), "onesr"),
            (b.negr, b.h_negr.ap(), "negr"),
            (b.bproj, b.h_bproj.ap(), "bproj")]:
        c.op("sy", "dma_start", out=sbuf[:], in_=src, writes=(key,), dsem="init")
    for dst, src, key in [
            (b.idb[:], b.h_idm.ap(), "idb"),
            (b.rbase[:], b.h_rbase.ap(), "rbase"),
            (b.p2m[:], b.h_p2.ap(), "p2m"),
            (b.emat[:], b.h_emat.ap(), "emat"),
            (b.maskl[:], b.h_maskl.ap(), "maskl"),
            (b.mspc[:], b.h_mspc.ap(), "mspc"),
            (b.mspl[:], b.h_mspl.ap(), "mspl"),
            (b.bh[:], b.h_bh.rearrange("h c p -> h (c p)"), "bh"),
            (b.bconsts[:], b.h_bconst.ap(), "bconsts"),
            (b.wo[:], b.h_wo[0].rearrange("(c p) m -> p c m", p=128), "wo")]:
        c.op("gq", "dma_start", out=dst, in_=src, writes=(key,), dsem="initg")


def emit_load_x(b, e):
    c = b.ctx
    c.op("sy", "dma_start", out=b.stream[:],
         in_=b.x_in[e].rearrange("(c p) t -> p c t", p=128),
         writes=("stream",), dsem="xin")
    c.flush()


def emit_store_x(b, e):
    c = b.ctx
    c.op("sy", "dma_start", out=b.x_out[e].rearrange("(c p) t -> p c t", p=128),
         in_=b.stream[:], reads=("stream",), writes=("xout",), dsem="xout")


def _ln_rows(b, n):
    c = b.ctx
    MU, RSTD = 0, 512
    c.op("dv", "tensor_copy", b.rows[:, MU:MU + n], b.pss[0:1, 0:n],
         reads=("pss",), writes=("rows",))
    c.op("dv", "tensor_tensor", b.rows[:, RSTD:RSTD + n], b.rows[:, MU:MU + n],
         b.rows[:, MU:MU + n], op=ALU.mult, reads=("rows",), writes=("rows",))
    c.op("dv", "scalar_tensor_tensor", b.rows[:, RSTD:RSTD + n],
         b.ps[4][0:1, 0:n], 1e-5, b.rows[:, RSTD:RSTD + n],
         op0=ALU.add, op1=ALU.subtract, reads=("ps4", "rows"), writes=("rows",))
    c.op("ac", "activation", out=b.rows[:, RSTD:RSTD + n],
         in_=b.rows[:, RSTD:RSTD + n], func=AF.Ln, reads=("rows",),
         writes=("rows",))
    c.op("ac", "activation", out=b.rows[:, RSTD:RSTD + n],
         in_=b.rows[:, RSTD:RSTD + n], func=AF.Exp, scale=-0.5,
         reads=("rows",), writes=("rows",))
    c.op("dv", "tensor_tensor", b.rows[:, MU:MU + n], b.rows[:, MU:MU + n],
         b.rows[:, RSTD:RSTD + n], op=ALU.mult, reads=("rows",), writes=("rows",))


def _ln_stats(b, src_fn, n):
    c = b.ctx
    for ch in range(CH):
        slot = ch % 2
        c.op("dv", "tensor_copy", b.sqs[:, slot, 0:n], src_fn(ch),
             reads=("stream",), writes=(f"sqs{slot}",))
        c.op("ac", "activation", out=b.sqs[:, 2 + slot, 0:n],
             in_=b.sqs[:, slot, 0:n], func=AF.Square,
             reads=(f"sqs{slot}",), writes=(f"sqs{2 + slot}",))
        c.op("te", "matmul", b.pss[0:1, 0:n], b.bconsts[:, 1:2],
             b.sqs[:, slot, 0:n], start=(ch == 0), stop=(ch == CH - 1),
             reads=(f"sqs{slot}", "bconsts"),
             writes=("pss",) if ch in (0, CH - 1) else ())
        c.op("te", "matmul", b.ps[4][0:1, 0:n], b.bconsts[:, 1:2],
             b.sqs[:, 2 + slot, 0:n], start=(ch == 0), stop=(ch == CH - 1),
             reads=(f"sqs{2 + slot}", "bconsts"),
             writes=("ps4",) if ch in (0, CH - 1) else ())
    _ln_rows(b, n)


def _ln_bcasts(b, n):
    c = b.ctx
    MU, RSTD = 0, 512
    c.op("te", "matmul", b.ps[2][:, 0:n], b.onesr[:],
         b.rows[0:1, RSTD:RSTD + n], start=True, stop=True,
         reads=("rows", "onesr"), writes=("ps2",))
    c.op("te", "matmul", b.ps[3][:, 0:n], b.negr[:],
         b.rows[0:1, MU:MU + n], start=True, stop=True,
         reads=("rows", "negr"), writes=("ps3",))


def emit_park_and_outer_ln(b):
    c = b.ctx
    c.op("sy", "dma_start", out=b.xpark.rearrange("(c p) t -> p c t", p=128),
         in_=b.stream[:], reads=("stream",), writes=("xpark",), dsem="xpark")
    for nt in range(4):
        ts = slice(nt * 512, (nt + 1) * 512)
        if nt == 0:
            c.handover(["scrf0", "scrf1"],
                       ["ck", "cv", "sc0", "sc1", "sc2", "sc3"])
            c.handover(["scrf2", "scrf3"],
                       [f"nm{s}" for s in range(4)]
                       + [f"zs{s}" for s in range(4)])
        _ln_stats(b, lambda ch: b.stream[:, ch, ts], 512)
        _ln_bcasts(b, 512)
        for ch in range(CH):
            c.op("dv", "tensor_tensor", b.scrf[:, 0:512], b.stream[:, ch, ts],
                 b.ps[2][:], op=ALU.mult, reads=("stream", "ps2"),
                 writes=("scrf0",))
            c.op("dv", "tensor_tensor", b.stream[:, ch, ts], b.scrf[:, 0:512],
                 b.ps[3][:], op=ALU.add, reads=("scrf0", "ps3"),
                 writes=("stream",))
    b.n_ln = getattr(b, "n_ln", 0) + 1
    dump(b, f"postln{b.n_ln}", b.stream[:], ("stream",))
    c.flush()


def emit_merge(b):
    c = b.ctx
    b.n_merge = getattr(b, "n_merge", 0) + 1
    xp = b.scratch_f
    c.handover(["xmerge"], ["xnh"])
    for nt in range(4):
        ts = slice(nt * 512, (nt + 1) * 512)
        c.op("sy", "dma_start",
             out=xp[:, 0:4096].rearrange("p (c t) -> p c t", c=8),
             in_=b.xpark.rearrange("(c p) t -> p c t", p=128)[:, :, ts],
             reads=("xpark",), writes=("xmerge",), dsem="xmerge")
        for ch in range(CH):
            c.op("dv", "scalar_tensor_tensor", b.stream[:, ch, ts],
                 b.stream[:, ch, ts], 0.5, xp[:, ch * 512:(ch + 1) * 512],
                 op0=ALU.mult, op1=ALU.add,
                 reads=("stream", "xmerge"), writes=("stream",))
        c.flush()
    dump(b, f"postmerge{b.n_merge}", b.stream[:], ("stream",))


def emit_ffn_layer(b, li, call):
    c = b.ctx
    depth, calls, sparse = b.depth, b.calls, b.sparse
    c.op("ac", "activation", out=b.rqm[:], in_=b.rbase[:], func=AF.Identity,
         scale=b.qs[:, li:li + 1], reads=("rbase", "qs"), writes=("rqm",))
    c.op("ac", "activation", out=b.rkm[:], in_=b.rbase[:], func=AF.Identity,
         scale=b.ks[:, li:li + 1], reads=("rbase", "ks"), writes=("rkm",))
    c.handover([f"wqkv{sl}" for sl in range(6)], ["Hbuf"])
    for sl in range(6):
        c.op("gq", "dma_start",
             out=b.bigbuf[:, sl * 4096:(sl + 1) * 4096].rearrange(
                 "p (k m) -> p k m", k=8),
             in_=b.h_wqkv[li].rearrange("(k p) m -> p k m", p=128)[
                 :, :, sl * 512:(sl + 1) * 512],
             writes=(f"wqkv{sl}",), dsem="wqkv")
    c.finalize_stream([f"wqkv{sl}" for sl in range(6)], "wqkv")
    c.handover(list(ATTN_KEYS), ["xnh", "xmerge"] + list(SP_KEYS))
    c.flush()

    def wqkv_ap(k, mcol0):
        sl = mcol0 // 512
        off = sl * 4096 + k * 512 + (mcol0 - sl * 512)
        return b.bigbuf[:, off:off + 128]

    xn = sc_bf(b, XN0, (128, 8, WIN))
    qrotP = [sc_bf(b, QROT, (128, 8, WIN)),
             sc_bf(b, XN1, (128, 8, WIN))]
    wsmv = b.wsm[:, 0:4096].rearrange("p (ch t) -> p ch t", ch=8)
    krotP = [sc_bf(b, KROT, (128, 8, 512)), wsmv]
    vtm = sc_bf(b, VTM, (128, 4, 1024))
    vstg = sc_bf(b, VSTG, (128, WIN))
    pht = sc_bf(b, PHT, (128, 2, 3 * 128)).rearrange(
        "p q (j k) -> p q j k", j=3)
    ow = sc_bf(b, OW, (128, 8, WIN))
    pbuf = sc_bf(b, PBUF, (128, 4, 384))
    c.handover(["krot1"], [f"w1{x}{s}" for x in "ag" for s in (0, 1)]
               + ["w2s0", "w2s1"])

    def thunks_b(w):
        th = []
        par = w % 2
        qrot, krot = qrotP[par], krotP[par]
        qk, kk = f"qrot{par}", f"krot{par}"
        t0 = w * WIN

        def cast_one(ch):
            if ch < 4:
                c.op("dv", "tensor_copy", xn[:, ch, :],
                     b.stream[:, ch, t0:t0 + WIN], reads=("stream",),
                     writes=("xn",))
            elif t0 == 0:
                c.op("dv", "tensor_copy", xn[:, ch, 1:WIN],
                     b.stream[:, ch, 0:WIN - 1], reads=("stream",),
                     writes=("xn",))
                c.op("po", "memset", xn[:, ch, 0:1], 0.0, writes=("xn",))
            else:
                c.op("dv", "tensor_copy", xn[:, ch, :],
                     b.stream[:, ch, t0 - 1:t0 + WIN - 1], reads=("stream",),
                     writes=("xn",))
        for ch in range(CH):
            th.append(lambda ch=ch: cast_one(ch))

        def tbl_one(t):
            c.op("gq", "dma_start", out=b.tblw[:, par, t, :],
                 in_=b.h_tabs[t][:, t0:t0 + WIN],
                 writes=(f"tblw{par}",), dsem=f"tblw{par}")
        for t in range(4):
            th.append(lambda t=t: tbl_one(t))

        def proj_group(m):
            ps, pk = b.ps[m % 3], f"ps{m % 3}"
            for k in range(CH):
                c.op("te", "matmul", ps[:, 0:WIN], wqkv_ap(k, m * 128),
                     xn[:, k, :], start=(k == 0), stop=(k == CH - 1),
                     reads=("xn", f"wqkv{(m * 128) // 512}"),
                     writes=(pk,) if k in (0, CH - 1) else ())
            if m < 8:
                c.op("ac", "copy", qrot[:, m, :], ps[:, 0:WIN], reads=(pk,),
                     writes=(qk,))
            elif m < 16:
                c.op("ac", "copy", krot[:, m - 8, WIN:512], ps[:, 0:WIN],
                     reads=(pk,), writes=(kk,))
            else:
                ch = m - 16
                c.op("ac", "copy", vstg[:], ps[:, 0:WIN], reads=(pk,),
                     writes=("vstg",))
                for j in range(2):
                    c.op("te", "transpose",
                         b.pstr[:, 768 + j * 128:768 + (j + 1) * 128],
                         vstg[:, j * 128:(j + 1) * 128], b.idb[:],
                         reads=("vstg", "idb"),
                         writes=("pstrv",) if j in (0, 1) else ())
                c.op("dv", "tensor_copy",
                     vtm[:, 2:4, ch * 128:(ch + 1) * 128],
                     b.pstr[:, 768:1024].rearrange("p (j k) -> p j k", j=2),
                     reads=("pstrv",), writes=("vtm",))
        for m in range(16):
            th.append(lambda m=m: proj_group(m))

        def norm_chunk(which, ch):
            slot = ch % 2
            roff = 0 if which == "q" else WIN
            src_ = qrot[:, ch, :] if which == "q" else krot[:, ch, WIN:512]
            skey = qk if which == "q" else kk
            c.op("ac", "activation", out=b.sqs[:, slot, 0:WIN], in_=src_,
                 func=AF.Square, reads=(skey,), writes=(f"sqs{slot}",))
            c.op("te", "matmul", b.pss[:, roff:roff + WIN],
                 b.emat[:, ch * 16:(ch + 1) * 16], b.sqs[:, slot, 0:WIN],
                 start=(ch == 0), stop=(ch == CH - 1),
                 reads=(f"sqs{slot}", "emat"),
                 writes=("pss",) if ch in (0, CH - 1) else ())

        def norm_fin(which):
            roff = 0 if which == "q" else WIN
            ra = b.rarena[:, roff:roff + WIN]
            c.op("dv", "tensor_scalar", out=ra, in0=b.pss[:, roff:roff + WIN],
                 scalar1=1e-24, scalar2=None, op0=ALU.max, reads=("pss",),
                 writes=("rarena",))
            c.op("ac", "activation", out=ra, in_=ra, func=AF.Ln,
                 reads=("rarena",), writes=("rarena",))
            c.op("ac", "activation", out=ra, in_=ra, func=AF.Exp, scale=-0.5,
                 reads=("rarena",), writes=("rarena",))
            if which == "q":
                c.op("ac", "copy", b.rqbf[:], b.rarena[:, 0:WIN],
                     reads=("rarena",), writes=("rqbf",))
            else:
                c.op("ac", "copy", b.rkbf[:], b.rarena[:, WIN:2 * WIN],
                     reads=("rarena",), writes=("rkbf",))
        for ch in range(CH):
            th.append(lambda ch=ch: norm_chunk("q", ch))
        th.append(lambda: norm_fin("q"))
        for ch in range(CH):
            th.append(lambda ch=ch: norm_chunk("k", ch))
        th.append(lambda: norm_fin("k"))

        def rot_chunk(which, ch):
            tkey = f"tblw{par}"
            tw = lambda t: b.tblw[:, par, t, :]
            ps, pk = b.ps[ch % 2], f"ps{ch % 2}"
            psr, pkr = b.ps[2], "ps2"
            t1 = b.sqs_f[:, 2, :]
            t2 = b.sqs_f[:, 3, :]
            if which == "q":
                c.op("te", "matmul", ps[:, 0:WIN], b.rqm[:], qrot[:, ch, :],
                     start=True, stop=True, reads=("rqm", qk), writes=(pk,))
                c.op("te", "matmul", psr[:, 0:WIN],
                     b.bh[:, ch * 128:(ch + 1) * 128], b.rqbf[:],
                     start=True, stop=True, reads=("bh", "rqbf"),
                     writes=(pkr,))
                c.op("dv", "tensor_tensor", t1, ps[:, 0:WIN],
                     tw(1), op=ALU.mult, reads=(pk, tkey), writes=("sqs2",))
                c.op("dv", "tensor_tensor", t2,
                     qrot[:, ch, :], tw(0), op=ALU.mult, reads=(qk, tkey),
                     writes=("sqs3",))
                c.op("dv", "scalar_tensor_tensor", t1,
                     t2, b.qs[:, li:li + 1],
                     t1, op0=ALU.mult, op1=ALU.add,
                     reads=("sqs2", "sqs3", "qs"), writes=("sqs2",))
                c.op("dv", "tensor_tensor", qrot[:, ch, :], t1,
                     psr[:, 0:WIN], op=ALU.mult, reads=("sqs2", pkr),
                     writes=(qk,))
            else:
                c.op("te", "matmul", ps[:, 0:WIN], b.rkm[:],
                     krot[:, ch, WIN:512], start=True, stop=True,
                     reads=("rkm", kk), writes=(pk,))
                c.op("te", "matmul", psr[:, 0:WIN],
                     b.bh[:, ch * 128:(ch + 1) * 128], b.rkbf[:],
                     start=True, stop=True, reads=("bh", "rkbf"),
                     writes=(pkr,))
                c.op("dv", "tensor_tensor", t1, ps[:, 0:WIN],
                     tw(3), op=ALU.mult, reads=(pk, tkey), writes=("sqs2",))
                c.op("dv", "tensor_tensor", t2,
                     krot[:, ch, WIN:512], tw(2), op=ALU.mult,
                     reads=(kk, tkey), writes=("sqs3",))
                c.op("dv", "scalar_tensor_tensor", t1,
                     t2, b.ks[:, li:li + 1],
                     t1, op0=ALU.mult, op1=ALU.add,
                     reads=("sqs2", "sqs3", "ks"), writes=("sqs2",))
                c.op("dv", "tensor_tensor", krot[:, ch, WIN:512],
                     t1, psr[:, 0:WIN], op=ALU.mult,
                     reads=("sqs2", pkr), writes=(kk,))
        for ch in range(CH):
            th.append(lambda ch=ch: rot_chunk("q", ch))
        for ch in range(CH):
            th.append(lambda ch=ch: rot_chunk("k", ch))

        def vproj_group(m):
            proj_group(m)
        for m in range(16, 24):
            th.append(lambda m=m: vproj_group(m))

        def rolls():
            if w > 0:
                c.op("po", "tensor_copy", krot[:, :, 0:WIN],
                     krotP[1 - par][:, :, WIN:512],
                     reads=(f"krot{1 - par}",), writes=(kk,))
                c.op("po", "tensor_copy", vtm[:, 0:2, :], vtm[:, 2:4, :],
                     reads=("vtm",), writes=("vtm",))
        return th

    def thunks_b_ordered(w):
        th = thunks_b(w)
        par = w % 2

        def kroll():
            if w > 0:
                c.op("po", "tensor_copy", krotP[par][:, :, 0:WIN],
                     krotP[1 - par][:, :, WIN:512],
                     reads=(f"krot{1 - par}",), writes=(f"krot{par}",))

        def vroll():
            if w > 0:
                c.op("po", "tensor_copy", vtm[:, 0:2, :], vtm[:, 2:4, :],
                     reads=("vtm",), writes=("vtm",))
        return th[:8] + [kroll] + th[8:-8] + [vroll] + th[-8:]

    def thunks_a(w):
        th = []
        par = w % 2
        qrot, krot = qrotP[par], krotP[par]
        qk, kk = f"qrot{par}", f"krot{par}"
        scb = b.scrf_b
        hqs = [(h, qc) for h in range(H) for qc in range(2)]

        def krange(qc):
            if w == 0:
                return (WIN, 128 if qc == 0 else WIN)
            return (qc * 128, 384)

        def attn_s1(i):
            h, qc = hqs[i]
            ch, hp = h // 2, (h % 2) * 64
            sl = i % 4
            psS, pkS = b.ps[4 + i % 2], f"ps{4 + i % 2}"
            k0, kn = krange(qc)
            c.op("te", "matmul", psS[:, 0:kn],
                 qrot[hp:hp + 64, ch, qc * 128:(qc + 1) * 128],
                 krot[hp:hp + 64, ch, k0:k0 + kn], start=True, stop=True,
                 reads=(qk, kk), writes=(pkS,))
            c.op("dv", "scalar_tensor_tensor",
                 scb[:, sl * 384:sl * 384 + kn], psS[:, 0:kn], 8.0,
                 b.maskl[:, qc, k0:k0 + kn], op0=ALU.mult, op1=ALU.add,
                 reads=(pkS, "maskl"), writes=(f"sc{sl}",))
            c.op("dv", "reduce_max", b.scrf[:, 1024 + sl:1025 + sl],
                 scb[:, sl * 384:sl * 384 + kn], axis=AX.X, negate=True,
                 reads=(f"sc{sl}",), writes=(f"nm{sl}",))

        def attn_s2(i):
            h, qc = hqs[i]
            sl = i % 4
            k0, kn = krange(qc)
            c.op("ac", "activation", out=pbuf[:, sl, 0:kn],
                 in_=scb[:, sl * 384:sl * 384 + kn], func=AF.Exp,
                 bias=b.scrf[:, 1024 + sl:1025 + sl],
                 accum_out=b.scrf[:, 1028 + sl:1029 + sl],
                 reads=(f"sc{sl}", f"nm{sl}"),
                 writes=(f"pbuf{sl}", f"zs{sl}"))
            c.op("dv", "reciprocal", b.scrf[:, 1032 + sl:1033 + sl],
                 b.scrf[:, 1028 + sl:1029 + sl], reads=(f"zs{sl}",),
                 writes=(f"zs{sl}",))

        def attn_s3(i):
            h, qc = hqs[i]
            ch, hp = h // 2, (h % 2) * 64
            sl = i % 4
            k0, kn = krange(qc)
            psO, pkO = b.ps[3], "ps3"
            c.op("ac", "activation", out=pbuf[:, sl, 0:kn],
                 in_=pbuf[:, sl, 0:kn], func=AF.Identity,
                 scale=b.scrf[:, 1032 + sl:1033 + sl],
                 reads=(f"pbuf{sl}", f"zs{sl}"), writes=(f"pbuf{sl}",))
            nj = kn // 128
            po_ = (i % 2) * 384
            pstk = f"pstr{i % 2}"
            for jj in range(nj):
                c.op("te", "transpose",
                     b.pstr[:, po_ + jj * 128:po_ + (jj + 1) * 128],
                     pbuf[:, sl, jj * 128:(jj + 1) * 128], b.idb[:],
                     reads=(f"pbuf{sl}", "idb"),
                     writes=(pstk,) if jj in (0, nj - 1) else ())
            c.op("dv", "tensor_copy", pht[:, i % 2, 0:nj, :],
                 b.pstr[:, po_:po_ + nj * 128].rearrange(
                     "p (jx k) -> p jx k", jx=nj),
                 reads=(pstk,), writes=(f"pht{i % 2}",))
            slots = ([2] if qc == 0 else [2, 3]) if w == 0 else \
                [qc, qc + 1, qc + 2]
            for idx, vs in enumerate(slots):
                c.op("te", "matmul",
                     psO[hp:hp + 64, qc * 128:(qc + 1) * 128],
                     vtm[:, vs, h * 64:(h + 1) * 64],
                     pht[:, i % 2, idx, :],
                     start=(idx == 0), stop=(idx == len(slots) - 1),
                     tile_position=(0, hp), reads=("vtm", f"pht{i % 2}"),
                     writes=(pkO,) if idx in (0, len(slots) - 1) else ())
            if qc == 1:
                c.op("ac", "copy", ow[hp:hp + 64, ch, :],
                     psO[hp:hp + 64, 0:WIN], reads=(pkO,), writes=("ow",))

        NI_ = len(hqs)

        def step(i):
            if i < NI_:
                attn_s1(i)
            if 3 <= i < NI_ + 3:
                attn_s2(i - 3)
            if i >= 6:
                attn_s3(i - 6)
        for i in range(NI_ + 6):
            th.append(lambda i=i: step(i))

        def wo_group(m):
            ps, pk = b.ps[3 + m % 2], f"ps{3 + m % 2}"
            for k in range(CH):
                c.op("te", "matmul", ps[:, 0:WIN],
                     b.wo[:, k, m * 128:(m + 1) * 128], ow[:, k, :],
                     start=(k == 0), stop=(k == CH - 1), reads=("ow", "wo"),
                     writes=(pk,) if k in (0, CH - 1) else ())
            c.op("dv", "tensor_tensor",
                 b.stream[:, m, w * WIN:(w + 1) * WIN], ps[:, 0:WIN],
                 b.stream[:, m, w * WIN:(w + 1) * WIN], op=ALU.add,
                 reads=(pk, "stream"), writes=("stream",))
        for m in range(CH):
            th.append(lambda m=m: wo_group(m))
        return th

    def interleave(la, lb):
        ia = ib = 0
        while ia < len(la) or ib < len(lb):
            if ib < len(lb):
                lb[ib]()
                ib += 1
            if ia < len(la):
                la[ia]()
                ia += 1

    for t in thunks_b_ordered(0):
        t()
    for w in range(NW):
        A = thunks_a(w)
        B = thunks_b_ordered(w + 1) if w + 1 < NW else []
        interleave(A, B)
        c.flush()

    for i in range(4):
        c.op("dv", "tensor_copy", b.bsave[:, i, :],
             b.stream[:, 4 + i, 1023:1024], reads=("stream",),
             writes=("bsave",))
    nxt = _next_wo(b, li, call)
    if nxt is not None:
        c.op("gq", "dma_start", out=b.wo[:],
             in_=nxt.rearrange("(c p) m -> p c m", p=128), writes=("wo",),
             dsem="wo")
    c.handover(["xnh"], ["xn0", "xn1", "qrot", "krot"])
    c.handover(["scrf0", "scrf1"], ["sc0", "sc1", "sc2", "sc3"])
    c.handover(["scrf2", "scrf3"],
               [f"nm{s}" for s in range(4)] + [f"zs{s}" for s in range(4)])
    xnh = sc_bf(b, XNH, (128, 8, 1024))
    Hbuf = b.bigbuf[:, 0:NI * 1024].rearrange("p (i t) -> p i t", i=NI)
    w1a = [b.wsm[:, o:o + 1024].rearrange("p (k m) -> p k m", k=8)
           for o in (0, 1024)]
    w1g = [b.wsm[:, o:o + 1024].rearrange("p (k m) -> p k m", k=8)
           for o in (2048, 3072)]
    w2s = [b.wsm[:, o:o + 2816].rearrange("p (i m) -> p i m", i=NI)
           for o in (0, 2816)]
    for hf in (range(2) if b.sub in ("both", "ffn") else ()):
        t0 = hf * 1024
        for nt in range(2):
            tt0 = t0 + nt * 512
            w0 = (hf == 0 and nt == 0)
            bs = (hf == 1 and nt == 0)
            for ch in range(CH):
                slot = ch % 2
                if ch >= 4 and (w0 or bs):
                    c.op("dv", "tensor_copy", b.sqs[:, slot, 1:512],
                         b.stream[:, ch, tt0:tt0 + 511], reads=("stream",),
                         writes=(f"sqs{slot}",))
                    if w0:
                        c.op("po", "memset", b.sqs[:, slot, 0:1], 0.0,
                             writes=(f"sqs{slot}",))
                    else:
                        c.op("dv", "tensor_copy", b.sqs[:, slot, 0:1],
                             b.bsave[:, ch - 4, :], reads=("bsave",),
                             writes=(f"sqs{slot}",))
                else:
                    src = b.stream[:, ch, tt0:tt0 + 512] if ch < 4 else \
                        b.stream[:, ch, tt0 - 1:tt0 + 511]
                    c.op("dv", "tensor_copy", b.sqs[:, slot, 0:512], src,
                         reads=("stream",), writes=(f"sqs{slot}",))
                c.op("ac", "activation", out=b.sqs[:, 2 + slot, 0:512],
                     in_=b.sqs[:, slot, 0:512], func=AF.Square,
                     reads=(f"sqs{slot}",), writes=(f"sqs{2 + slot}",))
                c.op("te", "matmul", b.pss[0:1, 0:512], b.bconsts[:, 1:2],
                     b.sqs[:, slot, 0:512], start=(ch == 0),
                     stop=(ch == CH - 1), reads=(f"sqs{slot}", "bconsts"),
                     writes=("pss",) if ch in (0, CH - 1) else ())
                c.op("te", "matmul", b.ps[4][0:1, 0:512], b.bconsts[:, 1:2],
                     b.sqs[:, 2 + slot, 0:512], start=(ch == 0),
                     stop=(ch == CH - 1), reads=(f"sqs{2 + slot}", "bconsts"),
                     writes=("ps4",) if ch in (0, CH - 1) else ())
            _ln_rows(b, 512)
            _ln_bcasts(b, 512)
            for ch in range(CH):
                dst = xnh[:, ch, nt * 512:(nt + 1) * 512]
                if ch >= 4 and (w0 or bs):
                    c.op("dv", "tensor_tensor", b.scrf[:, 1:512],
                         b.stream[:, ch, tt0:tt0 + 511], b.ps[2][:, 1:512],
                         op=ALU.mult, reads=("stream", "ps2"),
                         writes=("scrf0",))
                    if bs:
                        c.op("dv", "tensor_tensor", b.scrf[:, 0:1],
                             b.bsave[:, ch - 4, :], b.ps[2][:, 0:1],
                             op=ALU.mult, reads=("bsave", "ps2"),
                             writes=("scrf0",))
                    else:
                        c.op("po", "memset", b.scrf[:, 0:1], 0.0,
                             writes=("scrf0",))
                else:
                    src = b.stream[:, ch, tt0:tt0 + 512] if ch < 4 else \
                        b.stream[:, ch, tt0 - 1:tt0 + 511]
                    c.op("dv", "tensor_tensor", b.scrf[:, 0:512], src,
                         b.ps[2][:], op=ALU.mult, reads=("stream", "ps2"),
                         writes=("scrf0",))
                c.op("dv", "tensor_tensor", b.scrf[:, 0:512],
                     b.scrf[:, 0:512], b.ps[3][:], op=ALU.add,
                     reads=("scrf0", "ps3"), writes=("scrf0",))
                c.op("ac", "activation", out=dst, in_=b.scrf[:, 0:512],
                     func=AF.Identity,
                     scale=b.lng[:, li * 8 + ch:li * 8 + ch + 1],
                     bias=b.lnb[:, li * 8 + ch:li * 8 + ch + 1],
                     reads=("scrf0", "lng", "lnb"), writes=("xnh",))
        c.flush()
        if hf == 0:
            c.handover(["Hbuf"], [f"wqkv{sl}" for sl in range(6)])
        c.handover([f"w1{x}{s}" for x in "ag" for s in (0, 1)],
                   ["w2s0", "w2s1"])
        for i in range(NI):
            sl = i % 2
            c.op("gq", "dma_start", out=w1a[sl][:],
                 in_=b.h_w1[li].rearrange("(k p) m -> p k m", p=128)[
                     :, :, i * 128:(i + 1) * 128], writes=(f"w1a{sl}",),
                 dsem=f"w1a{sl}")
            c.op("gq", "dma_start", out=w1g[sl][:],
                 in_=b.h_w1[li].rearrange("(k p) m -> p k m", p=128)[
                     :, :, DIP + i * 128:DIP + (i + 1) * 128],
                 writes=(f"w1g{sl}",), dsem=f"w1g{sl}")
            for nt in range(2):
                psA, pkA = b.ps[nt], f"ps{nt}"
                psB, pkB = b.ps[4 + nt], f"ps{4 + nt}"
                for k in range(CH):
                    c.op("te", "matmul", psA[:], w1a[sl][:, k, :],
                         xnh[:, k, nt * 512:(nt + 1) * 512],
                         start=(k == 0), stop=(k == CH - 1),
                         reads=("xnh", f"w1a{sl}"),
                         writes=(pkA,) if k in (0, CH - 1) else ())
                for k in range(CH):
                    c.op("te", "matmul", psB[:], w1g[sl][:, k, :],
                         xnh[:, k, nt * 512:(nt + 1) * 512],
                         start=(k == 0), stop=(k == CH - 1),
                         reads=("xnh", f"w1g{sl}"),
                         writes=(pkB,) if k in (0, CH - 1) else ())
                c.op("ac", "activation", out=b.scrf[:, 0:512], in_=psB[:],
                     func=AF.Gelu, bias=b.b1[:, li, NI + i:NI + i + 1],
                     reads=(pkB, "b1"), writes=("scrf0",))
                c.op("dv", "scalar_tensor_tensor",
                     Hbuf[:, i, nt * 512:(nt + 1) * 512], psA[:],
                     b.b1[:, li, i:i + 1], b.scrf[:, 0:512],
                     op0=ALU.add, op1=ALU.mult,
                     reads=(pkA, "b1", "scrf0"), writes=("Hbuf",))
        c.flush()
        c.handover(["w2s0", "w2s1"],
                   [f"w1{x}{s}" for x in "ag" for s in (0, 1)])
        for m in range(CH):
            sl = m % 2
            c.op("gq", "dma_start", out=w2s[sl][:],
                 in_=b.h_w2[li].rearrange("(i p) m -> p i m", p=128)[
                     :, :, m * 128:(m + 1) * 128], writes=(f"w2s{sl}",),
                 dsem=f"w2s{sl}")
            pso = (b.ps[0], b.ps[1]) if m % 2 == 0 else (b.ps[2], b.ps[3])
            pko = ("ps0", "ps1") if m % 2 == 0 else ("ps2", "ps3")
            for i in range(NI):
                for nt in range(2):
                    c.op("te", "matmul", pso[nt][:], w2s[sl][:, i, :],
                         Hbuf[:, i, nt * 512:(nt + 1) * 512],
                         start=(i == 0), stop=(i == NI - 1),
                         reads=("Hbuf", f"w2s{sl}"),
                         writes=(pko[nt],) if i in (0, NI - 1) else ())
            for nt in range(2):
                ts = slice(t0 + nt * 512, t0 + (nt + 1) * 512)
                c.op("dv", "scalar_tensor_tensor", b.stream[:, m, ts],
                     pso[nt][:], b.b2[:, li * 8 + m:li * 8 + m + 1],
                     b.stream[:, m, ts], op0=ALU.add, op1=ALU.add,
                     reads=(pko[nt], "b2", "stream"), writes=("stream",))
        c.flush()


def _next_wo(b, li, call):
    if li + 1 < b.depth:
        return b.h_wo[li + 1]
    if b.sparse and call + 1 < b.calls:
        return b.h_wproj.ap()
    return b.h_wo[0]


def emit_sparse(b):
    c = b.ctx
    qkv = sc_bf(b, SQKV, (128, 24, WIN))
    xng = sc_bf(b, SXN, (128, 8, WIN))
    kcr = sc_bf(b, SKCR, (128, 8, 64))
    vct = b.scratch[0:64, SVCT:SVCT + 1024]
    vlt = sc_bf(b, SVLT, (128, 2, 1024))
    phs = sc_bf(b, SPHT, (128, 3 * 128)).rearrange("p (j k) -> p j k", j=3)
    yfm = sc_bf(b, SY_, (128, 8, WIN))
    aag = sc_bf(b, SAG, (128, 8, WIN))
    spb = sc_bf(b, SPB, (128, 320))
    sscr = b.scratch_f[:, SSCR_F:SSCR_F + 320]
    ssml = b.scratch_f[:, SSML_F:SSML_F + 8]
    sqf = b.sqs_f[:]
    ck = b.scrf[:, 0:512].rearrange("p (ch l) -> p ch l", ch=8)
    cv = b.scrf[:, 512:1024].rearrange("p (ch l) -> p ch l", ch=8)
    c.handover(list(SP_KEYS), list(ATTN_KEYS) + ["xnh", "xmerge"])
    c.handover(["ck", "cv"], ["scrf0", "scrf1", "scrf2", "scrf3",
                               "sc0", "sc1", "sc2", "sc3"]
               + [f"nm{s}" for s in range(4)] + [f"zs{s}" for s in range(4)])
    c.op("po", "memset", b.scrf[:, 0:512], 0.0, writes=("ck",))
    c.op("po", "memset", b.scrf[:, 512:1024], 0.0, writes=("cv",))
    c.handover([f"wqkv{sl}" for sl in range(6)], ["Hbuf"])
    for sl in range(6):
        c.op("gq", "dma_start",
             out=b.bigbuf[:, sl * 4096:(sl + 1) * 4096].rearrange(
                 "p (k m) -> p k m", k=8),
             in_=b.h_wattn.rearrange("(k p) m -> p k m", p=128)[
                 :, :, sl * 512:(sl + 1) * 512],
             writes=(f"wqkv{sl}",), dsem="wqkv")
    c.finalize_stream([f"wqkv{sl}" for sl in range(6)], "wqkv")
    c.op("gq", "dma_start", out=b.wo[:],
         in_=b.h_wproj.ap().rearrange("(c p) m -> p c m", p=128),
         writes=("wo",), dsem="wo")
    c.op("gq", "dma_start", out=b.tblw[:, 0, 0, :], in_=b.h_tsq.ap(),
         writes=("tblw0",), dsem="tblw0")
    c.op("gq", "dma_start", out=b.tblw[:, 0, 1, 0:64], in_=b.h_tskc.ap(),
         writes=("tblw0",), dsem="tblw0")
    c.flush()

    def wattn_ap(k, mcol0):
        sl = mcol0 // 512
        off = sl * 4096 + k * 512 + (mcol0 - sl * 512)
        return b.bigbuf[:, off:off + 128]

    for g in range(G):
        ts = slice(g * WIN, (g + 1) * WIN)
        c.op("gq", "dma_start", out=aag[:],
             in_=b.h_aexp.rearrange("(ch p) t -> p ch t", p=128)[:, :, ts],
             writes=("aag",), dsem="aag")
        c.op("gq", "dma_start", out=b.tblw[:, 1, g % 2, :],
             in_=b.h_tskl[:, ts], writes=("tblw1",), dsem="tblw1")
        _ln_stats(b, lambda ch: b.stream[:, ch, ts], WIN)
        _ln_bcasts(b, WIN)
        for ch in range(CH):
            c.op("dv", "tensor_tensor", sqf[:, 3, :], b.stream[:, ch, ts],
                 b.ps[2][:, 0:WIN], op=ALU.mult, reads=("stream", "ps2"),
                 writes=("sqs3",))
            c.op("dv", "tensor_tensor", xng[:, ch, :], sqf[:, 3, :],
                 b.ps[3][:, 0:WIN], op=ALU.add, reads=("sqs3", "ps3"),
                 writes=("xng",))
        for m in range(24):
            ps, pk = b.ps[m % 4], f"ps{m % 4}"
            for k in range(CH):
                c.op("te", "matmul", ps[:, 0:WIN], wattn_ap(k, m * 128),
                     xng[:, k, :], start=(k == 0), stop=(k == CH - 1),
                     reads=("xng", f"wqkv{(m * 128) // 512}"),
                     writes=(pk,) if k in (0, CH - 1) else ())
            c.op("ac", "activation", out=qkv[:, m, :], in_=ps[:, 0:WIN],
                 func=AF.Identity, bias=b.battn[:, m:m + 1],
                 reads=(pk, "battn"), writes=("sqkv",))
        if g == 1:
            dump(b, "xng", xng, ("xng",))
            dump(b, "qkv_raw", qkv, ("sqkv",))
        for ch in range(CH):
            for j in range(2):
                c.op("te", "transpose", b.pstr[:, j * 128:(j + 1) * 128],
                     qkv[:, 16 + ch, j * 128:(j + 1) * 128], b.idb[:],
                     reads=("sqkv", "idb"),
                     writes=("pstr",) if j in (0, 1) else ())
            c.op("dv", "tensor_copy", vlt[:, :, ch * 128:(ch + 1) * 128],
                 b.pstr[:, 0:256].rearrange("p (j k) -> p j k", j=2),
                 reads=("pstr",), writes=("vlt",))
        for ch in range(CH):
            c.op("dv", "tensor_tensor", sqf[:, 3, :], xng[:, ch, :],
                 aag[:, ch, :], op=ALU.mult, reads=("xng", "aag"),
                 writes=("sqs3",))
            c.op("te", "matmul", b.pss[0:1, 0:WIN],
                 b.consts[:, C_ONE:C_ONE + 1], sqf[:, 3, :],
                 start=(ch == 0), stop=(ch == CH - 1),
                 reads=("sqs3", "consts"),
                 writes=("pss",) if ch in (0, CH - 1) else ())
        LG, WR, SM = 0, 256, 768
        c.op("dv", "tensor_copy", b.rows[:, LG:LG + WIN], b.pss[0:1, 0:WIN],
             reads=("pss",), writes=("rows",))
        lg3 = b.rows[:, LG:LG + WIN].rearrange("o (s t) -> o s t", s=8)
        wr3 = b.rows[:, WR:WR + WIN].rearrange("o (s t) -> o s t", s=8)
        if g == 1:
            dump(b, "row_logits", b.rows[:, LG:LG + WIN], ("rows",))
        c.op("dv", "reduce_max", b.rows[:, SM:SM + 8], lg3, axis=AX.X,
             negate=True, reads=("rows",), writes=("rows",))
        if g == 1:
            dump(b, "row_negmax", b.rows[:, SM:SM + 8], ("rows",))
        c.op("dv", "tensor_tensor", lg3, lg3,
             b.rows[:, SM:SM + 8][:, :, None].broadcast_to((1, 8, 32)),
             op=ALU.add, reads=("rows",), writes=("rows",))
        if g == 1:
            dump(b, "row_sub", b.rows[:, LG:LG + WIN], ("rows",))
        c.op("ac", "activation", out=b.rows[:, LG:LG + WIN],
             in_=b.rows[:, LG:LG + WIN], func=AF.Exp, reads=("rows",),
             writes=("rows",))
        if g == 1:
            dump(b, "row_exp", b.rows[:, LG:LG + WIN], ("rows",))
        c.op("dv", "reduce_sum", b.rows[:, SM:SM + 8], lg3, axis=AX.X,
             reads=("rows",), writes=("rows",))
        c.op("dv", "reciprocal", b.rows[:, SM:SM + 8], b.rows[:, SM:SM + 8],
             reads=("rows",), writes=("rows",))
        if g == 1:
            dump(b, "row_rz", b.rows[:, SM:SM + 8], ("rows",))
        c.op("dv", "tensor_tensor", wr3, lg3,
             b.rows[:, SM:SM + 8][:, :, None].broadcast_to((1, 8, 32)),
             op=ALU.mult, reads=("rows",), writes=("rows",))
        if g == 1:
            dump(b, "wrow", b.rows[:, WR:WR + WIN], ("rows",))
        c.op("te", "matmul", b.ps[4][:, 0:WIN], b.onesr[:],
             b.rows[0:1, WR:WR + WIN], start=True, stop=True,
             reads=("rows", "onesr"), writes=("ps4",))
        for ch in range(CH):
            c.op("dv", "tensor_tensor", sqf[:, 3, :], qkv[:, 8 + ch, :],
                 b.ps[4][:, 0:WIN], op=ALU.mult, reads=("sqkv", "ps4"),
                 writes=("sqs3",))
            c.op("dv", "tensor_reduce", ck[:, ch, g * 8:(g + 1) * 8],
                 sqf[:, 3, :].rearrange("p (s t) -> p s t", s=8),
                 axis=AX.X, op=ALU.add, reads=("sqs3",), writes=("ck",))
            c.op("dv", "tensor_tensor", sqf[:, 3, :], qkv[:, 16 + ch, :],
                 b.ps[4][:, 0:WIN], op=ALU.mult, reads=("sqkv", "ps4"),
                 writes=("sqs3",))
            c.op("dv", "tensor_reduce", cv[:, ch, g * 8:(g + 1) * 8],
                 sqf[:, 3, :].rearrange("p (s t) -> p s t", s=8),
                 axis=AX.X, op=ALU.add, reads=("sqs3",), writes=("cv",))
        if g == 1:
            dump(b, "ck", b.scrf[:, 0:512], ("ck",))
            dump(b, "cv", b.scrf[:, 512:1024], ("cv",))
        for ch in range(CH):
            ps, pk = b.ps[2 + ch % 2], f"ps{2 + ch % 2}"
            c.op("te", "matmul", ps[:, 0:WIN], b.p2m[:], qkv[:, ch, :],
                 start=True, stop=True, reads=("p2m", "sqkv"), writes=(pk,))
            c.op("dv", "tensor_tensor", qkv[:, ch, :], ps[:, 0:WIN],
                 b.tblw[:, 0, 0, :], op=ALU.mult, reads=(pk, "tblw0"),
                 writes=("sqkv",))
            c.op("te", "matmul", ps[:, 0:WIN], b.p2m[:], qkv[:, 8 + ch, :],
                 start=True, stop=True, reads=("p2m", "sqkv"), writes=(pk,))
            c.op("dv", "tensor_tensor", qkv[:, 8 + ch, :], ps[:, 0:WIN],
                 b.tblw[:, 1, g % 2, :], op=ALU.mult,
                 reads=(pk, "tblw1"), writes=("sqkv",))
        for ch in range(CH):
            c.op("ac", "copy", b.sqs[:, 0, 0:64], ck[:, ch, :],
                 reads=("ck",), writes=("sqs0",))
            c.op("te", "matmul", b.ps[4][:, 0:64], b.p2m[:],
                 b.sqs[:, 0, 0:64], start=True, stop=True,
                 reads=("p2m", "sqs0"), writes=("ps4",))
            c.op("dv", "tensor_tensor", kcr[:, ch, :], b.ps[4][:, 0:64],
                 b.tblw[:, 0, 1, 0:64], op=ALU.mult, reads=("ps4", "tblw0"),
                 writes=("kcr",))
            c.op("ac", "copy", b.sqs[:, 1, 0:64], cv[:, ch, :],
                 reads=("cv",), writes=("sqs1",))
            for half in range(2):
                h2, hp = ch * 2 + half, half * 64
                c.op("te", "transpose", b.pstr[0:64, 0:64],
                     b.sqs[hp:hp + 64, 1, 0:64], b.idb[hp:hp + 64, hp:hp + 64],
                     reads=("sqs1", "idb"), writes=("pstr",))
                c.op("dv", "tensor_copy", vct[:, h2 * 64:(h2 + 1) * 64],
                     b.pstr[0:64, 0:64], reads=("pstr",), writes=("vct",))
        if g == 1:
            dump(b, "qkv_rot", qkv, ("sqkv",))
            dump(b, "kcr", kcr, ("kcr",))
            dump(b, "vct", vct, ("vct",))
        c.flush()
        for h in range(H):
            ch, hp = h // 2, (h % 2) * 64
            psS, pkS = b.ps[4 + h % 2], f"ps{4 + h % 2}"
            psO, pkO = b.ps[2 + h % 2], f"ps{2 + h % 2}"
            for qc in range(2):
                c.op("te", "matmul", psS[:, 0:64],
                     qkv[hp:hp + 64, ch, qc * 128:(qc + 1) * 128],
                     kcr[hp:hp + 64, ch, :], start=True, stop=True,
                     reads=("sqkv", "kcr"), writes=(pkS,))
                c.op("te", "matmul", psS[:, 64:320],
                     qkv[hp:hp + 64, ch, qc * 128:(qc + 1) * 128],
                     qkv[hp:hp + 64, 8 + ch, :], start=True, stop=True,
                     reads=("sqkv",), writes=(pkS,))
                c.op("dv", "scalar_tensor_tensor", sscr[:, 0:64],
                     psS[:, 0:64], 0.125, b.mspc[:, g, :], op0=ALU.mult,
                     op1=ALU.add, reads=(pkS, "mspc"), writes=("sscr",))
                c.op("dv", "scalar_tensor_tensor", sscr[:, 64:320],
                     psS[:, 64:320], 0.125, b.mspl[:, qc, :], op0=ALU.mult,
                     op1=ALU.add, reads=(pkS, "mspl"), writes=("sscr",))
                c.op("dv", "reduce_max", ssml[:, 0:1], sscr[:, 0:320],
                     axis=AX.X, negate=True, reads=("sscr",),
                     writes=("sscr2",))
                c.op("ac", "activation", out=spb[:], in_=sscr[:, 0:320],
                     func=AF.Exp, bias=ssml[:, 0:1], accum_out=ssml[:, 2:3],
                     reads=("sscr", "sscr2"), writes=("spbuf", "sscr2"))
                if g == 1 and h == 0 and qc == 0:
                    dump(b, "scores00", sscr[:, 0:320], ("sscr",))
                c.op("dv", "reciprocal", ssml[:, 4:5], ssml[:, 2:3],
                     reads=("sscr2",), writes=("sscr2",))
                c.op("ac", "activation", out=spb[:], in_=spb[:],
                     func=AF.Identity, scale=ssml[:, 4:5],
                     reads=("spbuf", "sscr2"), writes=("spbuf",))
                c.op("te", "transpose", b.pstr[0:64, 0:128], spb[:, 0:64],
                     b.idb[:], reads=("spbuf", "idb"), writes=("pstr",))
                for j in range(2):
                    c.op("te", "transpose",
                         b.pstr[:, 128 + j * 128:128 + (j + 1) * 128],
                         spb[:, 64 + j * 128:64 + (j + 1) * 128], b.idb[:],
                         reads=("spbuf", "idb"),
                         writes=("pstr",) if j == 1 else ())
                c.op("dv", "tensor_copy", phs[0:64, 0, :],
                     b.pstr[0:64, 0:128], reads=("pstr",), writes=("phs",))
                c.op("dv", "tensor_copy", phs[:, 1:3, :],
                     b.pstr[:, 128:384].rearrange("p (j k) -> p j k", j=2),
                     reads=("pstr",), writes=("phs",))
                c.op("te", "matmul",
                     psO[hp:hp + 64, qc * 128:(qc + 1) * 128],
                     vct[0:64, h * 64:(h + 1) * 64], phs[0:64, 0, :],
                     start=True, stop=False, tile_position=(0, hp),
                     reads=("vct", "phs"), writes=(pkO,))
                for j in range(2):
                    c.op("te", "matmul",
                         psO[hp:hp + 64, qc * 128:(qc + 1) * 128],
                         vlt[:, j, h * 64:(h + 1) * 64], phs[:, 1 + j, :],
                         start=False, stop=(j == 1), tile_position=(0, hp),
                         reads=("vlt", "phs"),
                         writes=(pkO,) if j == 1 else ())
            c.op("ac", "copy", yfm[hp:hp + 64, ch, :], psO[hp:hp + 64, 0:WIN],
                 reads=(pkO,), writes=("yfm",))
        if g == 1:
            dump(b, "yfm", yfm, ("yfm",))
        for m in range(CH):
            ps, pk = b.ps[m % 4], f"ps{m % 4}"
            for k in range(CH):
                c.op("te", "matmul", ps[:, 0:WIN],
                     b.wo[:, k, m * 128:(m + 1) * 128], yfm[:, k, :],
                     start=(k == 0), stop=(k == CH - 1), reads=("yfm", "wo"),
                     writes=(pk,) if k in (0, CH - 1) else ())
            c.op("dv", "scalar_tensor_tensor", b.stream[:, m, ts],
                 ps[:, 0:WIN], b.bproj[:, m:m + 1], b.stream[:, m, ts],
                 op0=ALU.add, op1=ALU.add, reads=(pk, "bproj", "stream"),
                 writes=("stream",))
        c.flush()
    c.op("gq", "dma_start", out=b.wo[:],
         in_=b.h_wo[0].rearrange("(c p) m -> p c m", p=128), writes=("wo",),
         dsem="wo")
    c.handover(["scrf0", "scrf1"], ["ck", "cv"])
    dump(b, "postsparse", b.stream[:], ("stream",))
    c.flush()


def kernel(**inputs):
    d = prep_inputs(inputs)
    nc, b = build(elems=2, calls=2, depth=6, sparse=True)
    shared = {k: v for k, v in d.items() if k != "x_all"}
    in_maps = []
    for cidx in range(8):
        m = dict(shared)
        m["x_in"] = np.ascontiguousarray(d["x_all"][2 * cidx:2 * cidx + 2])
        in_maps.append(m)
    res = run_bass_kernel_spmd(nc, in_maps, core_ids=list(range(8)))
    out = np.concatenate([r["x_out"] for r in res.results], axis=0)
    return np.ascontiguousarray(out.transpose(0, 2, 1)).astype(np.float32)


# version 10
# speedup vs baseline: 1.2131x; 1.2125x over previous
import math
import numpy as np
import concourse.bass as bass
import concourse.mybir as mybir
from concourse.bass_utils import run_bass_kernel_spmd

F32 = mybir.dt.float32
BF16 = mybir.dt.bfloat16
AF = mybir.ActivationFunctionType
ALU = mybir.AluOpType
AX = mybir.AxisListType

NEG = float(-np.finfo(np.float32).max)

D = 1024
T = 2048
CH = 8
H = 16
HD = 64
WIN = 256
NW = 8
DI = 2730
DIP = 2816
NI = 22
G = 8
NTPC = 32
NCL = 64
CMAX = 56

CONST_VALS = [1.0, -1.0, 1.0 / 1024.0, 8.0, 0.125, 0.5, 1e-5, 1e-12, 0.0, -0.5, 1e-24]
C_ONE, C_NEG1, C_INV1024 = 0, 1, 2

XN0, XN1 = 0, 2048
QROT = 4096
KROT = 6144
VTM = 10240
VSTG = 14336
PHT = 14592
OW = 15360
PBUF = 17408
XNH = 0
SQKV = 0
SXN = 6144
SKCR = 8192
SVCT = 8704
SVLT = 9728
SPHT = 11776
SY_ = 12160
SAG = 14208
SPB = 16256
SSCR_F = 8320
SSML_F = 8644

ATTN_KEYS = ("xn0", "xn1", "qrot", "krot", "vtm", "vstg", "pht", "ow", "pbuf")
SP_KEYS = ("sqkv", "xng", "kcr", "vct", "vlt", "phs", "yfm", "aag", "spbuf",
           "sscr", "sscr2")


class Sem:
    def __init__(self, ctx, name, step):
        self.h = ctx.stack.enter_context(ctx.nc.semaphore(name))
        self.n = 0
        self.step = step
        self.name = name


class Ctx:
    LOGICAL = ("te", "dv", "ac", "po", "sy", "gq")
    PHYS = dict(te="tensor", dv="vector", ac="scalar", po="gpsimd",
                sy="sync", gq="gpsimd")

    def __init__(self, nc, stack, block):
        self.nc = nc
        self.stack = stack
        self.block = block
        self.sems = dict(
            te=Sem(self, "s_te", 1), dv=Sem(self, "s_dv", 1),
            ac=Sem(self, "s_ac", 1), po=Sem(self, "s_po", 1))
        self.dsems = {}
        self.by_name = {s.name: s for s in self.sems.values()}
        self.steps = {e: [] for e in self.LOGICAL}
        self.seq = 0
        self.writer = {}
        self.readers = {}
        self.waited = {}
        self.n_inst = 0

    def dsem(self, name):
        if name not in self.dsems:
            s = Sem(self, "d_" + name, 16)
            self.dsems[name] = s
            self.by_name[s.name] = s
        return self.dsems[name]

    def _wait(self, eng, sem, n):
        if n <= 0:
            return
        got = self.waited.get((eng, sem.name), -1)
        if got >= n:
            return
        self.waited[(eng, sem.name)] = n
        self.seq += 1
        self.steps[eng].append((self.seq, "wait_ge", (sem.h, n), {}, []))
        self.n_inst += 1

    def op(self, eng, fname, *args, reads=(), writes=(), **kw):
        dsem_name = kw.pop("dsem", None)
        for k in reads:
            w = self.writer.get(k)
            if w is not None:
                self._wait(eng, w[0], w[0].n if w[0].step == 16 else w[1])
        for k in writes:
            w = self.writer.get(k)
            if w is not None:
                self._wait(eng, w[0], w[0].n if w[0].step == 16 else w[1])
            for sname, n in self.readers.get(k, {}).items():
                s = self.by_name[sname]
                self._wait(eng, s, s.n if s.step == 16 else n)
        if fname == "dma_start":
            assert dsem_name is not None, "dma_start needs dsem"
            sem = self.dsem(dsem_name)
        else:
            sem = self.sems[eng]
        sem.n += sem.step
        self.seq += 1
        self.steps[eng].append((self.seq, fname, args, kw, [(sem.h, sem.step)]))
        self.n_inst += 1
        for k in reads:
            self.readers.setdefault(k, {})[sem.name] = sem.n
        for k in writes:
            self.writer[k] = (sem, sem.n)
            self.readers[k] = {}

    def finalize_stream(self, keys, dsem_name):
        sem = self.dsem(dsem_name)
        for k in keys:
            if k in self.writer:
                self.writer[k] = (sem, sem.n)

    def handover(self, newkeys, oldkeys):
        acc = {}
        for ok in oldkeys:
            w = self.writer.pop(ok, None)
            if w is not None:
                acc[w[0].name] = max(acc.get(w[0].name, 0), w[1])
            for sname, n in self.readers.pop(ok, {}).items():
                acc[sname] = max(acc.get(sname, 0), n)
        for nk in newkeys:
            rd = self.readers.setdefault(nk, {})
            for sname, n in acc.items():
                rd[sname] = max(rd.get(sname, 0), n)

    def barrier_wait(self, eng, keys):
        for k in keys:
            w = self.writer.get(k)
            if w is not None:
                got = self.waited.get((eng, w[0].name), -1)
                if got < w[1]:
                    self.waited[(eng, w[0].name)] = w[1]
                    self.seq += 1
                    self.steps[eng].append(
                        (self.seq, "wait_ge", (w[0].h, w[1]), {}, []))
                    self.n_inst += 1

    def flush(self):
        phys_groups = {}
        for e in self.LOGICAL:
            if self.steps[e]:
                phys_groups.setdefault(self.PHYS[e], []).extend(self.steps[e])
                self.steps[e] = []
        for phys, steps in phys_groups.items():
            steps.sort(key=lambda s: s[0])

            def emit(engine, steps=steps):
                for _, fname, args, kw, incs in steps:
                    if fname == "wait_ge":
                        engine.wait_ge(*args)
                    else:
                        inst = getattr(engine, fname)(*args, **kw)
                        for h, v in incs:
                            inst = inst.then_inc(h, v)
            getattr(self.block, phys)(emit)


def _ffn_tables():
    half = np.arange(0, HD, 2, dtype=np.float32)
    inv_freq = (1.0 / 10000.0 ** (half / HD)).astype(np.float32)
    t = np.arange(T, dtype=np.float32)
    fr = t[:, None] * inv_freq[None, :]
    fr = np.concatenate([fr, fr], 1).astype(np.float32)
    base = ((half + 0.4 * HD) / (1.4 * HD)).astype(np.float32)
    power = ((t - T // 2) / (WIN // 2)).astype(np.float32)
    sc = (base[None, :] ** power[:, None]).astype(np.float32)
    sc = np.concatenate([sc, sc], 1).astype(np.float32)
    cos = np.cos(fr).astype(np.float32)
    sin = np.sin(fr).astype(np.float32)
    inv = (1.0 / sc).astype(np.float32)

    def dev(tab):
        return np.ascontiguousarray(np.tile(tab.T, (2, 1)).astype(np.float32))
    return np.stack([dev(cos * sc), dev(sin * sc), dev(cos * inv), dev(sin * inv)])


def _sparse_tables():
    pos = np.arange(CMAX + WIN, dtype=np.float32)[:, None]
    div = np.exp(np.arange(0, HD, 2, dtype=np.float32)
                 * np.float32(-math.log(10000.0) / HD)).astype(np.float32)
    ang = (pos * div).astype(np.float32)
    S = np.stack([np.sin(ang), np.cos(ang)], -1).reshape(CMAX + WIN, HD)
    S = S.astype(np.float32)

    def dev(posarr):
        Sp = S[posarr]
        Tt = np.empty((64, len(posarr)), np.float32)
        for j in range(64):
            Tt[j] = -Sp[:, 32 + j // 2] if j % 2 == 0 else Sp[:, (j - 1) // 2]
        return np.ascontiguousarray(np.tile(Tt, (2, 1)))
    t_sq = dev(np.arange(WIN))
    t_skl = dev((np.arange(T) // WIN) * (WIN // NTPC) + np.arange(T) % WIN)
    t_skc = dev(np.minimum(np.arange(NCL), CMAX - 1))
    return t_sq, t_skl, t_skc


def _masks():
    p = np.arange(128)
    maskl = np.zeros((128, 2, 512), np.float32)
    for qc in range(2):
        qpos = qc * 128 + p
        j = np.arange(512)
        bad = (j[None, :] < qpos[:, None]) | (j[None, :] > qpos[:, None] + WIN)
        maskl[:, qc][bad] = NEG
    msp_c = np.zeros((128, G, NCL), np.float32)
    for g in range(G):
        msp_c[:, g, :][:, np.arange(NCL) >= (WIN // NTPC) * g] = NEG
    msp_l = np.zeros((128, 2, WIN), np.float32)
    for qc in range(2):
        qpos = qc * 128 + p
        j = np.arange(WIN)
        msp_l[:, qc][j[None, :] > qpos[:, None]] = NEG
    return maskl, msp_c, msp_l


def _perm_mats():
    rbase = np.zeros((128, 128), np.float32)
    for pp in range(128):
        j = pp % 64
        m = pp + 32 if j < 32 else pp - 32
        rbase[m, pp] = -1.0 if j < 32 else 1.0
    p2 = np.zeros((128, 128), np.float32)
    for pp in range(128):
        p2[pp ^ 1, pp] = 1.0
    emat = np.zeros((128, 128), np.float32)
    for c in range(8):
        for pp in range(128):
            emat[pp, c * 16 + 2 * c + (1 if pp >= 64 else 0)] = 1.0
    bh = np.zeros((16, 8, 128), np.float32)
    for c in range(8):
        for pp in range(128):
            bh[2 * c + (1 if pp >= 64 else 0), c, pp] = 1.0
    idm = np.eye(128, dtype=np.float32)
    return rbase, p2, emat, bh, idm


def prep_inputs(inp):
    d = {}
    x = np.asarray(inp["L0_attn_x"], np.float32)
    d["x_all"] = np.ascontiguousarray(x.transpose(0, 2, 1))
    d["wqkv"] = np.ascontiguousarray(np.asarray(inp["L0_ffn_Wqkv"], np.float32))
    d["wo"] = np.ascontiguousarray(np.asarray(inp["L0_ffn_Wo"], np.float32))
    w1 = np.asarray(inp["L0_ffn_W1"], np.float32)
    w1p = np.zeros((6, D, 2 * DIP), np.float32)
    w1p[:, :, :DI] = w1[:, :, :DI]
    w1p[:, :, DIP:DIP + DI] = w1[:, :, DI:]
    d["w1"] = w1p
    w2 = np.asarray(inp["L0_ffn_W2"], np.float32)
    w2p = np.zeros((6, DIP, D), np.float32)
    w2p[:, :DI] = w2
    d["w2"] = w2p
    d["wattn"] = np.ascontiguousarray(np.asarray(inp["L0_attn_w_attn"], np.float32).T)
    d["wproj"] = np.ascontiguousarray(np.asarray(inp["L0_attn_w_proj"], np.float32).T)
    wass = np.asarray(inp["L0_attn_w_assign"], np.float32)
    d["aexp"] = np.ascontiguousarray(np.repeat(wass, NTPC, axis=0).T)
    qs = np.asarray(inp["L0_ffn_q_scale"], np.float32)
    ks = np.asarray(inp["L0_ffn_k_scale"], np.float32)
    d["qs"] = np.ascontiguousarray(np.tile(qs, (1, 2)).T)
    d["ks"] = np.ascontiguousarray(np.tile(ks, (1, 2)).T)
    b1 = np.asarray(inp["L0_ffn_b1"], np.float32)
    b1a = np.zeros((6, DIP), np.float32)
    b1a[:, :DI] = b1[:, :DI]
    b1g = np.zeros((6, DIP), np.float32)
    b1g[:, :DI] = b1[:, DI:]
    b1l = np.zeros((128, 6, 2 * NI), np.float32)
    for li in range(6):
        b1l[:, li, :NI] = b1a[li].reshape(NI, 128).T
        b1l[:, li, NI:] = b1g[li].reshape(NI, 128).T
    d["b1"] = np.ascontiguousarray(b1l)
    rs = lambda a: np.ascontiguousarray(
        np.asarray(a, np.float32).reshape(6, 8, 128).transpose(2, 0, 1)
        .reshape(128, 48))
    d["b2"] = rs(inp["L0_ffn_b2"])
    d["lng"] = rs(inp["L0_ffn_ln_g"])
    d["lnb"] = rs(inp["L0_ffn_ln_b"])
    d["battn"] = np.ascontiguousarray(
        np.asarray(inp["L0_attn_b_attn"], np.float32).reshape(24, 128).T)
    d["bproj"] = np.ascontiguousarray(
        np.asarray(inp["L0_attn_b_proj"], np.float32).reshape(8, 128).T)
    d["tabs"] = _ffn_tables()
    d["t_sq"], d["t_skl"], d["t_skc"] = _sparse_tables()
    d["maskl"], d["msp_c"], d["msp_l"] = _masks()
    d["rbase"], d["p2m"], d["emat"], d["bh"], d["idm"] = _perm_mats()
    d["consts"] = np.ascontiguousarray(
        np.tile(np.array(CONST_VALS + [0.0] * (16 - len(CONST_VALS)),
                         np.float32)[None, :], (128, 1)))
    d["bconst"] = np.ascontiguousarray(
        np.tile(np.array([1.0, 1.0 / 1024.0, 0.0, 0.0], np.float32)[None, :],
                (128, 1)))
    d["ones_r"] = np.ones((1, 128), np.float32)
    d["neg_r"] = np.full((1, 128), -1.0, np.float32)
    return d


class B:
    pass


def build(elems=2, calls=2, depth=6, sparse=True, sub="both", dbg=False):
    from contextlib import ExitStack
    nc = bass.Bass("TRN2", target_bir_lowering=False)
    b = B()
    b.nc = nc
    b.elems, b.calls, b.depth, b.sparse = elems, calls, depth, sparse
    b.sub = sub
    b.dbg = dbg
    b.dbg_tensors = {}
    di = lambda n, sh: nc.dram_tensor(n, sh, F32, kind="ExternalInput")
    b.x_in = di("x_in", [elems, D, T])
    b.h_wqkv = di("wqkv", [6, D, 3 * D])
    b.h_wo = di("wo", [6, D, D])
    b.h_w1 = di("w1", [6, D, 2 * DIP])
    b.h_w2 = di("w2", [6, DIP, D])
    b.h_wattn = di("wattn", [D, 3 * D])
    b.h_wproj = di("wproj", [D, D])
    b.h_aexp = di("aexp", [D, T])
    b.h_tabs = di("tabs", [4, 128, T])
    b.h_tsq = di("t_sq", [128, WIN])
    b.h_tskl = di("t_skl", [128, T])
    b.h_tskc = di("t_skc", [128, NCL])
    b.h_maskl = di("maskl", [128, 2, 512])
    b.h_mspc = di("msp_c", [128, G, NCL])
    b.h_mspl = di("msp_l", [128, 2, WIN])
    b.h_rbase = di("rbase", [128, 128])
    b.h_p2 = di("p2m", [128, 128])
    b.h_emat = di("emat", [128, 128])
    b.h_bh = di("bh", [16, 8, 128])
    b.h_idm = di("idm", [128, 128])
    b.h_consts = di("consts", [128, 16])
    b.h_bconst = di("bconst", [128, 4])
    b.h_onesr = di("ones_r", [1, 128])
    b.h_negr = di("neg_r", [1, 128])
    b.h_qs = di("qs", [128, 6])
    b.h_ks = di("ks", [128, 6])
    b.h_b1 = di("b1", [128, 6, 2 * NI])
    b.h_b2 = di("b2", [128, 48])
    b.h_lng = di("lng", [128, 48])
    b.h_lnb = di("lnb", [128, 48])
    b.h_battn = di("battn", [128, 24])
    b.h_bproj = di("bproj", [128, 8])
    b.xpark = nc.dram_tensor("xpark", [D, T], F32, kind="Internal")
    b.x_out = nc.dram_tensor("x_out", [elems, D, T], F32, kind="ExternalOutput")

    stack = ExitStack()
    with stack:
        E = stack.enter_context
        sb = lambda n, sh, dt: E(nc.sbuf_tensor(n, sh, dt))
        b.stream = sb("stream", [128, CH, T], F32)
        b.tblw = sb("tblw", [128, 2, 4, WIN], BF16)
        b.maskl = sb("masklS", [128, 2, 512], BF16)
        b.mspc = sb("mspcS", [128, G, NCL], BF16)
        b.mspl = sb("msplS", [128, 2, WIN], BF16)
        b.consts = sb("constsS", [128, 16], F32)
        b.bconsts = sb("bconstsS", [128, 4], BF16)
        b.onesr = sb("onesrS", [1, 128], F32)
        b.negr = sb("negrS", [1, 128], F32)
        b.idb = sb("idb", [128, 128], BF16)
        b.rbase = sb("rbaseS", [128, 128], BF16)
        b.rqm = sb("rqm", [128, 128], BF16)
        b.rkm = sb("rkm", [128, 128], BF16)
        b.p2m = sb("p2mS", [128, 128], BF16)
        b.emat = sb("ematS", [128, 128], BF16)
        b.qs = sb("qsS", [128, 6], F32)
        b.ks = sb("ksS", [128, 6], F32)
        b.b1 = sb("b1S", [128, 6, 2 * NI], F32)
        b.b2 = sb("b2S", [128, 48], F32)
        b.lng = sb("lngS", [128, 48], F32)
        b.lnb = sb("lnbS", [128, 48], F32)
        b.battn = sb("battnS", [128, 24], F32)
        b.bproj = sb("bprojS", [128, 8], F32)
        b.bh = sb("bhS", [16, 8 * 128], BF16)
        b.rows = sb("rows", [1, 1024], F32)
        b.rarena = sb("rarena", [16, 512], F32)
        b.rkbf = sb("rkbf", [16, WIN], BF16)
        b.rqbf = sb("rqbf", [16, WIN], BF16)
        b.bsave = sb("bsave", [128, 4, 1], F32)
        b.sqs = sb("sqs", [128, 4, 512], BF16)
        b.scrf = sb("scrf", [128, 1152], F32)
        b.bigbuf = sb("bigbuf", [128, 24576], BF16)
        b.wo = sb("woS", [128, CH, D], BF16)
        b.wsm = sb("wsm", [128, 5632], BF16)
        b.scratch = sb("scratchS", [128, 18944], BF16)
        b.scratch_f = b.scratch.bitcast(F32)
        b.scrf_b = b.scrf.bitcast(BF16)
        b.sqs_f = b.sqs.bitcast(F32)
        b.ps = [E(nc.psum_tensor(f"ps_{i}", [128, 512], F32)) for i in range(6)]
        b.pstr = E(nc.psum_tensor("ps_tr", [128, 1024], BF16))
        b.pss = E(nc.psum_tensor("ps_s", [16, 512], F32))

        for i, v in enumerate(CONST_VALS):
            nc.const_aps.register(F32, float(v), b.consts[:, i:i + 1])

        block = E(nc.Block())
        c = Ctx(nc, stack, block)
        b.ctx = c

        emit_init(b)
        for engx in ("te", "dv", "ac", "po"):
            for sname in ("init", "initg"):
                s = c.dsem(sname)
                c._wait(engx, s, s.n)
        c.flush()
        for e in range(elems):
            emit_load_x(b, e)
            for call in range(calls):
                if call > 0 and sparse:
                    emit_sparse(b)
                emit_park_and_outer_ln(b)
                for li in range(depth):
                    emit_ffn_layer(b, li, call)
                emit_merge(b)
                c.flush()
            emit_store_x(b, e)
            c.flush()
        for s in c.dsems.values():
            c._wait("sy", s, s.n)
        c.flush()
    return nc, b


def dump(b, name, ap, reads):
    if not b.dbg or name in b.dbg_tensors:
        return
    sh = list(ap.shape)
    t = b.nc.dram_tensor("dbg_" + name, sh, ap.dtype, kind="ExternalOutput")
    b.dbg_tensors[name] = t
    b.ctx.op("sy", "dma_start", out=t.ap(), in_=ap, reads=reads,
             writes=("dbg",), dsem="dbg")


def sc_bf(b, off, shape):
    n = int(np.prod(shape[1:]))
    ap = b.scratch[0:shape[0], off:off + n]
    if len(shape) == 3:
        ap = ap.rearrange("p (a c) -> p a c", a=shape[1])
    return ap


def emit_init(b):
    c = b.ctx
    for sbuf, src, key in [
            (b.consts, b.h_consts.ap(), "consts"),
            (b.qs, b.h_qs.ap(), "qs"),
            (b.ks, b.h_ks.ap(), "ks"),
            (b.b1, b.h_b1.ap(), "b1"),
            (b.b2, b.h_b2.ap(), "b2"),
            (b.lng, b.h_lng.ap(), "lng"),
            (b.lnb, b.h_lnb.ap(), "lnb"),
            (b.battn, b.h_battn.ap(), "battn"),
            (b.onesr, b.h_onesr.ap(), "onesr"),
            (b.negr, b.h_negr.ap(), "negr"),
            (b.bproj, b.h_bproj.ap(), "bproj")]:
        c.op("sy", "dma_start", out=sbuf[:], in_=src, writes=(key,), dsem="init")
    for dst, src, key in [
            (b.idb[:], b.h_idm.ap(), "idb"),
            (b.rbase[:], b.h_rbase.ap(), "rbase"),
            (b.p2m[:], b.h_p2.ap(), "p2m"),
            (b.emat[:], b.h_emat.ap(), "emat"),
            (b.maskl[:], b.h_maskl.ap(), "maskl"),
            (b.mspc[:], b.h_mspc.ap(), "mspc"),
            (b.mspl[:], b.h_mspl.ap(), "mspl"),
            (b.bh[:], b.h_bh.rearrange("h c p -> h (c p)"), "bh"),
            (b.bconsts[:], b.h_bconst.ap(), "bconsts"),
            (b.wo[:], b.h_wo[0].rearrange("(c p) m -> p c m", p=128), "wo")]:
        c.op("gq", "dma_start", out=dst, in_=src, writes=(key,), dsem="initg")


def emit_load_x(b, e):
    c = b.ctx
    c.op("sy", "dma_start", out=b.stream[:],
         in_=b.x_in[e].rearrange("(c p) t -> p c t", p=128),
         writes=("stream",), dsem="xin")
    c.flush()


def emit_store_x(b, e):
    c = b.ctx
    c.op("sy", "dma_start", out=b.x_out[e].rearrange("(c p) t -> p c t", p=128),
         in_=b.stream[:], reads=("stream",), writes=("xout",), dsem="xout")


def _ln_rows(b, n):
    c = b.ctx
    MU, RSTD = 0, 512
    c.op("dv", "tensor_copy", b.rows[:, MU:MU + n], b.pss[0:1, 0:n],
         reads=("pss",), writes=("rows",))
    c.op("dv", "tensor_tensor", b.rows[:, RSTD:RSTD + n], b.rows[:, MU:MU + n],
         b.rows[:, MU:MU + n], op=ALU.mult, reads=("rows",), writes=("rows",))
    c.op("dv", "scalar_tensor_tensor", b.rows[:, RSTD:RSTD + n],
         b.ps[4][0:1, 0:n], 1e-5, b.rows[:, RSTD:RSTD + n],
         op0=ALU.add, op1=ALU.subtract, reads=("ps4", "rows"), writes=("rows",))
    c.op("ac", "activation", out=b.rows[:, RSTD:RSTD + n],
         in_=b.rows[:, RSTD:RSTD + n], func=AF.Ln, reads=("rows",),
         writes=("rows",))
    c.op("ac", "activation", out=b.rows[:, RSTD:RSTD + n],
         in_=b.rows[:, RSTD:RSTD + n], func=AF.Exp, scale=-0.5,
         reads=("rows",), writes=("rows",))
    c.op("dv", "tensor_tensor", b.rows[:, MU:MU + n], b.rows[:, MU:MU + n],
         b.rows[:, RSTD:RSTD + n], op=ALU.mult, reads=("rows",), writes=("rows",))


def _ln_stats(b, src_fn, n):
    c = b.ctx
    for ch in range(CH):
        slot = ch % 2
        c.op("dv", "tensor_copy", b.sqs[:, slot, 0:n], src_fn(ch),
             reads=("stream",), writes=(f"sqs{slot}",))
        c.op("ac", "activation", out=b.sqs[:, 2 + slot, 0:n],
             in_=b.sqs[:, slot, 0:n], func=AF.Square,
             reads=(f"sqs{slot}",), writes=(f"sqs{2 + slot}",))
        c.op("te", "matmul", b.pss[0:1, 0:n], b.bconsts[:, 1:2],
             b.sqs[:, slot, 0:n], start=(ch == 0), stop=(ch == CH - 1),
             reads=(f"sqs{slot}", "bconsts"),
             writes=("pss",) if ch in (0, CH - 1) else ())
        c.op("te", "matmul", b.ps[4][0:1, 0:n], b.bconsts[:, 1:2],
             b.sqs[:, 2 + slot, 0:n], start=(ch == 0), stop=(ch == CH - 1),
             reads=(f"sqs{2 + slot}", "bconsts"),
             writes=("ps4",) if ch in (0, CH - 1) else ())
    _ln_rows(b, n)


def _ln_bcasts(b, n):
    c = b.ctx
    MU, RSTD = 0, 512
    c.op("te", "matmul", b.ps[2][:, 0:n], b.onesr[:],
         b.rows[0:1, RSTD:RSTD + n], start=True, stop=True,
         reads=("rows", "onesr"), writes=("ps2",))
    c.op("te", "matmul", b.ps[3][:, 0:n], b.negr[:],
         b.rows[0:1, MU:MU + n], start=True, stop=True,
         reads=("rows", "negr"), writes=("ps3",))


def emit_park_and_outer_ln(b):
    c = b.ctx
    c.op("sy", "dma_start", out=b.xpark.rearrange("(c p) t -> p c t", p=128),
         in_=b.stream[:], reads=("stream",), writes=("xpark",), dsem="xpark")
    for nt in range(4):
        ts = slice(nt * 512, (nt + 1) * 512)
        if nt == 0:
            c.handover(["scrf0", "scrf1"],
                       ["ck", "cv", "sc0", "sc1", "sc2", "sc3"])
            c.handover(["scrf2", "scrf3"],
                       [f"nm{s}" for s in range(4)]
                       + [f"zs{s}" for s in range(4)])
        _ln_stats(b, lambda ch: b.stream[:, ch, ts], 512)
        _ln_bcasts(b, 512)
        for ch in range(CH):
            c.op("dv", "tensor_tensor", b.scrf[:, 0:512], b.stream[:, ch, ts],
                 b.ps[2][:], op=ALU.mult, reads=("stream", "ps2"),
                 writes=("scrf0",))
            c.op("dv", "tensor_tensor", b.stream[:, ch, ts], b.scrf[:, 0:512],
                 b.ps[3][:], op=ALU.add, reads=("scrf0", "ps3"),
                 writes=("stream",))
    b.n_ln = getattr(b, "n_ln", 0) + 1
    dump(b, f"postln{b.n_ln}", b.stream[:], ("stream",))
    c.flush()


def emit_merge(b):
    c = b.ctx
    b.n_merge = getattr(b, "n_merge", 0) + 1
    xp = b.scratch_f
    c.handover(["xmerge"], ["xnh"])
    for nt in range(4):
        ts = slice(nt * 512, (nt + 1) * 512)
        c.op("sy", "dma_start",
             out=xp[:, 0:4096].rearrange("p (c t) -> p c t", c=8),
             in_=b.xpark.rearrange("(c p) t -> p c t", p=128)[:, :, ts],
             reads=("xpark",), writes=("xmerge",), dsem="xmerge")
        for ch in range(CH):
            c.op("dv", "scalar_tensor_tensor", b.stream[:, ch, ts],
                 b.stream[:, ch, ts], 0.5, xp[:, ch * 512:(ch + 1) * 512],
                 op0=ALU.mult, op1=ALU.add,
                 reads=("stream", "xmerge"), writes=("stream",))
        c.flush()
    dump(b, f"postmerge{b.n_merge}", b.stream[:], ("stream",))


def emit_ffn_layer(b, li, call):
    c = b.ctx
    depth, calls, sparse = b.depth, b.calls, b.sparse
    c.op("ac", "activation", out=b.rqm[:], in_=b.rbase[:], func=AF.Identity,
         scale=b.qs[:, li:li + 1], reads=("rbase", "qs"), writes=("rqm",))
    c.op("ac", "activation", out=b.rkm[:], in_=b.rbase[:], func=AF.Identity,
         scale=b.ks[:, li:li + 1], reads=("rbase", "ks"), writes=("rkm",))
    c.handover([f"wqkv{sl}" for sl in range(6)], ["Hbuf"])
    for sl in range(6):
        c.op("gq", "dma_start",
             out=b.bigbuf[:, sl * 4096:(sl + 1) * 4096].rearrange(
                 "p (k m) -> p k m", k=8),
             in_=b.h_wqkv[li].rearrange("(k p) m -> p k m", p=128)[
                 :, :, sl * 512:(sl + 1) * 512],
             writes=(f"wqkv{sl}",), dsem="wqkv")
    c.finalize_stream([f"wqkv{sl}" for sl in range(6)], "wqkv")
    c.handover(list(ATTN_KEYS), ["xnh", "xmerge"] + list(SP_KEYS))
    c.flush()

    def wqkv_ap(k, mcol0):
        sl = mcol0 // 512
        off = sl * 4096 + k * 512 + (mcol0 - sl * 512)
        return b.bigbuf[:, off:off + 128]

    xnbufs = [sc_bf(b, XN0, (128, 8, WIN)), sc_bf(b, XN1, (128, 8, WIN))]
    qrot = sc_bf(b, QROT, (128, 8, WIN))
    krot = sc_bf(b, KROT, (128, 8, 512))
    vtm = sc_bf(b, VTM, (128, 4, 1024))
    vstg = sc_bf(b, VSTG, (128, WIN))
    pht = sc_bf(b, PHT, (128, 2, 3 * 128)).rearrange(
        "p q (j k) -> p q j k", j=3)
    ow = sc_bf(b, OW, (128, 8, WIN))
    pbuf = sc_bf(b, PBUF, (128, 4, 384))

    def cast_xn(w):
        buf, key = xnbufs[w % 2], f"xn{w % 2}"
        t0 = w * WIN
        for ch in range(CH):
            if ch < 4:
                c.op("dv", "tensor_copy", buf[:, ch, :],
                     b.stream[:, ch, t0:t0 + WIN], reads=("stream",),
                     writes=(key,))
            elif t0 == 0:
                c.op("dv", "tensor_copy", buf[:, ch, 1:WIN],
                     b.stream[:, ch, 0:WIN - 1], reads=("stream",),
                     writes=(key,))
                c.op("po", "memset", buf[:, ch, 0:1], 0.0, writes=(key,))
            else:
                c.op("dv", "tensor_copy", buf[:, ch, :],
                     b.stream[:, ch, t0 - 1:t0 + WIN - 1], reads=("stream",),
                     writes=(key,))

    cast_xn(0)
    for w in (range(NW) if b.sub in ("both", "attn") else ()):
        wkey, xn = f"xn{w % 2}", xnbufs[w % 2]
        if w + 1 < NW:
            cast_xn(w + 1)
        for t in range(4):
            c.op("gq", "dma_start", out=b.tblw[:, w % 2, t, :],
                 in_=b.h_tabs[t][:, w * WIN:(w + 1) * WIN],
                 writes=(f"tblw{w % 2}",), dsem=f"tblw{w % 2}")
        if w > 0:
            c.op("po", "tensor_copy", krot[:, :, 0:WIN], krot[:, :, WIN:512],
                 reads=("krot",), writes=("krot",))
            c.op("po", "tensor_copy", vtm[:, 0:2, :], vtm[:, 2:4, :],
                 reads=("vtm",), writes=("vtm",))
        for m in range(24):
            ps, pk = b.ps[m % 4], f"ps{m % 4}"
            for k in range(CH):
                c.op("te", "matmul", ps[:, 0:WIN], wqkv_ap(k, m * 128),
                     xn[:, k, :], start=(k == 0), stop=(k == CH - 1),
                     reads=(wkey, f"wqkv{(m * 128) // 512}"),
                     writes=(pk,) if k in (0, CH - 1) else ())
            if m < 8:
                c.op("ac", "copy", qrot[:, m, :], ps[:, 0:WIN], reads=(pk,),
                     writes=("qrot",))
            elif m < 16:
                c.op("ac", "copy", krot[:, m - 8, WIN:512], ps[:, 0:WIN],
                     reads=(pk,), writes=("krot",))
            else:
                ch = m - 16
                c.op("ac", "copy", vstg[:], ps[:, 0:WIN], reads=(pk,),
                     writes=("vstg",))
                for j in range(2):
                    c.op("te", "transpose", b.pstr[:, j * 128:(j + 1) * 128],
                         vstg[:, j * 128:(j + 1) * 128], b.idb[:],
                         reads=("vstg", "idb"),
                         writes=("pstr",) if j in (0, 1) else ())
                c.op("dv", "tensor_copy",
                     vtm[:, 2:4, ch * 128:(ch + 1) * 128],
                     b.pstr[:, 0:256].rearrange("p (j k) -> p j k", j=2),
                     reads=("pstr",), writes=("vtm",))
        for which, roff in (("q", 0), ("k", WIN)):
            for ch in range(CH):
                slot = ch % 2
                src = qrot[:, ch, :] if which == "q" else krot[:, ch, WIN:512]
                skey = "qrot" if which == "q" else "krot"
                c.op("ac", "activation", out=b.sqs[:, slot, 0:WIN], in_=src,
                     func=AF.Square, reads=(skey,), writes=(f"sqs{slot}",))
                c.op("te", "matmul", b.pss[:, roff:roff + WIN],
                     b.emat[:, ch * 16:(ch + 1) * 16], b.sqs[:, slot, 0:WIN],
                     start=(ch == 0), stop=(ch == CH - 1),
                     reads=(f"sqs{slot}", "emat"),
                     writes=("pss",) if ch in (0, CH - 1) else ())
            ra = b.rarena[:, roff:roff + WIN]
            c.op("dv", "tensor_scalar", out=ra, in0=b.pss[:, roff:roff + WIN],
                 scalar1=1e-24, scalar2=None, op0=ALU.max, reads=("pss",),
                 writes=("rarena",))
            c.op("ac", "activation", out=ra, in_=ra, func=AF.Ln,
                 reads=("rarena",), writes=("rarena",))
            c.op("ac", "activation", out=ra, in_=ra, func=AF.Exp, scale=-0.5,
                 reads=("rarena",), writes=("rarena",))
        c.op("ac", "copy", b.rqbf[:], b.rarena[:, 0:WIN],
             reads=("rarena",), writes=("rqbf",))
        c.op("ac", "copy", b.rkbf[:], b.rarena[:, WIN:2 * WIN],
             reads=("rarena",), writes=("rkbf",))
        tkey = f"tblw{w % 2}"
        tw = lambda t: b.tblw[:, w % 2, t, :]
        c.handover(["scrf0", "scrf1"], ["sc0", "sc1", "sc2", "sc3"])
        
        for ch in range(CH):
            ps, pk = b.ps[2 + ch % 2], f"ps{2 + ch % 2}"
            psr, pkr = b.ps[4 + ch % 2], f"ps{4 + ch % 2}"
            c.op("te", "matmul", ps[:, 0:WIN], b.rqm[:], qrot[:, ch, :],
                 start=True, stop=True, reads=("rqm", "qrot"), writes=(pk,))
            c.op("te", "matmul", psr[:, 0:WIN],
                 b.bh[:, ch * 128:(ch + 1) * 128], b.rqbf[:],
                 start=True, stop=True, reads=("bh", "rqbf"), writes=(pkr,))
            c.op("dv", "tensor_tensor", b.scrf[:, 0:WIN], ps[:, 0:WIN], tw(1),
                 op=ALU.mult, reads=(pk, tkey), writes=("scrf0",))
            c.op("dv", "tensor_tensor", b.scrf[:, 512:512 + WIN],
                 qrot[:, ch, :], tw(0), op=ALU.mult, reads=("qrot", tkey),
                 writes=("scrf1",))
            c.op("dv", "scalar_tensor_tensor", b.scrf[:, 0:WIN],
                 b.scrf[:, 512:512 + WIN], b.qs[:, li:li + 1],
                 b.scrf[:, 0:WIN], op0=ALU.mult, op1=ALU.add,
                 reads=("scrf0", "scrf1", "qs"), writes=("scrf0",))
            c.op("dv", "tensor_tensor", qrot[:, ch, :], b.scrf[:, 0:WIN],
                 psr[:, 0:WIN], op=ALU.mult, reads=("scrf0", pkr),
                 writes=("qrot",))
            c.op("te", "matmul", ps[:, 0:WIN], b.rkm[:], krot[:, ch, WIN:512],
                 start=True, stop=True, reads=("rkm", "krot"), writes=(pk,))
            c.op("te", "matmul", psr[:, 0:WIN],
                 b.bh[:, ch * 128:(ch + 1) * 128], b.rkbf[:],
                 start=True, stop=True, reads=("bh", "rkbf"), writes=(pkr,))
            c.op("dv", "tensor_tensor", b.scrf[:, 0:WIN], ps[:, 0:WIN], tw(3),
                 op=ALU.mult, reads=(pk, tkey), writes=("scrf0",))
            c.op("dv", "tensor_tensor", b.scrf[:, 512:512 + WIN],
                 krot[:, ch, WIN:512], tw(2), op=ALU.mult,
                 reads=("krot", tkey), writes=("scrf1",))
            c.op("dv", "scalar_tensor_tensor", b.scrf[:, 0:WIN],
                 b.scrf[:, 512:512 + WIN], b.ks[:, li:li + 1],
                 b.scrf[:, 0:WIN], op0=ALU.mult, op1=ALU.add,
                 reads=("scrf0", "scrf1", "ks"), writes=("scrf0",))
            c.op("dv", "tensor_tensor", krot[:, ch, WIN:512],
                 b.scrf[:, 0:WIN], psr[:, 0:WIN], op=ALU.mult,
                 reads=("scrf0", pkr), writes=("krot",))
        c.flush()
        c.handover(["sc0", "sc1", "sc2", "sc3"], ["scrf0", "scrf1"])
        c.handover([f"nm{s}" for s in range(4)] + [f"zs{s}" for s in range(4)],
                   ["scrf2", "scrf3"])
        
        scb = b.scrf_b
        hqs = [(h, qc) for h in range(H) for qc in range(2)]

        def krange(qc):
            if w == 0:
                return (WIN, 128 if qc == 0 else WIN)
            return (qc * 128, 384)

        def attn_s1(i):
            h, qc = hqs[i]
            ch, hp = h // 2, (h % 2) * 64
            sl = i % 4
            psS, pkS = b.ps[4 + i % 2], f"ps{4 + i % 2}"
            k0, kn = krange(qc)
            c.op("te", "matmul", psS[:, 0:kn],
                 qrot[hp:hp + 64, ch, qc * 128:(qc + 1) * 128],
                 krot[hp:hp + 64, ch, k0:k0 + kn], start=True, stop=True,
                 reads=("qrot", "krot"), writes=(pkS,))
            c.op("dv", "scalar_tensor_tensor",
                 scb[:, sl * 384:sl * 384 + kn], psS[:, 0:kn], 8.0,
                 b.maskl[:, qc, k0:k0 + kn], op0=ALU.mult, op1=ALU.add,
                 reads=(pkS, "maskl"), writes=(f"sc{sl}",))
            c.op("dv", "reduce_max", b.scrf[:, 1024 + sl:1025 + sl],
                 scb[:, sl * 384:sl * 384 + kn], axis=AX.X, negate=True,
                 reads=(f"sc{sl}",), writes=(f"nm{sl}",))

        def attn_s2(i):
            h, qc = hqs[i]
            sl = i % 4
            k0, kn = krange(qc)
            c.op("ac", "activation", out=pbuf[:, sl, 0:kn],
                 in_=scb[:, sl * 384:sl * 384 + kn], func=AF.Exp,
                 bias=b.scrf[:, 1024 + sl:1025 + sl],
                 accum_out=b.scrf[:, 1028 + sl:1029 + sl],
                 reads=(f"sc{sl}", f"nm{sl}"),
                 writes=(f"pbuf{sl}", f"zs{sl}"))
            c.op("dv", "reciprocal", b.scrf[:, 1032 + sl:1033 + sl],
                 b.scrf[:, 1028 + sl:1029 + sl], reads=(f"zs{sl}",),
                 writes=(f"zs{sl}",))

        def attn_s3a(i):
            h, qc = hqs[i]
            sl = i % 4
            k0, kn = krange(qc)
            c.op("ac", "activation", out=pbuf[:, sl, 0:kn],
                 in_=pbuf[:, sl, 0:kn], func=AF.Identity,
                 scale=b.scrf[:, 1032 + sl:1033 + sl],
                 reads=(f"pbuf{sl}", f"zs{sl}"), writes=(f"pbuf{sl}",))
            nj = kn // 128
            po_ = (i % 2) * 512
            pstk = "pstr"
            for jj in range(nj):
                c.op("te", "transpose",
                     b.pstr[:, po_ + jj * 128:po_ + (jj + 1) * 128],
                     pbuf[:, sl, jj * 128:(jj + 1) * 128], b.idb[:],
                     reads=(f"pbuf{sl}", "idb"),
                     writes=(pstk,) if jj in (0, nj - 1) else ())
            c.op("dv", "tensor_copy", pht[:, i % 2, 0:nj, :],
                 b.pstr[:, po_:po_ + nj * 128].rearrange(
                     "p (jx k) -> p jx k", jx=nj),
                 reads=(pstk,), writes=(f"pht{i % 2}",))

        def attn_s4(i):
            h, qc = hqs[i]
            ch, hp = h // 2, (h % 2) * 64
            k0, kn = krange(qc)
            psO, pkO = b.ps[2 + h % 2], f"ps{2 + h % 2}"
            slots = ([2] if qc == 0 else [2, 3]) if w == 0 else \
                [qc, qc + 1, qc + 2]
            for idx, vs in enumerate(slots):
                c.op("te", "matmul",
                     psO[hp:hp + 64, qc * 128:(qc + 1) * 128],
                     vtm[:, vs, h * 64:(h + 1) * 64],
                     pht[:, i % 2, idx, :],
                     start=(idx == 0), stop=(idx == len(slots) - 1),
                     tile_position=(0, hp), reads=("vtm", f"pht{i % 2}"),
                     writes=(pkO,) if idx in (0, len(slots) - 1) else ())
            if qc == 1:
                c.op("ac", "copy", ow[hp:hp + 64, ch, :],
                     psO[hp:hp + 64, 0:WIN], reads=(pkO,), writes=("ow",))

        NI_ = len(hqs)
        for i in range(NI_ + 8):
            if i >= 8:
                attn_s4(i - 8)
            if 6 <= i < NI_ + 6:
                attn_s3a(i - 6)
            if 3 <= i < NI_ + 3:
                attn_s2(i - 3)
            if i < NI_:
                attn_s1(i)
        for m in range(CH):
            ps, pk = b.ps[m % 4], f"ps{m % 4}"
            for k in range(CH):
                c.op("te", "matmul", ps[:, 0:WIN],
                     b.wo[:, k, m * 128:(m + 1) * 128], ow[:, k, :],
                     start=(k == 0), stop=(k == CH - 1), reads=("ow", "wo"),
                     writes=(pk,) if k in (0, CH - 1) else ())
            c.op("dv", "tensor_tensor", b.stream[:, m, w * WIN:(w + 1) * WIN],
                 ps[:, 0:WIN], b.stream[:, m, w * WIN:(w + 1) * WIN],
                 op=ALU.add, reads=(pk, "stream"), writes=("stream",))
        c.flush()

    b.n_attn = getattr(b, "n_attn", 0) + 1
    dump(b, f"postattn{b.n_attn}", b.stream[:], ("stream",))
    for i in range(4):
        c.op("dv", "tensor_copy", b.bsave[:, i, :],
             b.stream[:, 4 + i, 1023:1024], reads=("stream",),
             writes=("bsave",))
    nxt = _next_wo(b, li, call)
    if nxt is not None:
        c.op("gq", "dma_start", out=b.wo[:],
             in_=nxt.rearrange("(c p) m -> p c m", p=128), writes=("wo",),
             dsem="wo")
    c.handover(["xnh"], ["xn0", "xn1", "qrot", "krot"])
    c.handover(["scrf0", "scrf1"], ["sc0", "sc1", "sc2", "sc3"])
    c.handover(["scrf2", "scrf3"],
               [f"nm{s}" for s in range(4)] + [f"zs{s}" for s in range(4)])
    xnh = sc_bf(b, XNH, (128, 8, 1024))
    Hbuf = b.bigbuf[:, 0:NI * 1024].rearrange("p (i t) -> p i t", i=NI)
    w1a = [b.wsm[:, o:o + 1024].rearrange("p (k m) -> p k m", k=8)
           for o in (0, 1024)]
    w1g = [b.wsm[:, o:o + 1024].rearrange("p (k m) -> p k m", k=8)
           for o in (2048, 3072)]
    w2s = [b.wsm[:, o:o + 2816].rearrange("p (i m) -> p i m", i=NI)
           for o in (0, 2816)]
    for hf in (range(2) if b.sub in ("both", "ffn") else ()):
        t0 = hf * 1024
        for nt in range(2):
            tt0 = t0 + nt * 512
            w0 = (hf == 0 and nt == 0)
            bs = (hf == 1 and nt == 0)
            for ch in range(CH):
                slot = ch % 2
                if ch >= 4 and (w0 or bs):
                    c.op("dv", "tensor_copy", b.sqs[:, slot, 1:512],
                         b.stream[:, ch, tt0:tt0 + 511], reads=("stream",),
                         writes=(f"sqs{slot}",))
                    if w0:
                        c.op("po", "memset", b.sqs[:, slot, 0:1], 0.0,
                             writes=(f"sqs{slot}",))
                    else:
                        c.op("dv", "tensor_copy", b.sqs[:, slot, 0:1],
                             b.bsave[:, ch - 4, :], reads=("bsave",),
                             writes=(f"sqs{slot}",))
                else:
                    src = b.stream[:, ch, tt0:tt0 + 512] if ch < 4 else \
                        b.stream[:, ch, tt0 - 1:tt0 + 511]
                    c.op("dv", "tensor_copy", b.sqs[:, slot, 0:512], src,
                         reads=("stream",), writes=(f"sqs{slot}",))
                c.op("ac", "activation", out=b.sqs[:, 2 + slot, 0:512],
                     in_=b.sqs[:, slot, 0:512], func=AF.Square,
                     reads=(f"sqs{slot}",), writes=(f"sqs{2 + slot}",))
                c.op("te", "matmul", b.pss[0:1, 0:512], b.bconsts[:, 1:2],
                     b.sqs[:, slot, 0:512], start=(ch == 0),
                     stop=(ch == CH - 1), reads=(f"sqs{slot}", "bconsts"),
                     writes=("pss",) if ch in (0, CH - 1) else ())
                c.op("te", "matmul", b.ps[4][0:1, 0:512], b.bconsts[:, 1:2],
                     b.sqs[:, 2 + slot, 0:512], start=(ch == 0),
                     stop=(ch == CH - 1), reads=(f"sqs{2 + slot}", "bconsts"),
                     writes=("ps4",) if ch in (0, CH - 1) else ())
            _ln_rows(b, 512)
            _ln_bcasts(b, 512)
            for ch in range(CH):
                dst = xnh[:, ch, nt * 512:(nt + 1) * 512]
                if ch >= 4 and (w0 or bs):
                    c.op("dv", "tensor_tensor", b.scrf[:, 1:512],
                         b.stream[:, ch, tt0:tt0 + 511], b.ps[2][:, 1:512],
                         op=ALU.mult, reads=("stream", "ps2"),
                         writes=("scrf0",))
                    if bs:
                        c.op("dv", "tensor_tensor", b.scrf[:, 0:1],
                             b.bsave[:, ch - 4, :], b.ps[2][:, 0:1],
                             op=ALU.mult, reads=("bsave", "ps2"),
                             writes=("scrf0",))
                    else:
                        c.op("po", "memset", b.scrf[:, 0:1], 0.0,
                             writes=("scrf0",))
                else:
                    src = b.stream[:, ch, tt0:tt0 + 512] if ch < 4 else \
                        b.stream[:, ch, tt0 - 1:tt0 + 511]
                    c.op("dv", "tensor_tensor", b.scrf[:, 0:512], src,
                         b.ps[2][:], op=ALU.mult, reads=("stream", "ps2"),
                         writes=("scrf0",))
                c.op("dv", "tensor_tensor", b.scrf[:, 0:512],
                     b.scrf[:, 0:512], b.ps[3][:], op=ALU.add,
                     reads=("scrf0", "ps3"), writes=("scrf0",))
                c.op("ac", "activation", out=dst, in_=b.scrf[:, 0:512],
                     func=AF.Identity,
                     scale=b.lng[:, li * 8 + ch:li * 8 + ch + 1],
                     bias=b.lnb[:, li * 8 + ch:li * 8 + ch + 1],
                     reads=("scrf0", "lng", "lnb"), writes=("xnh",))
        c.flush()
        if hf == 0:
            c.handover(["Hbuf"], [f"wqkv{sl}" for sl in range(6)])
        c.handover([f"w1{x}{s}" for x in "ag" for s in (0, 1)],
                   ["w2s0", "w2s1"])
        for i in range(NI):
            sl = i % 2
            c.op("gq", "dma_start", out=w1a[sl][:],
                 in_=b.h_w1[li].rearrange("(k p) m -> p k m", p=128)[
                     :, :, i * 128:(i + 1) * 128], writes=(f"w1a{sl}",),
                 dsem=f"w1a{sl}")
            c.op("gq", "dma_start", out=w1g[sl][:],
                 in_=b.h_w1[li].rearrange("(k p) m -> p k m", p=128)[
                     :, :, DIP + i * 128:DIP + (i + 1) * 128],
                 writes=(f"w1g{sl}",), dsem=f"w1g{sl}")
            for nt in range(2):
                psA, pkA = b.ps[nt], f"ps{nt}"
                psB, pkB = b.ps[4 + nt], f"ps{4 + nt}"
                for k in range(CH):
                    c.op("te", "matmul", psA[:], w1a[sl][:, k, :],
                         xnh[:, k, nt * 512:(nt + 1) * 512],
                         start=(k == 0), stop=(k == CH - 1),
                         reads=("xnh", f"w1a{sl}"),
                         writes=(pkA,) if k in (0, CH - 1) else ())
                for k in range(CH):
                    c.op("te", "matmul", psB[:], w1g[sl][:, k, :],
                         xnh[:, k, nt * 512:(nt + 1) * 512],
                         start=(k == 0), stop=(k == CH - 1),
                         reads=("xnh", f"w1g{sl}"),
                         writes=(pkB,) if k in (0, CH - 1) else ())
                c.op("ac", "activation", out=b.scrf[:, 0:512], in_=psB[:],
                     func=AF.Gelu, bias=b.b1[:, li, NI + i:NI + i + 1],
                     reads=(pkB, "b1"), writes=("scrf0",))
                c.op("dv", "scalar_tensor_tensor",
                     Hbuf[:, i, nt * 512:(nt + 1) * 512], psA[:],
                     b.b1[:, li, i:i + 1], b.scrf[:, 0:512],
                     op0=ALU.add, op1=ALU.mult,
                     reads=(pkA, "b1", "scrf0"), writes=("Hbuf",))
        c.flush()
        c.handover(["w2s0", "w2s1"],
                   [f"w1{x}{s}" for x in "ag" for s in (0, 1)])
        for m in range(CH):
            sl = m % 2
            c.op("gq", "dma_start", out=w2s[sl][:],
                 in_=b.h_w2[li].rearrange("(i p) m -> p i m", p=128)[
                     :, :, m * 128:(m + 1) * 128], writes=(f"w2s{sl}",),
                 dsem=f"w2s{sl}")
            pso = (b.ps[0], b.ps[1]) if m % 2 == 0 else (b.ps[2], b.ps[3])
            pko = ("ps0", "ps1") if m % 2 == 0 else ("ps2", "ps3")
            for i in range(NI):
                for nt in range(2):
                    c.op("te", "matmul", pso[nt][:], w2s[sl][:, i, :],
                         Hbuf[:, i, nt * 512:(nt + 1) * 512],
                         start=(i == 0), stop=(i == NI - 1),
                         reads=("Hbuf", f"w2s{sl}"),
                         writes=(pko[nt],) if i in (0, NI - 1) else ())
            for nt in range(2):
                ts = slice(t0 + nt * 512, t0 + (nt + 1) * 512)
                c.op("dv", "scalar_tensor_tensor", b.stream[:, m, ts],
                     pso[nt][:], b.b2[:, li * 8 + m:li * 8 + m + 1],
                     b.stream[:, m, ts], op0=ALU.add, op1=ALU.add,
                     reads=(pko[nt], "b2", "stream"), writes=("stream",))
        c.flush()


def _next_wo(b, li, call):
    if li + 1 < b.depth:
        return b.h_wo[li + 1]
    if b.sparse and call + 1 < b.calls:
        return b.h_wproj.ap()
    return b.h_wo[0]


def emit_sparse(b):
    c = b.ctx
    qkv = sc_bf(b, SQKV, (128, 24, WIN))
    xng = sc_bf(b, SXN, (128, 8, WIN))
    kcr = sc_bf(b, SKCR, (128, 8, 64))
    vct = b.scratch[0:64, SVCT:SVCT + 1024]
    vlt = sc_bf(b, SVLT, (128, 2, 1024))
    phs = sc_bf(b, SPHT, (128, 3 * 128)).rearrange("p (j k) -> p j k", j=3)
    yfm = sc_bf(b, SY_, (128, 8, WIN))
    aag = sc_bf(b, SAG, (128, 8, WIN))
    spb = sc_bf(b, SPB, (128, 320))
    sscr = b.scratch_f[:, SSCR_F:SSCR_F + 320]
    ssml = b.scratch_f[:, SSML_F:SSML_F + 8]
    sqf = b.sqs_f[:]
    ck = b.scrf[:, 0:512].rearrange("p (ch l) -> p ch l", ch=8)
    cv = b.scrf[:, 512:1024].rearrange("p (ch l) -> p ch l", ch=8)
    c.handover(list(SP_KEYS), list(ATTN_KEYS) + ["xnh", "xmerge"])
    c.handover(["ck", "cv"], ["scrf0", "scrf1", "scrf2", "scrf3",
                               "sc0", "sc1", "sc2", "sc3"]
               + [f"nm{s}" for s in range(4)] + [f"zs{s}" for s in range(4)])
    c.op("po", "memset", b.scrf[:, 0:512], 0.0, writes=("ck",))
    c.op("po", "memset", b.scrf[:, 512:1024], 0.0, writes=("cv",))
    c.handover([f"wqkv{sl}" for sl in range(6)], ["Hbuf"])
    for sl in range(6):
        c.op("gq", "dma_start",
             out=b.bigbuf[:, sl * 4096:(sl + 1) * 4096].rearrange(
                 "p (k m) -> p k m", k=8),
             in_=b.h_wattn.rearrange("(k p) m -> p k m", p=128)[
                 :, :, sl * 512:(sl + 1) * 512],
             writes=(f"wqkv{sl}",), dsem="wqkv")
    c.finalize_stream([f"wqkv{sl}" for sl in range(6)], "wqkv")
    c.op("gq", "dma_start", out=b.wo[:],
         in_=b.h_wproj.ap().rearrange("(c p) m -> p c m", p=128),
         writes=("wo",), dsem="wo")
    c.op("gq", "dma_start", out=b.tblw[:, 0, 0, :], in_=b.h_tsq.ap(),
         writes=("tblw0",), dsem="tblw0")
    c.op("gq", "dma_start", out=b.tblw[:, 0, 1, 0:64], in_=b.h_tskc.ap(),
         writes=("tblw0",), dsem="tblw0")
    c.flush()

    def wattn_ap(k, mcol0):
        sl = mcol0 // 512
        off = sl * 4096 + k * 512 + (mcol0 - sl * 512)
        return b.bigbuf[:, off:off + 128]

    for g in range(G):
        ts = slice(g * WIN, (g + 1) * WIN)
        c.op("gq", "dma_start", out=aag[:],
             in_=b.h_aexp.rearrange("(ch p) t -> p ch t", p=128)[:, :, ts],
             writes=("aag",), dsem="aag")
        c.op("gq", "dma_start", out=b.tblw[:, 1, g % 2, :],
             in_=b.h_tskl[:, ts], writes=("tblw1",), dsem="tblw1")
        _ln_stats(b, lambda ch: b.stream[:, ch, ts], WIN)
        _ln_bcasts(b, WIN)
        for ch in range(CH):
            c.op("dv", "tensor_tensor", sqf[:, 3, :], b.stream[:, ch, ts],
                 b.ps[2][:, 0:WIN], op=ALU.mult, reads=("stream", "ps2"),
                 writes=("sqs3",))
            c.op("dv", "tensor_tensor", xng[:, ch, :], sqf[:, 3, :],
                 b.ps[3][:, 0:WIN], op=ALU.add, reads=("sqs3", "ps3"),
                 writes=("xng",))
        for m in range(24):
            ps, pk = b.ps[m % 4], f"ps{m % 4}"
            for k in range(CH):
                c.op("te", "matmul", ps[:, 0:WIN], wattn_ap(k, m * 128),
                     xng[:, k, :], start=(k == 0), stop=(k == CH - 1),
                     reads=("xng", f"wqkv{(m * 128) // 512}"),
                     writes=(pk,) if k in (0, CH - 1) else ())
            c.op("ac", "activation", out=qkv[:, m, :], in_=ps[:, 0:WIN],
                 func=AF.Identity, bias=b.battn[:, m:m + 1],
                 reads=(pk, "battn"), writes=("sqkv",))
        if g == 1:
            dump(b, "xng", xng, ("xng",))
            dump(b, "qkv_raw", qkv, ("sqkv",))
        for ch in range(CH):
            for j in range(2):
                c.op("te", "transpose", b.pstr[:, j * 128:(j + 1) * 128],
                     qkv[:, 16 + ch, j * 128:(j + 1) * 128], b.idb[:],
                     reads=("sqkv", "idb"),
                     writes=("pstr",) if j in (0, 1) else ())
            c.op("dv", "tensor_copy", vlt[:, :, ch * 128:(ch + 1) * 128],
                 b.pstr[:, 0:256].rearrange("p (j k) -> p j k", j=2),
                 reads=("pstr",), writes=("vlt",))
        for ch in range(CH):
            c.op("dv", "tensor_tensor", sqf[:, 3, :], xng[:, ch, :],
                 aag[:, ch, :], op=ALU.mult, reads=("xng", "aag"),
                 writes=("sqs3",))
            c.op("te", "matmul", b.pss[0:1, 0:WIN],
                 b.consts[:, C_ONE:C_ONE + 1], sqf[:, 3, :],
                 start=(ch == 0), stop=(ch == CH - 1),
                 reads=("sqs3", "consts"),
                 writes=("pss",) if ch in (0, CH - 1) else ())
        LG, WR, SM = 0, 256, 768
        c.op("dv", "tensor_copy", b.rows[:, LG:LG + WIN], b.pss[0:1, 0:WIN],
             reads=("pss",), writes=("rows",))
        lg3 = b.rows[:, LG:LG + WIN].rearrange("o (s t) -> o s t", s=8)
        wr3 = b.rows[:, WR:WR + WIN].rearrange("o (s t) -> o s t", s=8)
        if g == 1:
            dump(b, "row_logits", b.rows[:, LG:LG + WIN], ("rows",))
        c.op("dv", "reduce_max", b.rows[:, SM:SM + 8], lg3, axis=AX.X,
             negate=True, reads=("rows",), writes=("rows",))
        if g == 1:
            dump(b, "row_negmax", b.rows[:, SM:SM + 8], ("rows",))
        c.op("dv", "tensor_tensor", lg3, lg3,
             b.rows[:, SM:SM + 8][:, :, None].broadcast_to((1, 8, 32)),
             op=ALU.add, reads=("rows",), writes=("rows",))
        if g == 1:
            dump(b, "row_sub", b.rows[:, LG:LG + WIN], ("rows",))
        c.op("ac", "activation", out=b.rows[:, LG:LG + WIN],
             in_=b.rows[:, LG:LG + WIN], func=AF.Exp, reads=("rows",),
             writes=("rows",))
        if g == 1:
            dump(b, "row_exp", b.rows[:, LG:LG + WIN], ("rows",))
        c.op("dv", "reduce_sum", b.rows[:, SM:SM + 8], lg3, axis=AX.X,
             reads=("rows",), writes=("rows",))
        c.op("dv", "reciprocal", b.rows[:, SM:SM + 8], b.rows[:, SM:SM + 8],
             reads=("rows",), writes=("rows",))
        if g == 1:
            dump(b, "row_rz", b.rows[:, SM:SM + 8], ("rows",))
        c.op("dv", "tensor_tensor", wr3, lg3,
             b.rows[:, SM:SM + 8][:, :, None].broadcast_to((1, 8, 32)),
             op=ALU.mult, reads=("rows",), writes=("rows",))
        if g == 1:
            dump(b, "wrow", b.rows[:, WR:WR + WIN], ("rows",))
        c.op("te", "matmul", b.ps[4][:, 0:WIN], b.onesr[:],
             b.rows[0:1, WR:WR + WIN], start=True, stop=True,
             reads=("rows", "onesr"), writes=("ps4",))
        for ch in range(CH):
            c.op("dv", "tensor_tensor", sqf[:, 3, :], qkv[:, 8 + ch, :],
                 b.ps[4][:, 0:WIN], op=ALU.mult, reads=("sqkv", "ps4"),
                 writes=("sqs3",))
            c.op("dv", "tensor_reduce", ck[:, ch, g * 8:(g + 1) * 8],
                 sqf[:, 3, :].rearrange("p (s t) -> p s t", s=8),
                 axis=AX.X, op=ALU.add, reads=("sqs3",), writes=("ck",))
            c.op("dv", "tensor_tensor", sqf[:, 3, :], qkv[:, 16 + ch, :],
                 b.ps[4][:, 0:WIN], op=ALU.mult, reads=("sqkv", "ps4"),
                 writes=("sqs3",))
            c.op("dv", "tensor_reduce", cv[:, ch, g * 8:(g + 1) * 8],
                 sqf[:, 3, :].rearrange("p (s t) -> p s t", s=8),
                 axis=AX.X, op=ALU.add, reads=("sqs3",), writes=("cv",))
        if g == 1:
            dump(b, "ck", b.scrf[:, 0:512], ("ck",))
            dump(b, "cv", b.scrf[:, 512:1024], ("cv",))
        for ch in range(CH):
            ps, pk = b.ps[2 + ch % 2], f"ps{2 + ch % 2}"
            c.op("te", "matmul", ps[:, 0:WIN], b.p2m[:], qkv[:, ch, :],
                 start=True, stop=True, reads=("p2m", "sqkv"), writes=(pk,))
            c.op("dv", "tensor_tensor", qkv[:, ch, :], ps[:, 0:WIN],
                 b.tblw[:, 0, 0, :], op=ALU.mult, reads=(pk, "tblw0"),
                 writes=("sqkv",))
            c.op("te", "matmul", ps[:, 0:WIN], b.p2m[:], qkv[:, 8 + ch, :],
                 start=True, stop=True, reads=("p2m", "sqkv"), writes=(pk,))
            c.op("dv", "tensor_tensor", qkv[:, 8 + ch, :], ps[:, 0:WIN],
                 b.tblw[:, 1, g % 2, :], op=ALU.mult,
                 reads=(pk, "tblw1"), writes=("sqkv",))
        for ch in range(CH):
            c.op("ac", "copy", b.sqs[:, 0, 0:64], ck[:, ch, :],
                 reads=("ck",), writes=("sqs0",))
            c.op("te", "matmul", b.ps[4][:, 0:64], b.p2m[:],
                 b.sqs[:, 0, 0:64], start=True, stop=True,
                 reads=("p2m", "sqs0"), writes=("ps4",))
            c.op("dv", "tensor_tensor", kcr[:, ch, :], b.ps[4][:, 0:64],
                 b.tblw[:, 0, 1, 0:64], op=ALU.mult, reads=("ps4", "tblw0"),
                 writes=("kcr",))
            c.op("ac", "copy", b.sqs[:, 1, 0:64], cv[:, ch, :],
                 reads=("cv",), writes=("sqs1",))
            for half in range(2):
                h2, hp = ch * 2 + half, half * 64
                c.op("te", "transpose", b.pstr[0:64, 0:64],
                     b.sqs[hp:hp + 64, 1, 0:64], b.idb[hp:hp + 64, hp:hp + 64],
                     reads=("sqs1", "idb"), writes=("pstr",))
                c.op("dv", "tensor_copy", vct[:, h2 * 64:(h2 + 1) * 64],
                     b.pstr[0:64, 0:64], reads=("pstr",), writes=("vct",))
        if g == 1:
            dump(b, "qkv_rot", qkv, ("sqkv",))
            dump(b, "kcr", kcr, ("kcr",))
            dump(b, "vct", vct, ("vct",))
        c.flush()
        for h in range(H):
            ch, hp = h // 2, (h % 2) * 64
            psS, pkS = b.ps[4 + h % 2], f"ps{4 + h % 2}"
            psO, pkO = b.ps[2 + h % 2], f"ps{2 + h % 2}"
            for qc in range(2):
                c.op("te", "matmul", psS[:, 0:64],
                     qkv[hp:hp + 64, ch, qc * 128:(qc + 1) * 128],
                     kcr[hp:hp + 64, ch, :], start=True, stop=True,
                     reads=("sqkv", "kcr"), writes=(pkS,))
                c.op("te", "matmul", psS[:, 64:320],
                     qkv[hp:hp + 64, ch, qc * 128:(qc + 1) * 128],
                     qkv[hp:hp + 64, 8 + ch, :], start=True, stop=True,
                     reads=("sqkv",), writes=(pkS,))
                c.op("dv", "scalar_tensor_tensor", sscr[:, 0:64],
                     psS[:, 0:64], 0.125, b.mspc[:, g, :], op0=ALU.mult,
                     op1=ALU.add, reads=(pkS, "mspc"), writes=("sscr",))
                c.op("dv", "scalar_tensor_tensor", sscr[:, 64:320],
                     psS[:, 64:320], 0.125, b.mspl[:, qc, :], op0=ALU.mult,
                     op1=ALU.add, reads=(pkS, "mspl"), writes=("sscr",))
                c.op("dv", "reduce_max", ssml[:, 0:1], sscr[:, 0:320],
                     axis=AX.X, negate=True, reads=("sscr",),
                     writes=("sscr2",))
                c.op("ac", "activation", out=spb[:], in_=sscr[:, 0:320],
                     func=AF.Exp, bias=ssml[:, 0:1], accum_out=ssml[:, 2:3],
                     reads=("sscr", "sscr2"), writes=("spbuf", "sscr2"))
                if g == 1 and h == 0 and qc == 0:
                    dump(b, "scores00", sscr[:, 0:320], ("sscr",))
                c.op("dv", "reciprocal", ssml[:, 4:5], ssml[:, 2:3],
                     reads=("sscr2",), writes=("sscr2",))
                c.op("ac", "activation", out=spb[:], in_=spb[:],
                     func=AF.Identity, scale=ssml[:, 4:5],
                     reads=("spbuf", "sscr2"), writes=("spbuf",))
                c.op("te", "transpose", b.pstr[0:64, 0:128], spb[:, 0:64],
                     b.idb[:], reads=("spbuf", "idb"), writes=("pstr",))
                for j in range(2):
                    c.op("te", "transpose",
                         b.pstr[:, 128 + j * 128:128 + (j + 1) * 128],
                         spb[:, 64 + j * 128:64 + (j + 1) * 128], b.idb[:],
                         reads=("spbuf", "idb"),
                         writes=("pstr",) if j == 1 else ())
                c.op("dv", "tensor_copy", phs[0:64, 0, :],
                     b.pstr[0:64, 0:128], reads=("pstr",), writes=("phs",))
                c.op("dv", "tensor_copy", phs[:, 1:3, :],
                     b.pstr[:, 128:384].rearrange("p (j k) -> p j k", j=2),
                     reads=("pstr",), writes=("phs",))
                c.op("te", "matmul",
                     psO[hp:hp + 64, qc * 128:(qc + 1) * 128],
                     vct[0:64, h * 64:(h + 1) * 64], phs[0:64, 0, :],
                     start=True, stop=False, tile_position=(0, hp),
                     reads=("vct", "phs"), writes=(pkO,))
                for j in range(2):
                    c.op("te", "matmul",
                         psO[hp:hp + 64, qc * 128:(qc + 1) * 128],
                         vlt[:, j, h * 64:(h + 1) * 64], phs[:, 1 + j, :],
                         start=False, stop=(j == 1), tile_position=(0, hp),
                         reads=("vlt", "phs"),
                         writes=(pkO,) if j == 1 else ())
            c.op("ac", "copy", yfm[hp:hp + 64, ch, :], psO[hp:hp + 64, 0:WIN],
                 reads=(pkO,), writes=("yfm",))
        if g == 1:
            dump(b, "yfm", yfm, ("yfm",))
        for m in range(CH):
            ps, pk = b.ps[m % 4], f"ps{m % 4}"
            for k in range(CH):
                c.op("te", "matmul", ps[:, 0:WIN],
                     b.wo[:, k, m * 128:(m + 1) * 128], yfm[:, k, :],
                     start=(k == 0), stop=(k == CH - 1), reads=("yfm", "wo"),
                     writes=(pk,) if k in (0, CH - 1) else ())
            c.op("dv", "scalar_tensor_tensor", b.stream[:, m, ts],
                 ps[:, 0:WIN], b.bproj[:, m:m + 1], b.stream[:, m, ts],
                 op0=ALU.add, op1=ALU.add, reads=(pk, "bproj", "stream"),
                 writes=("stream",))
        c.flush()
    c.op("gq", "dma_start", out=b.wo[:],
         in_=b.h_wo[0].rearrange("(c p) m -> p c m", p=128), writes=("wo",),
         dsem="wo")
    c.handover(["scrf0", "scrf1"], ["ck", "cv"])
    dump(b, "postsparse", b.stream[:], ("stream",))
    c.flush()


def kernel(**inputs):
    d = prep_inputs(inputs)
    nc, b = build(elems=2, calls=2, depth=6, sparse=True)
    shared = {k: v for k, v in d.items() if k != "x_all"}
    in_maps = []
    for cidx in range(8):
        m = dict(shared)
        m["x_in"] = np.ascontiguousarray(d["x_all"][2 * cidx:2 * cidx + 2])
        in_maps.append(m)
    res = run_bass_kernel_spmd(nc, in_maps, core_ids=list(range(8)))
    out = np.concatenate([r["x_out"] for r in res.results], axis=0)
    return np.ascontiguousarray(out.transpose(0, 2, 1)).astype(np.float32)
